# version 05
# baseline (speedup 1.0000x reference)
import numpy as np
import concourse.bass as bass
import concourse.mybir as mybir
import concourse.library_config as _lc
from concourse.library_overlay import lower_extended_insts
from concourse.bass_utils import run_bass_kernel_spmd

F32 = mybir.dt.float32
I16 = mybir.dt.int16
U32 = mybir.dt.uint32
AF = mybir.ActivationFunctionType
A = mybir.AluOpType
AXX = mybir.AxisListType.X

D = 1024
H = 16
DH = 64
INTER = 4096
NEXP = 16384
PI = float(np.pi)


class Ctr:
    def __init__(self, h):
        self.h = h
        self.n = 0


class Prog:
    ENGS = ("tensor", "vector", "scalar", "gpsimd", "sync")

    def __init__(self, nc):
        self.nc = nc
        self.q = {e: [] for e in self.ENGS}
        self.chain = {}
        self.disabled = False

    def init_sems(self, sems):
        self.c = {k: Ctr(v) for k, v in sems.items()}
        self.chain = {"vector": self.c["vec"], "scalar": self.c["sca"],
                      "gpsimd": self.c["gps"]}

    def emit(self, eng, fn, waits=(), incs=(), chain=True):
        if self.disabled:
            return None
        chain_ctr = self.chain.get(eng) if chain else None
        w = [(c.h, v) for (c, v) in waits if v > 0]
        amts = []
        if chain_ctr is not None:
            w.append((chain_ctr.h, chain_ctr.n))
            chain_ctr.n += 1
            amts.append((chain_ctr.h, 1))
        for (c, amt) in incs:
            c.n += amt
            amts.append((c.h, amt))

        def run(e, _fn=fn, _w=w, _amts=amts):
            for (h, v) in _w[1:]:
                e.wait_ge(h, v)
            inst = _fn(e)
            if _w:
                inst._wait_ge(_w[0][0], _w[0][1])
            for (h, amt) in _amts:
                inst.then_inc(h, amt)
            return inst

        self.q[eng].append(run)
        return chain_ctr.n if chain_ctr is not None else None

    def wait(self, eng, waits):
        if self.disabled:
            return
        w = [(c.h, v) for (c, v) in waits if v > 0]
        if not w:
            return

        def run(e, _w=w):
            for (h, v) in _w:
                e.wait_ge(h, v)

        self.q[eng].append(run)


class Banks:
    def __init__(self, prog, psum_tensors, ten_ctr):
        self.p = prog
        self.b = psum_tensors
        self.ten = ten_ctr
        self.writer = [0] * len(self.b)
        self.readers = [[] for _ in self.b]

    def open(self, i, extra_waits=()):
        ws = list(self.readers[i]) + list(extra_waits)
        self.p.wait("tensor", ws)
        self.readers[i] = []
        return self.b[i]

    def close(self, i):
        self.writer[i] = self.ten.n

    def rw(self, i):
        return [(self.ten, self.writer[i])]

    def reader(self, i, ctr):
        self.readers[i].append((ctr, ctr.n))


def build_core_program(Tl=4096, SC=512, race=False, do_attn=True, do_expert=True, do_mlp=True, do_proj=True, do_wo=True, phase_limit=99, dbg=False):
    NT = SC // 128
    n_sc = Tl // SC
    GB = 2
    NGRP = 32

    nc = bass.Bass("TRN2", target_bir_lowering=False, detect_race_conditions=race,
                   dynamic_dma_scratch_size=16384)

    x_in = nc.dram_tensor("x", [Tl, D], F32, kind="ExternalInput")
    wq_d = nc.dram_tensor("wq", [2, 128, 4096], F32, kind="ExternalInput")
    wk_d = nc.dram_tensor("wk", [2, 128, 4096], F32, kind="ExternalInput")
    wv_d = nc.dram_tensor("wv", [2, 128, 4096], F32, kind="ExternalInput")
    wo_d = nc.dram_tensor("wo", [2, 128, 4096], F32, kind="ExternalInput")
    bqd = nc.dram_tensor("bq", [D], F32, kind="ExternalInput")
    bkd = nc.dram_tensor("bk", [D], F32, kind="ExternalInput")
    bvd = nc.dram_tensor("bv", [D], F32, kind="ExternalInput")
    bod = nc.dram_tensor("bo", [D], F32, kind="ExternalInput")
    wqf_d = nc.dram_tensor("wqf", [1, 128, 4096], F32, kind="ExternalInput")
    keysd = nc.dram_tensor("keysT", [8, 64, 128], F32, kind="ExternalInput")
    wg_d = nc.dram_tensor("wg", [8, 128, 4096], F32, kind="ExternalInput")
    wu_d = nc.dram_tensor("wu", [8, 128, 4096], F32, kind="ExternalInput")
    wd_d = nc.dram_tensor("wd", [8, 128, 4096], F32, kind="ExternalInput")
    tab = nc.dram_tensor("tab", [NEXP, 2 * D], F32, kind="ExternalInput")
    y_out = nc.dram_tensor("y", [Tl, D], F32, kind="ExternalOutput")
    ydbg = (nc.dram_tensor("ydbg", [10, 128, NT * D], F32, kind="ExternalOutput")
            if dbg else None)

    ctx = []

    def sb(name, shape, dtype=F32):
        g = nc.sbuf_tensor(name, shape, dtype)
        t = g.__enter__()
        ctx.append(g)
        return t

    def psum(name):
        g = nc.psum_tensor(name, [128, 512], F32)
        t = g.__enter__()
        ctx.append(g)
        return t

    def sem(name):
        g = nc.semaphore(name)
        h = g.__enter__()
        ctx.append(g)
        return h

    x_sc = sb("x_sc", [128, NT, D])
    xln = sb("xln", [128, NT, D])
    xlnT = sb("xlnT", [128, NT, D])
    q_tm = sb("q_tm", [128, NT, D])
    k_tm = sb("k_tm", [128, NT, D])
    v_tm = sb("v_tm", [128, NT, D])
    mlpacc = q_tm

    fmh = sb("fmh", [128, 2, 4, 128])
    amt = sb("amt", [128, 2, 128])
    kscr = sb("kscr", [128, 2, 4, 64])
    s_cos = sb("s_cos", [128, H, DH])
    s_sin = sb("s_sin", [128, H, DH])
    z_cos = sb("z_cos", [128, H, DH])
    z_sin = sb("z_sin", [128, H, DH])

    sxy = sb("sxy", [128, 8, 8])
    ixy = sb("ixy", [128, 8, 8])
    idxu = sb("idxu", [128, 8], U32)
    ascore = sb("ascore", [128, 4, 64])
    score8 = sb("score8", [128, 4, 8])
    pk_f = sb("pk_f", [128, 4, 8])
    isel = sb("isel", [128, 4, 8])
    jsel = sb("jsel", [128, 4, 8])
    idx_f = sb("idx_f", [128, 32])
    smw = sb("smw", [128, NT, 32])
    sm_s = sb("sm_s", [128, 8])
    wdot = sb("wdot", [128, NT, 32])
    wfin = sb("wfin", [128, NT, 32])
    idx_i16 = sb("idx_i16", [128, NT, 32], I16)
    idx_tmp = sb("idx_tmp", [128, NT, 256], I16)
    idx_g = sb("idx_g", [128, NT, 256], I16)
    gbuf = sb("gbuf", [128, GB, 2 * D])

    g_sc = sb("g_sc", [128, NT, 512])
    gu_sb = sb("gu_sb", [128, 512])
    guT_sc = sb("guT_sc", [128, NT, 4, 128])
    qf_all = bass.AP(fmh, 0, [[1024, 128], [1, 512]])

    def qf_chunk(c):
        return bass.AP(fmh, c * 128, [[1024, 128], [1, 128]])

    def sim_slot(p8):
        return bass.AP(guT_sc, p8 * 128, [[NT * 512, 128], [1, 128]])

    def sim_half(half):
        return bass.AP(guT_sc, half * 512, [[NT * 512, 128], [1, 512]])
    wbuf = sb("wbuf", [128, 2, 8, 512])

    ones = sb("ones", [128, 128])
    mask = sb("mask", [128, 128])
    ident = sb("ident", [128, 128])
    iota8 = sb("iota8", [128, 8])
    c8 = sb("c8", [128, 8])
    cos_col = sb("cos_col", [128, Tl // 128])
    sin_col = sb("sin_col", [128, Tl // 128])
    keys_sb = sb("keys_sb", [128, 8, 128])
    brep = sb("brep", [128, D])
    mv = sb("mv", [128, 2])
    rs_t = sb("rs_t", [128, 2])
    dscr = sb("dscr", [128, D])
    ang = sb("ang", [128, 2 * (Tl // 128)])

    pb = [psum(f"pb{i}") for i in range(8)]

    sems = dict(
        vec=sem("vec"), sca=sem("sca"), gps=sem("gps"), ten=sem("ten"),
        dw0=sem("dw0"), dw1=sem("dw1"), dx=sem("dx"), dy=sem("dy"),
        di=sem("di"), dk=sem("dk"), dg0=sem("dg0"), dg1=sem("dg1"),
    )
    P = Prog(nc)
    P.init_sems(sems)
    C = P.c
    ten, vec, sca, gps = C["ten"], C["vec"], C["sca"], C["gps"]
    bank = Banks(P, pb, ten)
    dg = [C["dg0"], C["dg1"]]
    E, W = P.emit, P.wait

    def dump(slot, tens):
        if not dbg:
            return
        W("sync", [(vec, vec.n), (ten, ten.n), (sca, sca.n), (gps, gps.n)])
        ap_s = bass.AP(tens, 0, [[NT * D, 128], [1, NT * D]])
        E("sync", lambda e, _s=slot, _a=ap_s: e.dma_start(
            out=ydbg[_s, :, :], in_=_a), incs=[(C["di"], 16)], chain=False)

    def _ph0(p):
        if phase_limit < p:
            P.disabled = True

    E("vector", lambda e: e.memset(ones[:], 1.0))
    v_ones = vec.n
    E("gpsimd", lambda e: e.affine_select(mask[:], ones[:], [[1, 128]], A.is_ge,
                                          0.0, base=0, channel_multiplier=-1),
      waits=[(vec, v_ones)])
    E("gpsimd", lambda e: e.affine_select(ident[:], ones[:], [[1, 128]],
                                          A.is_equal, 0.0, base=0,
                                          channel_multiplier=-1))
    E("gpsimd", lambda e: e.iota(iota8[:], [[1, 8]], base=0, channel_multiplier=0, allow_small_or_imprecise_dtypes=True))
    g_i8 = gps.n
    E("vector", lambda e: e.tensor_scalar_mul(c8[:], iota8[:], 8.0),
      waits=[(gps, g_i8)])
    ncc = Tl // 128
    E("gpsimd", lambda e: e.iota(ang[:, 0:ncc], [[128, ncc]], base=0,
                                 channel_multiplier=1, allow_small_or_imprecise_dtypes=True))
    E("gpsimd", lambda e: e.iota(ang[:, ncc:2 * ncc], [[-128, ncc]], base=Tl,
                                 channel_multiplier=-1, allow_small_or_imprecise_dtypes=True))
    g_ang = gps.n
    E("gpsimd", lambda e: e.load_library(_lc.mlp), chain=False)
    _ph0(0.2)
    E("scalar", lambda e: e.activation(sin_col[:], ang[:, 0:ncc], AF.Sin,
                                       bias=0.0, scale=PI / (2 * Tl)),
      waits=[(gps, g_ang)])
    E("scalar", lambda e: e.activation(cos_col[:], ang[:, ncc:2 * ncc], AF.Sin,
                                       bias=0.0, scale=PI / (2 * Tl)))
    s_masters = sca.n
    _ph0(0.3)
    for t_ in (fmh, s_cos, s_sin, z_cos, z_sin, keys_sb):
        E("vector", lambda e, _t=t_: e.memset(_t[:], 0.0))
    E("vector", lambda e: e.memset(idx_tmp[:], 0))
    E("vector", lambda e: e.memset(idx_g[:], 0))
    v_zero = vec.n
    _ph0(0.4)
    W("sync", [(vec, v_zero)])
    for j in range(8):
        E("sync", lambda e, _j=j: e.dma_start(
            out=keys_sb[(_j % 2) * 64:(_j % 2) * 64 + 64, _j, :],
            in_=keysd[_j, :, :]),
          incs=[(C["dk"], 16)], chain=False)
    dk_done = C["dk"].n
    _ph0(0.5)

    def set_bias(dram_b):
        W("sync", [(vec, vec.n)])
        E("sync", lambda e, _b=dram_b: e.dma_start(
            out=brep[0:1, :], in_=bass.AP(_b, 0, [[0, 1], [1, D]])),
          incs=[(C["dk"], 16)], chain=False)
        W("gpsimd", [(C["dk"], C["dk"].n)])
        E("gpsimd", lambda e: e.partition_broadcast(
            brep[:, :], brep[0:1, :], channels=128))
        return gps.n

    wsems = [C["dw0"], C["dw1"]]
    wring = {"pending": [], "handles": [], "nq": 0, "consumed": [(vec, 0), (vec, 0)],
             "cnt": [0, 0], "owed": 2}

    def _ring_emit_one():
        if not wring["pending"]:
            wring["owed"] += 1
            return
        (dram, si) = wring["pending"].pop(0)
        i = len(wring["handles"])
        s = i % 2
        war = wring["consumed"][s]
        srcap = bass.AP(dram, si * 128 * 4096, [[4096, 128], [1, 4096]])
        dstap = bass.AP(wbuf, s * 4096, [[2 * 4096, 128], [1, 4096]])
        W("sync", [war])
        E("sync", lambda e, _d=dstap, _s=srcap: e.dma_start(out=_d, in_=_s),
          incs=[(wsems[s], 16)], chain=False)
        wring["cnt"][s] += 1
        wring["handles"].append((s, (wsems[s], 16 * wring["cnt"][s])))

    def ring_queue(dram, si=0, ow=512, kparts=8):
        i = wring["nq"]
        wring["nq"] += 1
        wring["pending"].append((dram, si))
        while wring["owed"] > 0 and wring["pending"]:
            wring["owed"] -= 1
            _ring_emit_one()
        return i

    def ring_get(i):
        assert i < len(wring["handles"]), "slice not yet loaded (ring order bug)"
        return wring["handles"][i]

    def wdone(s):
        wring["consumed"][s] = (ten, ten.n)
        _ring_emit_one()

    def layernorm(src, dst, sqb, extra=()):
        vv0 = vec.n
        E("scalar", lambda e, _s=src(0, 1024), _q=sqb: e.activation(
            _q, _s, AF.Square), waits=[(vec, vv0)] + list(extra))
        ssq = sca.n
        E("vector", lambda e, _s=src(0, 1024): e.tensor_reduce(
            mv[:, 0:1], _s, AXX, A.add))
        E("vector", lambda e, _q=sqb: e.tensor_reduce(mv[:, 1:2], _q, AXX,
                                                      A.add), waits=[(sca, ssq)])
        E("vector", lambda e: e.tensor_scalar_mul(mv[:, 0:1], mv[:, 0:1],
                                                  1.0 / D))
        E("vector", lambda e: e.tensor_scalar_mul(mv[:, 1:2], mv[:, 1:2],
                                                  1.0 / D))
        E("vector", lambda e: e.tensor_tensor(rs_t[:, 0:1], mv[:, 0:1],
                                              mv[:, 0:1], A.mult))
        E("vector", lambda e: e.tensor_tensor(rs_t[:, 0:1], mv[:, 1:2],
                                              rs_t[:, 0:1], A.subtract))
        E("vector", lambda e: e.tensor_scalar(rs_t[:, 0:1], rs_t[:, 0:1], 1e-5,
                                              None, A.add))
        vv = vec.n
        E("scalar", lambda e: e.activation(rs_t[:, 1:2], rs_t[:, 0:1], AF.Sqrt),
          waits=[(vec, vv)])
        ssn = sca.n
        E("vector", lambda e: e.reciprocal(rs_t[:, 0:1], rs_t[:, 1:2]),
          waits=[(sca, ssn)])
        E("vector", lambda e, _s=src(0, 1024), _d=dst(0, 1024): e.tensor_scalar(
            _d, _s, mv[:, 0:1], rs_t[:, 0:1], A.subtract, A.mult))
        return vec.n

    def transpose8(src3, jj, dst3, extra_waits, b0=6, b1=7):
        for half, bi in ((0, b0), (1, b1)):
            bank.open(bi, extra_waits=extra_waits)
            for c in range(4):
                cc = half * 4 + c
                ap_o = pb[bi][:, c * 128:(c + 1) * 128]
                ap_i = src3[:, jj, cc * 128:(cc + 1) * 128]
                E("tensor", lambda e, _o=ap_o, _i=ap_i, _st=(c == 0), _sp=(c == 3):
                  e.matmul(_o, _i, ident[:], start=_st, stop=_sp,
                           is_transpose=True),
                  incs=([(ten, 1)] if c == 3 else []), chain=False)
            bank.close(bi)
            ap_d = dst3[:, jj, half * 512:(half + 1) * 512]
            E("vector", lambda e, _d=ap_d, _bi=bi: e.tensor_copy(
                _d, pb[_bi][:, 0:512]), waits=bank.rw(bi))
            bank.reader(bi, vec)
        return vec.n

    nx = 0
    ny = 0
    x_gate = (vec, 0)
    state_v = {h: 0 for h in range(H)}
    kscr_free = [0, 0]
    fmh_free = [0, 0]
    v_fm = [0, 0]

    gspecs = []
    gtails = []
    gstate = {"issued": 0}
    gtmpl = {"raw": None}

    def gather_emit(e, ap_o, ap_i):
        if gtmpl["raw"] is None:
            inst = e.dma_gather(out_ap=ap_o, in_ap=tab[:, :], idxs_ap=ap_i,
                                num_idxs=128, num_idxs_reg=128, elem_size=2 * D)
            gtmpl["raw"] = inst.ins
            return inst
        t = gtmpl["raw"]
        raw = mybir.InstDMAGatherAnt(
            name=nc.get_next_instruction_name(), engine=mybir.EngineType.Pool,
            ins=[t.ins[0], e.lower_ap(ap_i), t.ins[2]],
            outs=[e.lower_ap(ap_o)],
            transpose=False, num_idxs=128, elem_size=2 * D,
            stride_bytes_256=t.stride_bytes_256, gen_mode=0,
            single_packet=True, queue_num=0)
        return e.add_instruction(raw)

    def issue_gathers(upto):
        while gstate["issued"] < min(upto, len(gspecs)):
            gi = gstate["issued"]
            gstate["issued"] += 1
            (tj, gg, vsh) = gspecs[gi]
            slot = gi % GB
            ws = [(vec, vsh)]
            if gi - GB >= 0:
                ws.append(gtails[gi - GB])
            ap_o = bass.AP(gbuf, slot * 2 * D,
                           [[GB * 2 * D, 128], [2 * D, 1], [1, 2 * D]])
            ap_i = bass.AP(idx_g, tj * 256 + gg * 8, [[NT * 256, 128], [1, 8]])
            W("gpsimd", ws)
            E("gpsimd", lambda e, _o=ap_o, _i=ap_i: gather_emit(e, _o, _i),
              incs=[(dg[slot], 16)], chain=False)

    _ph = _ph0

    for sc in range(n_sc):
        t0 = sc * SC
        gt = t0 // 128
        _ph(1)
        W("sync", [x_gate])
        for j in range(NT):
            ap_d = x_sc[:, j, :]
            ap_s = x_in[t0 + j * 128:t0 + (j + 1) * 128, :]
            E("sync", lambda e, _d=ap_d, _s=ap_s: e.dma_start(out=_d, in_=_s),
              incs=[(C["dx"], 16)], chain=False)
            nx += 1
        dx_need = 16 * nx


        _ph(2)
        E("vector", lambda e: e.memset(fmh[64:128, :, :, :], 0.0))
        xT_v = [0] * NT
        for j in range(NT):
            W("vector", [(C["dx"], dx_need), (C["dk"], dk_done)])
            lv = layernorm(lambda lo, hi, _j=j: x_sc[:, _j, lo:hi],
                           lambda lo, hi, _j=j: xln[:, _j, lo:hi],
                           v_tm[:, 0, :], extra=[(C["dx"], dx_need)])
            xT_v[j] = transpose8(xln, j, xlnT, [(vec, lv)])

        if sc == 0:
            dump(0, xln)
            dump(1, xlnT)
        _ph(3)
        qkv = [(wq_d, 0, 0), (wq_d, 512, 0), (wk_d, 0, 1), (wk_d, 512, 1),
               (wv_d, 0, 2), (wv_d, 512, 2)]
        qkv_ids = [ring_queue(d, o // 512) for (d, o, _) in qkv] if do_proj else []
        prod = [0] * NT
        g_b = 0
        for pi_, (dram, of0, which) in enumerate(qkv if do_proj else []):
            if pi_ % 2 == 0:
                g_b = set_bias([bqd, bkd, bvd][which])
            wslot, wwait = ring_get(qkv_ids[pi_])
            oh = of0 // 512
            for j in range(NT):
                bi = j % 2
                bank.open(bi, extra_waits=[wwait, (vec, xT_v[j])])
                for k in range(8):
                    ap_l = xlnT[:, j, k * 128:(k + 1) * 128]
                    ap_r = wbuf[:, wslot, k, 0:512]
                    E("tensor", lambda e, _bi=bi, _l=ap_l, _r=ap_r,
                      _st=(k == 0), _sp=(k == 7): e.matmul(
                          pb[_bi][:, 0:512], _l, _r, start=_st, stop=_sp),
                      incs=([(ten, 1)] if k == 7 else []), chain=False)
                bank.close(bi)
                if which in (0, 1):
                    dst = q_tm if which == 0 else k_tm
                    ap_b = brep[:, oh * 512:(oh + 1) * 512]
                    E("vector", lambda e, _bi=bi, _b=ap_b: e.tensor_tensor(
                        dscr[:, 0:512], pb[_bi][:, 0:512], _b, A.add),
                      waits=bank.rw(bi) + [(gps, g_b)])
                    bank.reader(bi, vec)
                    ap_d = dst[:, j, oh * 512:(oh + 1) * 512]
                    E("vector", lambda e, _d=ap_d: e.tensor_scalar_max(
                        _d, dscr[:, 0:512], 0.0))
                else:
                    ap_d = v_tm[:, j, oh * 512:(oh + 1) * 512]
                    ap_b = brep[:, oh * 512:(oh + 1) * 512]
                    E("vector", lambda e, _bi=bi, _d=ap_d, _b=ap_b:
                      e.tensor_tensor(_d, pb[_bi][:, 0:512], _b, A.add),
                      waits=bank.rw(bi) + [(gps, g_b)])
                    bank.reader(bi, vec)
                prod[j] = vec.n
            wdone(wslot)

        if sc == 0:
            dump(2, q_tm)
            dump(3, k_tm)
            dump(4, v_tm)
        _ph(4)
        ao_v = [0] * NT

        def attn_stage(j, h):
            par = h % 2
            hsl = slice(h * DH, (h + 1) * DH)
            ap_q = q_tm[:, j, hsl]
            ap_k = k_tm[:, j, hsl]
            ap_cc = cos_col[:, gt + j:gt + j + 1]
            ap_sc = sin_col[:, gt + j:gt + j + 1]
            for si_, (ap_in, ap_col) in enumerate(
                    ((ap_q, ap_cc), (ap_q, ap_sc), (ap_k, ap_cc), (ap_k, ap_sc))):
                E("vector", lambda e, _i=ap_in, _c=ap_col, _p=par, _s=si_:
                  e.tensor_scalar(kscr[:, _p, _s, :], _i, _c, None, A.mult),
                  waits=([(ten, kscr_free[par])] if si_ == 0 else ()))
            v_scale = vec.n
            bank.open(5, extra_waits=[(vec, v_scale)])
            for si_ in range(4):
                E("tensor", lambda e, _p=par, _s=si_: e.matmul(
                    pb[5][0:64, _s * 128:(_s + 1) * 128], kscr[:, _p, _s, :],
                    ident[:], start=(_s == 0), stop=(_s == 3),
                    is_transpose=True),
                  incs=([(ten, 1)] if si_ == 3 else []), chain=False)
            bank.close(5)
            E("vector", lambda e, _p=par: e.tensor_copy(
                bass.AP(fmh, (_p * 4) * 128, [[2 * 4 * 128, 64], [1, 512]]),
                pb[5][0:64, 0:512]),
              waits=bank.rw(5) + [(ten, fmh_free[par])])
            bank.reader(5, vec)
            v_fm[par] = vec.n

        for j in range(NT if do_attn else 0):
            attn_stage(j, 0)
            for h in range(H):
                par = h % 2
                hp = h // 2
                hsl = slice(h * DH, (h + 1) * DH)
                ab = 1 + par
                bank.open(ab, extra_waits=[(vec, v_fm[par])])
                E("tensor", lambda e, _ab=ab, _p=par: e.matmul(
                    pb[_ab][:, 0:128], fmh[:, _p, 2, :], fmh[:, _p, 0, :],
                    start=True, stop=False), chain=False)
                E("tensor", lambda e, _ab=ab, _p=par: e.matmul(
                    pb[_ab][:, 0:128], fmh[:, _p, 3, :], fmh[:, _p, 1, :],
                    start=False, stop=True), incs=[(ten, 1)], chain=False)
                bank.close(ab)
                E("vector", lambda e, _ab=ab, _p=par: e.tensor_tensor(
                    amt[:, _p, :], pb[_ab][:, 0:128], mask[:], A.mult),
                  waits=bank.rw(ab))
                bank.reader(ab, vec)
                v_amt = vec.n
                if h + 1 < H:
                    attn_stage(j, h + 1)
                ob = 3
                if par == 0:
                    bank.open(ob, extra_waits=[(vec, v_amt), (vec, state_v[h])])
                else:
                    W("tensor", [(vec, v_amt), (vec, state_v[h])])
                tp = (0, 64 * par)
                po = slice(64 * par, 64 * par + 64)
                ap_v = v_tm[:, j, hsl]
                mm6 = [
                    (pb[ob][po, 0:128], ap_v, amt[:, par, :], True, False),
                    (pb[ob][po, 0:128], s_cos[:, h, :], fmh[:, par, 0, :], False, False),
                    (pb[ob][po, 0:128], s_sin[:, h, :], fmh[:, par, 1, :], False, True),
                    (pb[ob][po, 128:256], ones[:, 0:64], amt[:, par, :], True, False),
                    (pb[ob][po, 128:256], z_cos[:, h, :], fmh[:, par, 0, :], False, False),
                    (pb[ob][po, 128:256], z_sin[:, h, :], fmh[:, par, 1, :], False, True),
                ]
                for mi, (o_, l_, r_, st_, sp_) in enumerate(mm6):
                    E("tensor", lambda e, _o=o_, _l=l_, _r=r_, _st=st_, _sp=sp_,
                      _tp=tp: e.matmul(_o, _l, _r, start=_st, stop=_sp,
                                       tile_position=_tp),
                      incs=([(ten, 1)] if mi == 5 else []), chain=False)
                fmh_free[par] = ten.n
                if par == 1:
                    bank.close(ob)
                stb = 4
                if par == 0:
                    bank.open(stb)
                co = 256 * par
                mm4 = [
                    (pb[stb][0:64, co:co + 64], kscr[:, par, 2, :], ap_v,
                     par == 0, False),
                    (pb[stb][0:64, co + 64:co + 128], kscr[:, par, 3, :], ap_v,
                     False, False),
                    (pb[stb][0:64, co + 128:co + 192], kscr[:, par, 2, :],
                     ones[:, 0:64], False, False),
                    (pb[stb][0:64, co + 192:co + 256], kscr[:, par, 3, :],
                     ones[:, 0:64], False, par == 1),
                ]
                for mi, (o_, l_, r_, st_, sp_) in enumerate(mm4):
                    E("tensor", lambda e, _o=o_, _l=l_, _r=r_, _st=st_, _sp=sp_:
                      e.matmul(_o, _l, _r, start=_st, stop=_sp),
                      incs=([(ten, 1)] if mi == 3 else []),
                      chain=False)
                kscr_free[par] = ten.n
                if par == 1:
                    bank.close(stb)

                if par == 1:
                    odw = bank.rw(ob)
                    E("vector", lambda e: e.tensor_scalar_max(
                        dscr[:, 0:128], pb[3][:, 128:256], 1e-6), waits=odw)
                    bank.reader(ob, vec)
                    E("vector", lambda e: e.reciprocal(dscr[:, 128:256],
                                                       dscr[:, 0:128]))
                    ap_ao = xln[:, j, hp * 128:(hp + 1) * 128]
                    E("vector", lambda e, _d=ap_ao: e.tensor_tensor(
                        _d, pb[3][:, 0:128], dscr[:, 128:256], A.mult))
                    bank.reader(ob, vec)
                    stw = bank.rw(stb)
                    for hh, cc in ((h - 1, 0), (h, 256)):
                        for ti, tt in enumerate((s_cos, s_sin, z_cos, z_sin)):
                            ap_t = tt[0:64, hh, :]
                            ap_p = pb[stb][0:64, cc + ti * 64:cc + ti * 64 + 64]
                            E("vector", lambda e, _t=ap_t, _p=ap_p:
                              e.tensor_tensor(_t, _t, _p, A.add),
                              waits=(stw if (ti == 0 and cc == 0) else ()))
                        state_v[hh] = vec.n
                    bank.reader(stb, vec)
            ao_v[j] = vec.n

        if sc == 0:
            dump(5, xln)
        _ph(5)
        oh_ids = [ring_queue(wo_d, 0), ring_queue(wo_d, 1)] if do_wo else []
        if do_wo:
            g_b = set_bias(bod)
        for ohx in range(2 if do_wo else 0):
            wslot, wwait = ring_get(oh_ids[ohx])
            for j in range(NT):
                bi = j % 2
                bank.open(bi, extra_waits=[wwait, (vec, ao_v[j])])
                for k in range(8):
                    ap_l = xln[:, j, k * 128:(k + 1) * 128]
                    ap_r = wbuf[:, wslot, k, 0:512]
                    E("tensor", lambda e, _bi=bi, _l=ap_l, _r=ap_r,
                      _st=(k == 0), _sp=(k == 7): e.matmul(
                          pb[_bi][:, 0:512], _l, _r, start=_st, stop=_sp),
                      incs=([(ten, 1)] if k == 7 else []), chain=False)
                bank.close(bi)
                ap_d = v_tm[:, j, ohx * 512:(ohx + 1) * 512]
                ap_b = brep[:, ohx * 512:(ohx + 1) * 512]
                E("vector", lambda e, _bi=bi, _d=ap_d, _b=ap_b: e.tensor_tensor(
                    _d, pb[_bi][:, 0:512], _b, A.add),
                  waits=bank.rw(bi) + [(gps, g_b)])
                bank.reader(bi, vec)
            wdone(wslot)
        for j in range(NT):
            layernorm(lambda lo, hi, _j=j: v_tm[:, _j, lo:hi],
                      lambda lo, hi: dscr[:, lo:hi], k_tm[:, 0, :])
            ap_x = x_sc[:, j, :]
            E("vector", lambda e, _x=ap_x: e.tensor_tensor(_x, _x, dscr[:, :],
                                                           A.add))

        if sc == 0:
            dump(6, v_tm)
            dump(7, x_sc)
        _ph(6)
        W("vector", [(ten, ten.n)])
        E("vector", lambda e: e.memset(mlpacc[:], 0.0))
        for j in range(NT):
            lv = layernorm(lambda lo, hi, _j=j: x_sc[:, _j, lo:hi],
                           lambda lo, hi, _j=j: xln[:, _j, lo:hi],
                           v_tm[:, 0, :])
            xT_v[j] = transpose8(xln, j, xlnT, [(vec, lv)])

        _ph(7)
        qf_id = ring_queue(wqf_d, 0)
        qslot, qwait = ring_get(qf_id)
        sc_gbase = len(gspecs)

        for j in range(NT if do_expert else 0):
            bank.open(0, extra_waits=[qwait, (vec, xT_v[j])])
            for oc in range(4):
                for k in range(8):
                    ap_l = wbuf[:, qslot, k, oc * 128:(oc + 1) * 128]
                    ap_r = xlnT[:, j, k * 128:(k + 1) * 128]
                    ap_o = pb[0][:, oc * 128:(oc + 1) * 128]
                    E("tensor", lambda e, _o=ap_o, _l=ap_l, _r=ap_r,
                      _st=(k == 0), _sp=(k == 7): e.matmul(_o, _l, _r,
                                                           start=_st, stop=_sp),
                      incs=([(ten, 1)] if (k == 7 and oc == 3) else []),
                      chain=False)
            bank.close(0)
            E("scalar", lambda e: e.copy(qf_all, pb[0][:, 0:512]),
              waits=bank.rw(0))
            bank.reader(0, sca)
            s_qf = sca.n
            for half, bi in ((0, 2), (1, 6)):
                bank.open(bi, extra_waits=[(sca, s_qf), (C["dk"], dk_done)])
                for q2 in range(4):
                    p8 = half * 4 + q2
                    ap_l = qf_chunk(p8 // 2)
                    ap_r = keys_sb[:, p8, :]
                    ap_o = pb[bi][:, q2 * 128:(q2 + 1) * 128]
                    E("tensor", lambda e, _o=ap_o, _l=ap_l, _r=ap_r: e.matmul(
                        _o, _l, _r, start=True, stop=True),
                      incs=([(ten, 1)] if q2 == 3 else []), chain=False)
                bank.close(bi)
                ap_d = sim_half(half)
                E("scalar", lambda e, _d=ap_d, _bi=bi: e.copy(_d, pb[_bi][:, 0:512]),
                  waits=bank.rw(bi))
                bank.reader(bi, sca)
            s_sim = sca.n
            W("vector", [(sca, s_sim)])
            for p8 in range(8):
                E("vector", lambda e, _p=p8: e.max(sxy[:, _p, :], sim_slot(_p)))
                E("vector", lambda e, _p=p8: e.max_index(idxu[:, :], sxy[:, _p, :],
                                                         sim_slot(_p)))
                E("vector", lambda e, _p=p8: e.tensor_copy(ixy[:, _p, :],
                                                           idxu[:, :]))
            E("vector", lambda e: e.tensor_tensor(
                bass.AP(ascore, 0, [[256, 128], [64, 4], [8, 8], [1, 8]]),
                bass.AP(sxy, 0, [[64, 128], [8, 4], [1, 8], [0, 8]]),
                bass.AP(sxy, 32, [[64, 128], [8, 4], [0, 8], [1, 8]]),
                A.add))
            for hh in range(4):
                E("vector", lambda e, _h=hh: e.max(score8[:, _h, :],
                                                   ascore[:, _h, :]))
                E("vector", lambda e, _h=hh: e.max_index(idxu[:, :],
                                                         score8[:, _h, :],
                                                         ascore[:, _h, :]))
                E("vector", lambda e, _h=hh: e.tensor_copy(pk_f[:, _h, :],
                                                           idxu[:, :]))
            E("vector", lambda e: e.tensor_tensor(
                bass.AP(ascore, 0, [[256, 128], [56, 4], [7, 8], [1, 7]]),
                bass.AP(pk_f, 0, [[32, 128], [8, 4], [1, 8], [0, 7]]),
                bass.AP(c8, 1, [[8, 128], [0, 4], [0, 8], [1, 7]]),
                A.is_ge))
            E("vector", lambda e: e.tensor_reduce(
                isel[:, :, :],
                bass.AP(ascore, 0, [[256, 128], [56, 4], [7, 8], [1, 7]]),
                AXX, A.add))
            E("vector", lambda e: e.scalar_tensor_tensor(
                jsel[:], isel[:], -8.0, pk_f[:], A.mult, A.add))
            for selt, off in ((isel, 0), (jsel, 32)):
                E("vector", lambda e, _s=selt: e.tensor_tensor(
                    bass.AP(ascore, 0, [[256, 128], [64, 4], [8, 8], [1, 8]]),
                    bass.AP(_s, 0, [[32, 128], [8, 4], [1, 8], [0, 8]]),
                    bass.AP(iota8, 0, [[8, 128], [0, 4], [0, 8], [1, 8]]),
                    A.is_equal))
                E("vector", lambda e, _o=off: e.tensor_tensor(
                    bass.AP(ascore, 0, [[256, 128], [64, 4], [8, 8], [1, 8]]),
                    bass.AP(ascore, 0, [[256, 128], [64, 4], [8, 8], [1, 8]]),
                    bass.AP(ixy, _o, [[64, 128], [8, 4], [0, 8], [1, 8]]),
                    A.mult))
                E("vector", lambda e, _d=selt: e.tensor_reduce(
                    _d[:, :, :],
                    bass.AP(ascore, 0, [[256, 128], [64, 4], [8, 8], [1, 8]]),
                    AXX, A.add))
            E("vector", lambda e: e.scalar_tensor_tensor(
                idx_f[:], bass.AP(isel, 0, [[32, 128], [1, 32]]), 128.0,
                bass.AP(jsel, 0, [[32, 128], [1, 32]]), A.mult, A.add))
            ap_smw3 = bass.AP(smw, j * 32, [[NT * 32, 128], [8, 4], [1, 8]])
            E("vector", lambda e, _d=ap_smw3: e.tensor_tensor(
                _d, bass.AP(score8, 0, [[32, 128], [8, 4], [1, 8]]),
                bass.AP(score8, 0, [[32, 128], [8, 4], [0, 8]]), A.subtract))
            vv1 = vec.n
            E("scalar", lambda e, _d=ap_smw3: e.activation(_d, _d, AF.Exp),
              waits=[(vec, vv1)])
            ss1 = sca.n
            E("vector", lambda e, _s=ap_smw3: e.tensor_reduce(
                sm_s[:, 0:4], _s, AXX, A.add), waits=[(sca, ss1)])
            E("vector", lambda e: e.reciprocal(sm_s[:, 4:8], sm_s[:, 0:4]))
            E("vector", lambda e, _d=ap_smw3: e.tensor_tensor(
                _d, _d, bass.AP(sm_s, 4, [[8, 128], [1, 4], [0, 8]]), A.mult))
            ap_ii = idx_i16[:, j, :]
            E("vector", lambda e, _d=ap_ii: e.tensor_copy(_d, idx_f[:]))
            v_i16 = vec.n
            W("sync", [(vec, v_i16)])
            for s8 in range(8):
                for dup in range(2):
                    ap_d = bass.AP(idx_tmp,
                                   (dup * 16) * (NT * 256) + j * 256 + s8 * 32,
                                   [[NT * 256, 16], [1, 32]])
                    ap_s = bass.AP(idx_i16, (16 * s8) * (NT * 32) + j * 32,
                                   [[NT * 32, 16], [1, 32]])
                    E("sync", lambda e, _d=ap_d, _s=ap_s: e.dma_start(
                        out=_d, in_=_s), incs=[(C["di"], 16)], chain=False)
            ap_gd = bass.AP(idx_g, j * 256, [[NT * 256, 32], [8, 32], [1, 8]])
            ap_gs = bass.AP(idx_tmp, j * 256, [[NT * 256, 32], [1, 32], [32, 8]])
            E("vector", lambda e, _d=ap_gd, _s=ap_gs: e.tensor_copy(_d, _s),
              waits=[(C["di"], C["di"].n)])
            v_shuf = vec.n
            for g in range(NGRP):
                gspecs.append((j, g, v_shuf))
                gtails.append(None)
            if j == 0 and sc == 0:
                issue_gathers(GB)
        wdone(qslot)

        def consume_unit(u):
            tj = u // 16
            u2 = u % 16
            base = sc_gbase + tj * NGRP
            for gg in (2 * u2, 2 * u2 + 1):
                gi = base + gg
                slot = gi % GB
                dgw = (dg[slot], 16 * (gi // GB + 1))
                ap_g = bass.AP(gbuf, slot * 2 * D, [[GB * 2 * D, 128], [1, D]])
                ap_x = xln[:, tj, :]
                E("vector", lambda e, _g=ap_g, _x=ap_x: e.tensor_tensor(
                    dscr[:, :], _g, _x, A.mult), waits=[dgw])
                ap_w = wdot[:, tj, gg:gg + 1]
                E("vector", lambda e, _w=ap_w: e.tensor_reduce(
                    _w, dscr[:, :], AXX, A.add))
            vv2 = vec.n
            ap_wd = wdot[:, tj, 2 * u2:2 * u2 + 2]
            ap_wf = wfin[:, tj, 2 * u2:2 * u2 + 2]
            ap_sm = smw[:, tj, 2 * u2:2 * u2 + 2]
            E("scalar", lambda e, _i=ap_wd, _o=ap_wf: e.activation(
                _o, _i, AF.Silu), waits=[(vec, vv2)])
            ss2 = sca.n
            E("vector", lambda e, _f=ap_wf, _s=ap_sm: e.tensor_tensor(
                _f, _f, _s, A.mult), waits=[(sca, ss2)])
            for gg in (2 * u2, 2 * u2 + 1):
                gi = base + gg
                slot = gi % GB
                ap_u = bass.AP(gbuf, slot * 2 * D + D, [[GB * 2 * D, 128], [1, D]])
                ap_m = mlpacc[:, tj, :]
                ap_w1 = wfin[:, tj, gg:gg + 1]
                E("vector", lambda e, _u=ap_u, _m=ap_m, _w=ap_w1:
                  e.scalar_tensor_tensor(_m, _u, _w, _m, A.mult, A.add))
                gtails[gi] = (vec, vec.n)

        cu = {"n": 0}

        def drain(kunits):
            if not do_expert:
                return
            for _ in range(kunits):
                u = cu["n"]
                if u >= NT * 16:
                    return
                cu["n"] += 1
                gu_base = sc_gbase + 2 * u
                issue_gathers(gu_base + 2)
                consume_unit(u)
                issue_gathers(gu_base + 4)

        if sc == 0:
            dump(8, xln)
        _ph(8)
        mlp_ids = []
        for ic in range(8 if do_mlp else 0):
            mlp_ids.append(ring_queue(wg_d, ic))
            mlp_ids.append(ring_queue(wu_d, ic))
            mlp_ids.append(ring_queue(wd_d, ic))
        for ic in range(8 if do_mlp else 0):
            gslot, gwait = ring_get(mlp_ids[3 * ic])
            sgl = [0] * NT
            for j in range(NT):
                bank.open(4, extra_waits=[gwait, (vec, xT_v[j])])
                for k in range(8):
                    ap_l = xlnT[:, j, k * 128:(k + 1) * 128]
                    ap_r = wbuf[:, gslot, k, 0:512]
                    E("tensor", lambda e, _l=ap_l, _r=ap_r, _st=(k == 0),
                      _sp=(k == 7): e.matmul(pb[4][:, 0:512], _l, _r,
                                             start=_st, stop=_sp),
                      incs=([(ten, 1)] if k == 7 else []), chain=False)
                bank.close(4)
                ap_gs2 = g_sc[:, j, :]
                E("scalar", lambda e, _d=ap_gs2: e.activation(
                    _d, pb[4][:, 0:512], AF.Silu), waits=bank.rw(4))
                bank.reader(4, sca)
                sgl[j] = sca.n
            wdone(gslot)
            uslot, uwait = ring_get(mlp_ids[3 * ic + 1])
            gTl = [0] * NT
            for j in range(NT):
                bank.open(5, extra_waits=[uwait, (vec, xT_v[j])])
                for k in range(8):
                    ap_l = xlnT[:, j, k * 128:(k + 1) * 128]
                    ap_r = wbuf[:, uslot, k, 0:512]
                    E("tensor", lambda e, _l=ap_l, _r=ap_r, _st=(k == 0),
                      _sp=(k == 7): e.matmul(pb[5][:, 0:512], _l, _r,
                                             start=_st, stop=_sp),
                      incs=([(ten, 1)] if k == 7 else []), chain=False)
                bank.close(5)
                ap_gs2 = g_sc[:, j, :]
                E("vector", lambda e, _g=ap_gs2: e.tensor_tensor(
                    gu_sb[:, :], _g, pb[5][:, 0:512], A.mult),
                  waits=bank.rw(5) + [(sca, sgl[j])])
                bank.reader(5, vec)
                v_gu = vec.n
                bank.open(6, extra_waits=[(vec, v_gu)])
                for c in range(4):
                    ap_i = gu_sb[:, c * 128:(c + 1) * 128]
                    ap_o = pb[6][:, c * 128:(c + 1) * 128]
                    E("tensor", lambda e, _o=ap_o, _i=ap_i, _st=(c == 0),
                      _sp=(c == 3): e.matmul(_o, _i, ident[:], start=_st,
                                             stop=_sp, is_transpose=True),
                      incs=([(ten, 1)] if c == 3 else []), chain=False)
                bank.close(6)
                ap_gT = bass.AP(guT_sc, j * 512, [[NT * 512, 128], [1, 512]])
                E("scalar", lambda e, _d=ap_gT: e.activation(
                    _d, pb[6][:, 0:512], AF.Copy), waits=bank.rw(6))
                bank.reader(6, sca)
                gTl[j] = sca.n
            wdone(uslot)
            dslot, dwait = ring_get(mlp_ids[3 * ic + 2])
            for j in range(NT):
                for oc2 in range(2):
                    bi = (j % 2) * 2 + oc2
                    bank.open(bi, extra_waits=[dwait, (sca, gTl[j])])
                    for k in range(4):
                        ap_l = bass.AP(guT_sc, j * 512 + k * 128,
                                       [[NT * 512, 128], [1, 128]])
                        ap_r = wbuf[:, dslot, k * 2 + oc2, 0:512]
                        E("tensor", lambda e, _bi=bi, _l=ap_l, _r=ap_r,
                          _st=(k == 0), _sp=(k == 3): e.matmul(
                              pb[_bi][:, 0:512], _l, _r, start=_st, stop=_sp),
                          incs=([(ten, 1)] if k == 3 else []), chain=False)
                    bank.close(bi)
                    ap_m = mlpacc[:, j, oc2 * 512:(oc2 + 1) * 512]
                    E("vector", lambda e, _bi=bi, _m=ap_m: e.scalar_tensor_tensor(
                        _m, pb[_bi][:, 0:512], 1.0, _m, A.mult, A.add),
                      waits=bank.rw(bi))
                    bank.reader(bi, vec)
                drain(1)
            wdone(dslot)
        drain(NT * 16)

        if sc == 0:
            dump(9, mlpacc)
        _ph(9)
        for j in range(NT):
            layernorm(lambda lo, hi, _j=j: mlpacc[:, _j, lo:hi],
                      lambda lo, hi: dscr[:, lo:hi], k_tm[:, 0, :])
            ap_x = x_sc[:, j, :]
            E("vector", lambda e, _x=ap_x: e.tensor_tensor(_x, _x, dscr[:, :],
                                                           A.add))
            v_res = vec.n
            ap_y = y_out[t0 + j * 128:t0 + (j + 1) * 128, :]
            W("sync", [(vec, v_res)])
            E("sync", lambda e, _y=ap_y, _x=ap_x: e.dma_start(out=_y, in_=_x),
              incs=[(C["dy"], 16)], chain=False)
            ny += 1
        x_gate = (C["dy"], 16 * ny)

    W("sync", [(C["dy"], 16 * ny)])

    with nc.Block() as block:
        @block.tensor
        def _(e):
            for fn in P.q["tensor"]:
                fn(e)

        @block.vector
        def _(e):
            for fn in P.q["vector"]:
                fn(e)

        @block.scalar
        def _(e):
            for fn in P.q["scalar"]:
                fn(e)

        @block.gpsimd
        def _(e):
            for fn in P.q["gpsimd"]:
                fn(e)

        @block.sync
        def _(e):
            for fn in P.q["sync"]:
                fn(e)

    for g in reversed(ctx):
        g.__exit__(None, None, None)
    return nc


def _pack_cols(M):
    R, C = M.shape
    kp = R // 128
    ns = C // 512
    return np.ascontiguousarray(
        M.reshape(kp, 128, ns, 512).transpose(2, 1, 0, 3).reshape(ns, 128, kp * 512))


def _pack_rows(M):
    R, C = M.shape
    ns = R // 512
    return np.ascontiguousarray(
        M.reshape(ns, 4, 128, C).transpose(0, 2, 1, 3).reshape(ns, 128, 4 * C))


def host_pack(x, Wq, bq, Wk, bk, Wv, bv, Wo, bo, fWq, keys, down, up, Wg, Wu, Wd):
    f32 = np.float32

    def c_(a):
        return np.ascontiguousarray(np.asarray(a), dtype=f32)

    keys_np = np.asarray(keys)
    keys_h = np.empty((8, 64, 128), f32)
    for p in range(2):
        for h in range(4):
            keys_h[p * 4 + h] = np.transpose(keys_np[h, :, p, :])
    return dict(
        x=c_(x),
        wq=_pack_cols(c_(Wq)), wk=_pack_cols(c_(Wk)),
        wv=_pack_cols(c_(Wv)), wo=_pack_cols(c_(Wo)),
        bq=c_(bq), bk=c_(bk), bv=c_(bv), bo=c_(bo),
        wqf=_pack_cols(c_(np.transpose(np.asarray(fWq)))),
        keysT=c_(keys_h),
        wg=_pack_cols(c_(np.transpose(np.asarray(Wg)))),
        wu=_pack_cols(c_(np.transpose(np.asarray(Wu)))),
        wd=_pack_rows(c_(np.transpose(np.asarray(Wd)))),
        tab=c_(np.concatenate([np.asarray(down), np.asarray(up)], axis=1)))


def kernel(L0_attn_x, L0_attn_Wq, L0_attn_bq, L0_attn_Wk, L0_attn_bk,
           L0_attn_Wv, L0_attn_bv, L0_attn_Wo, L0_attn_bo, L0_ffn_Wq,
           L0_ffn_keys, L0_ffn_down_embed, L0_ffn_up_embed, L0_ffn_Wg,
           L0_ffn_Wu, L0_ffn_Wd):
    B = int(np.asarray(L0_attn_x).shape[0])
    xs = np.asarray(L0_attn_x)
    base = host_pack(xs[0], L0_attn_Wq, L0_attn_bq, L0_attn_Wk, L0_attn_bk,
                     L0_attn_Wv, L0_attn_bv, L0_attn_Wo, L0_attn_bo, L0_ffn_Wq,
                     L0_ffn_keys, L0_ffn_down_embed, L0_ffn_up_embed, L0_ffn_Wg,
                     L0_ffn_Wu, L0_ffn_Wd)
    nc = build_core_program()
    lower_extended_insts(nc)
    in_maps = []
    for c in range(B):
        m = dict(base)
        m["x"] = np.ascontiguousarray(xs[c], dtype=np.float32)
        in_maps.append(m)
    res = run_bass_kernel_spmd(nc, in_maps, core_ids=list(range(B)))
    out = np.stack([np.asarray(res.results[c]["y"]) for c in range(B)], axis=0)
    return out.astype(np.float32)


# version 06
# speedup vs baseline: 1.0904x; 1.0904x over previous
import numpy as np
import concourse.bass as bass
import concourse.mybir as mybir
import concourse.library_config as _lc
from concourse.library_overlay import lower_extended_insts
from concourse.bass_utils import run_bass_kernel_spmd

F32 = mybir.dt.float32
I16 = mybir.dt.int16
U32 = mybir.dt.uint32
AF = mybir.ActivationFunctionType
A = mybir.AluOpType
AXX = mybir.AxisListType.X

D = 1024
H = 16
DH = 64
INTER = 4096
NEXP = 16384
PI = float(np.pi)


class Ctr:
    def __init__(self, h):
        self.h = h
        self.n = 0


class Prog:
    ENGS = ("tensor", "vector", "scalar", "gpsimd", "sync")

    def __init__(self, nc):
        self.nc = nc
        self.q = {e: [] for e in self.ENGS}
        self.chain = {}
        self.disabled = False

    def init_sems(self, sems):
        self.c = {k: Ctr(v) for k, v in sems.items()}
        self.chain = {"vector": self.c["vec"], "scalar": self.c["sca"],
                      "gpsimd": self.c["gps"]}

    def emit(self, eng, fn, waits=(), incs=(), chain=True):
        if self.disabled:
            return None
        chain_ctr = self.chain.get(eng) if chain else None
        w = [(c.h, v) for (c, v) in waits if v > 0]
        amts = []
        if chain_ctr is not None:
            w.append((chain_ctr.h, chain_ctr.n))
            chain_ctr.n += 1
            amts.append((chain_ctr.h, 1))
        for (c, amt) in incs:
            c.n += amt
            amts.append((c.h, amt))

        def run(e, _fn=fn, _w=w, _amts=amts):
            for (h, v) in _w[1:]:
                e.wait_ge(h, v)
            inst = _fn(e)
            if _w:
                inst._wait_ge(_w[0][0], _w[0][1])
            for (h, amt) in _amts:
                inst.then_inc(h, amt)
            return inst

        self.q[eng].append(run)
        return chain_ctr.n if chain_ctr is not None else None

    def wait(self, eng, waits):
        if self.disabled:
            return
        w = [(c.h, v) for (c, v) in waits if v > 0]
        if not w:
            return

        def run(e, _w=w):
            for (h, v) in _w:
                e.wait_ge(h, v)

        self.q[eng].append(run)


class Banks:
    def __init__(self, prog, psum_tensors, ten_ctr):
        self.p = prog
        self.b = psum_tensors
        self.ten = ten_ctr
        self.writer = [0] * len(self.b)
        self.readers = [[] for _ in self.b]

    def open(self, i, extra_waits=()):
        ws = list(self.readers[i]) + list(extra_waits)
        self.p.wait("tensor", ws)
        self.readers[i] = []
        return self.b[i]

    def close(self, i):
        self.writer[i] = self.ten.n

    def rw(self, i):
        return [(self.ten, self.writer[i])]

    def reader(self, i, ctr):
        self.readers[i].append((ctr, ctr.n))


def build_core_program(Tl=4096, SC=512, race=False, do_attn=True, do_expert=True, do_mlp=True, do_proj=True, do_wo=True, phase_limit=99, dbg=False):
    NT = SC // 128
    n_sc = Tl // SC
    GB = 2
    NGRP = 32

    nc = bass.Bass("TRN2", target_bir_lowering=False, detect_race_conditions=race,
                   dynamic_dma_scratch_size=16384)

    x_in = nc.dram_tensor("x", [Tl, D], F32, kind="ExternalInput")
    wq_d = nc.dram_tensor("wq", [2, 128, 4096], F32, kind="ExternalInput")
    wk_d = nc.dram_tensor("wk", [2, 128, 4096], F32, kind="ExternalInput")
    wv_d = nc.dram_tensor("wv", [2, 128, 4096], F32, kind="ExternalInput")
    wo_d = nc.dram_tensor("wo", [2, 128, 4096], F32, kind="ExternalInput")
    bqd = nc.dram_tensor("bq", [D], F32, kind="ExternalInput")
    bkd = nc.dram_tensor("bk", [D], F32, kind="ExternalInput")
    bvd = nc.dram_tensor("bv", [D], F32, kind="ExternalInput")
    bod = nc.dram_tensor("bo", [D], F32, kind="ExternalInput")
    wqf_d = nc.dram_tensor("wqf", [1, 128, 4096], F32, kind="ExternalInput")
    keysd = nc.dram_tensor("keysT", [8, 64, 128], F32, kind="ExternalInput")
    wg_d = nc.dram_tensor("wg", [8, 128, 4096], F32, kind="ExternalInput")
    wu_d = nc.dram_tensor("wu", [8, 128, 4096], F32, kind="ExternalInput")
    wd_d = nc.dram_tensor("wd", [8, 128, 4096], F32, kind="ExternalInput")
    tab = nc.dram_tensor("tab", [NEXP, 2 * D], F32, kind="ExternalInput")
    y_out = nc.dram_tensor("y", [Tl, D], F32, kind="ExternalOutput")
    ydbg = (nc.dram_tensor("ydbg", [10, 128, NT * D], F32, kind="ExternalOutput")
            if dbg else None)

    ctx = []

    def sb(name, shape, dtype=F32):
        g = nc.sbuf_tensor(name, shape, dtype)
        t = g.__enter__()
        ctx.append(g)
        return t

    def psum(name):
        g = nc.psum_tensor(name, [128, 512], F32)
        t = g.__enter__()
        ctx.append(g)
        return t

    def sem(name):
        g = nc.semaphore(name)
        h = g.__enter__()
        ctx.append(g)
        return h

    x_sc = sb("x_sc", [128, NT, D])
    xln = sb("xln", [128, NT, D])
    xlnT = sb("xlnT", [128, NT, D])
    q_tm = sb("q_tm", [128, NT, D])
    k_tm = sb("k_tm", [128, NT, D])
    v_tm = sb("v_tm", [128, NT, D])
    mlpacc = q_tm

    fmh = sb("fmh", [128, 2, 4, 128])
    amt = sb("amt", [128, 2, 128])
    kscr = sb("kscr", [128, 2, 4, 64])
    s_cos = sb("s_cos", [128, H, DH])
    s_sin = sb("s_sin", [128, H, DH])
    z_cos = sb("z_cos", [128, H, DH])
    z_sin = sb("z_sin", [128, H, DH])

    sxy = sb("sxy", [128, 8, 8])
    ixy = sb("ixy", [128, 8, 8])
    idxu = sb("idxu", [128, 8], U32)
    ascore = sb("ascore", [128, 4, 64])
    score8 = sb("score8", [128, 4, 8])
    pk_f = sb("pk_f", [128, 4, 8])
    isel = sb("isel", [128, 4, 8])
    jsel = sb("jsel", [128, 4, 8])
    idx_f = sb("idx_f", [128, 32])
    smw = sb("smw", [128, NT, 32])
    sm_s = sb("sm_s", [128, 8])
    wdot = sb("wdot", [128, NT, 32])
    wfin = sb("wfin", [128, NT, 32])
    idx_i16 = sb("idx_i16", [128, NT, 32], I16)
    idx_tmp = sb("idx_tmp", [128, NT, 256], I16)
    idx_g = sb("idx_g", [128, NT, 256], I16)
    gbuf = sb("gbuf", [128, GB, 2 * D])

    g_sc = sb("g_sc", [128, NT, 512])
    gu_sb = sb("gu_sb", [128, 512])
    guT_sc = sb("guT_sc", [128, NT, 4, 128])
    qf_all = bass.AP(fmh, 0, [[1024, 128], [1, 512]])

    def qf_chunk(c):
        return bass.AP(fmh, c * 128, [[1024, 128], [1, 128]])

    def sim_slot(p8):
        return bass.AP(guT_sc, p8 * 128, [[NT * 512, 128], [1, 128]])

    def sim_half(half):
        return bass.AP(guT_sc, half * 512, [[NT * 512, 128], [1, 512]])
    wbuf = sb("wbuf", [128, 2, 8, 512])

    ones = sb("ones", [128, 128])
    mask = sb("mask", [128, 128])
    ident = sb("ident", [128, 128])
    iota8 = sb("iota8", [128, 8])
    c8 = sb("c8", [128, 8])
    cos_col = sb("cos_col", [128, Tl // 128])
    sin_col = sb("sin_col", [128, Tl // 128])
    keys_sb = sb("keys_sb", [128, 8, 128])
    brep = sb("brep", [128, D])
    mv = sb("mv", [128, 2])
    rs_t = sb("rs_t", [128, 2])
    dscr = sb("dscr", [128, D])
    ang = sb("ang", [128, 2 * (Tl // 128)])

    pb = [psum(f"pb{i}") for i in range(8)]

    sems = dict(
        vec=sem("vec"), sca=sem("sca"), gps=sem("gps"), ten=sem("ten"),
        dw0=sem("dw0"), dw1=sem("dw1"), dx=sem("dx"), dy=sem("dy"),
        di=sem("di"), dk=sem("dk"), dg0=sem("dg0"), dg1=sem("dg1"),
    )
    P = Prog(nc)
    P.init_sems(sems)
    C = P.c
    ten, vec, sca, gps = C["ten"], C["vec"], C["sca"], C["gps"]
    bank = Banks(P, pb, ten)
    dg = [C["dg0"], C["dg1"]]
    E, W = P.emit, P.wait

    def dump(slot, tens):
        if not dbg:
            return
        W("sync", [(vec, vec.n), (ten, ten.n), (sca, sca.n), (gps, gps.n)])
        ap_s = bass.AP(tens, 0, [[NT * D, 128], [1, NT * D]])
        E("sync", lambda e, _s=slot, _a=ap_s: e.dma_start(
            out=ydbg[_s, :, :], in_=_a), incs=[(C["di"], 16)], chain=False)

    def _ph0(p):
        if phase_limit < p:
            P.disabled = True

    E("vector", lambda e: e.memset(ones[:], 1.0))
    v_ones = vec.n
    E("gpsimd", lambda e: e.affine_select(mask[:], ones[:], [[1, 128]], A.is_ge,
                                          0.0, base=0, channel_multiplier=-1),
      waits=[(vec, v_ones)])
    E("gpsimd", lambda e: e.affine_select(ident[:], ones[:], [[1, 128]],
                                          A.is_equal, 0.0, base=0,
                                          channel_multiplier=-1))
    E("gpsimd", lambda e: e.iota(iota8[:], [[1, 8]], base=0, channel_multiplier=0, allow_small_or_imprecise_dtypes=True))
    g_i8 = gps.n
    E("vector", lambda e: e.tensor_scalar_mul(c8[:], iota8[:], 8.0),
      waits=[(gps, g_i8)])
    ncc = Tl // 128
    E("gpsimd", lambda e: e.iota(ang[:, 0:ncc], [[128, ncc]], base=0,
                                 channel_multiplier=1, allow_small_or_imprecise_dtypes=True))
    E("gpsimd", lambda e: e.iota(ang[:, ncc:2 * ncc], [[-128, ncc]], base=Tl,
                                 channel_multiplier=-1, allow_small_or_imprecise_dtypes=True))
    g_ang = gps.n
    E("gpsimd", lambda e: e.load_library(_lc.mlp), chain=False)
    _ph0(0.2)
    E("scalar", lambda e: e.activation(sin_col[:], ang[:, 0:ncc], AF.Sin,
                                       bias=0.0, scale=PI / (2 * Tl)),
      waits=[(gps, g_ang)])
    E("scalar", lambda e: e.activation(cos_col[:], ang[:, ncc:2 * ncc], AF.Sin,
                                       bias=0.0, scale=PI / (2 * Tl)))
    s_masters = sca.n
    _ph0(0.3)
    for t_ in (fmh, s_cos, s_sin, z_cos, z_sin, keys_sb):
        E("vector", lambda e, _t=t_: e.memset(_t[:], 0.0))
    E("vector", lambda e: e.memset(idx_tmp[:], 0))
    E("vector", lambda e: e.memset(idx_g[:], 0))
    v_zero = vec.n
    _ph0(0.4)
    W("sync", [(vec, v_zero)])
    for j in range(8):
        E("sync", lambda e, _j=j: e.dma_start(
            out=keys_sb[(_j % 2) * 64:(_j % 2) * 64 + 64, _j, :],
            in_=keysd[_j, :, :]),
          incs=[(C["dk"], 16)], chain=False)
    dk_done = C["dk"].n
    _ph0(0.5)

    def set_bias(dram_b):
        W("sync", [(vec, vec.n)])
        E("sync", lambda e, _b=dram_b: e.dma_start(
            out=brep[0:1, :], in_=bass.AP(_b, 0, [[0, 1], [1, D]])),
          incs=[(C["dk"], 16)], chain=False)
        W("gpsimd", [(C["dk"], C["dk"].n)])
        E("gpsimd", lambda e: e.partition_broadcast(
            brep[:, :], brep[0:1, :], channels=128))
        return gps.n

    wsems = [C["dw0"], C["dw1"]]
    wring = {"pending": [], "handles": [], "nq": 0, "consumed": [(vec, 0), (vec, 0)],
             "cnt": [0, 0], "owed": 2}

    def _ring_emit_one():
        if not wring["pending"]:
            wring["owed"] += 1
            return
        (dram, si) = wring["pending"].pop(0)
        i = len(wring["handles"])
        s = i % 2
        war = wring["consumed"][s]
        srcap = bass.AP(dram, si * 128 * 4096, [[4096, 128], [1, 4096]])
        dstap = bass.AP(wbuf, s * 4096, [[2 * 4096, 128], [1, 4096]])
        W("sync", [war])
        E("sync", lambda e, _d=dstap, _s=srcap: e.dma_start(out=_d, in_=_s),
          incs=[(wsems[s], 16)], chain=False)
        wring["cnt"][s] += 1
        wring["handles"].append((s, (wsems[s], 16 * wring["cnt"][s])))

    def ring_queue(dram, si=0, ow=512, kparts=8):
        i = wring["nq"]
        wring["nq"] += 1
        wring["pending"].append((dram, si))
        while wring["owed"] > 0 and wring["pending"]:
            wring["owed"] -= 1
            _ring_emit_one()
        return i

    def ring_get(i):
        assert i < len(wring["handles"]), "slice not yet loaded (ring order bug)"
        return wring["handles"][i]

    def wdone(s):
        wring["consumed"][s] = (ten, ten.n)
        _ring_emit_one()

    def layernorm(src, dst, sqb, extra=()):
        vv0 = vec.n
        E("scalar", lambda e, _s=src(0, 1024), _q=sqb: e.activation(
            _q, _s, AF.Square), waits=[(vec, vv0)] + list(extra))
        ssq = sca.n
        E("vector", lambda e, _s=src(0, 1024): e.tensor_reduce(
            mv[:, 0:1], _s, AXX, A.add))
        E("vector", lambda e, _q=sqb: e.tensor_reduce(mv[:, 1:2], _q, AXX,
                                                      A.add), waits=[(sca, ssq)])
        E("vector", lambda e: e.tensor_scalar_mul(mv[:, 0:1], mv[:, 0:1],
                                                  1.0 / D))
        E("vector", lambda e: e.tensor_scalar_mul(mv[:, 1:2], mv[:, 1:2],
                                                  1.0 / D))
        E("vector", lambda e: e.tensor_tensor(rs_t[:, 0:1], mv[:, 0:1],
                                              mv[:, 0:1], A.mult))
        E("vector", lambda e: e.tensor_tensor(rs_t[:, 0:1], mv[:, 1:2],
                                              rs_t[:, 0:1], A.subtract))
        E("vector", lambda e: e.tensor_scalar(rs_t[:, 0:1], rs_t[:, 0:1], 1e-5,
                                              None, A.add))
        vv = vec.n
        E("scalar", lambda e: e.activation(rs_t[:, 1:2], rs_t[:, 0:1], AF.Sqrt),
          waits=[(vec, vv)])
        ssn = sca.n
        E("vector", lambda e: e.reciprocal(rs_t[:, 0:1], rs_t[:, 1:2]),
          waits=[(sca, ssn)])
        E("vector", lambda e, _s=src(0, 1024), _d=dst(0, 1024): e.tensor_scalar(
            _d, _s, mv[:, 0:1], rs_t[:, 0:1], A.subtract, A.mult))
        return vec.n

    def transpose8(src3, jj, dst3, extra_waits, b0=6, b1=7):
        for half, bi in ((0, b0), (1, b1)):
            bank.open(bi, extra_waits=extra_waits)
            for c in range(4):
                cc = half * 4 + c
                ap_o = pb[bi][:, c * 128:(c + 1) * 128]
                ap_i = src3[:, jj, cc * 128:(cc + 1) * 128]
                E("tensor", lambda e, _o=ap_o, _i=ap_i, _st=(c == 0), _sp=(c == 3):
                  e.matmul(_o, _i, ident[:], start=_st, stop=_sp,
                           is_transpose=True),
                  incs=([(ten, 1)] if c == 3 else []), chain=False)
            bank.close(bi)
            ap_d = dst3[:, jj, half * 512:(half + 1) * 512]
            E("vector", lambda e, _d=ap_d, _bi=bi: e.tensor_copy(
                _d, pb[_bi][:, 0:512]), waits=bank.rw(bi))
            bank.reader(bi, vec)
        return vec.n

    nx = 0
    ny = 0
    x_gate = (vec, 0)
    state_v = {h: 0 for h in range(H)}
    kscr_free = [0, 0]
    fmh_free = [0, 0]
    v_fm = [0, 0]

    gspecs = []
    gtails = []
    gstate = {"issued": 0}
    gtmpl = {"raw": None}

    def gather_emit(e, ap_o, ap_i):
        if gtmpl["raw"] is None:
            inst = e.dma_gather(out_ap=ap_o, in_ap=tab[:, :], idxs_ap=ap_i,
                                num_idxs=128, num_idxs_reg=128, elem_size=2 * D)
            gtmpl["raw"] = inst.ins
            return inst
        t = gtmpl["raw"]
        raw = mybir.InstDMAGatherAnt(
            name=nc.get_next_instruction_name(), engine=mybir.EngineType.Pool,
            ins=[t.ins[0], e.lower_ap(ap_i), t.ins[2]],
            outs=[e.lower_ap(ap_o)],
            transpose=False, num_idxs=128, elem_size=2 * D,
            stride_bytes_256=t.stride_bytes_256, gen_mode=0,
            single_packet=True, queue_num=0)
        return e.add_instruction(raw)

    def issue_gathers(upto):
        while gstate["issued"] < min(upto, len(gspecs)):
            gi = gstate["issued"]
            gstate["issued"] += 1
            (tj, gg, vsh) = gspecs[gi]
            slot = gi % GB
            ws = [(vec, vsh)]
            if gi - GB >= 0:
                ws.append(gtails[gi - GB])
            ap_o = bass.AP(gbuf, slot * 2 * D,
                           [[GB * 2 * D, 128], [2 * D, 1], [1, 2 * D]])
            ap_i = bass.AP(idx_g, tj * 256 + gg * 8, [[NT * 256, 128], [1, 8]])
            W("gpsimd", ws)
            E("gpsimd", lambda e, _o=ap_o, _i=ap_i: gather_emit(e, _o, _i),
              incs=[(dg[slot], 16)], chain=False)

    _ph = _ph0

    for sc in range(n_sc):
        t0 = sc * SC
        gt = t0 // 128
        _ph(1)
        W("sync", [x_gate])
        for j in range(NT):
            ap_d = x_sc[:, j, :]
            ap_s = x_in[t0 + j * 128:t0 + (j + 1) * 128, :]
            E("sync", lambda e, _d=ap_d, _s=ap_s: e.dma_start(out=_d, in_=_s),
              incs=[(C["dx"], 16)], chain=False)
            nx += 1
        dx_need = 16 * nx


        _ph(2)
        E("vector", lambda e: e.memset(fmh[64:128, :, :, :], 0.0))
        xT_v = [0] * NT
        for j in range(NT):
            W("vector", [(C["dx"], dx_need), (C["dk"], dk_done)])
            lv = layernorm(lambda lo, hi, _j=j: x_sc[:, _j, lo:hi],
                           lambda lo, hi, _j=j: xln[:, _j, lo:hi],
                           v_tm[:, 0, :], extra=[(C["dx"], dx_need)])
            xT_v[j] = transpose8(xln, j, xlnT, [(vec, lv)])

        if sc == 0:
            dump(0, xln)
            dump(1, xlnT)
        _ph(3)
        qkv = [(wq_d, 0, 0), (wq_d, 512, 0), (wk_d, 0, 1), (wk_d, 512, 1),
               (wv_d, 0, 2), (wv_d, 512, 2)]
        qkv_ids = [ring_queue(d, o // 512) for (d, o, _) in qkv] if do_proj else []
        prod = [0] * NT
        g_b = 0
        for pi_, (dram, of0, which) in enumerate(qkv if do_proj else []):
            if pi_ % 2 == 0:
                g_b = set_bias([bqd, bkd, bvd][which])
            wslot, wwait = ring_get(qkv_ids[pi_])
            oh = of0 // 512
            for j in range(NT):
                bi = j % 2
                bank.open(bi, extra_waits=[wwait, (vec, xT_v[j])])
                for k in range(8):
                    ap_l = xlnT[:, j, k * 128:(k + 1) * 128]
                    ap_r = wbuf[:, wslot, k, 0:512]
                    E("tensor", lambda e, _bi=bi, _l=ap_l, _r=ap_r,
                      _st=(k == 0), _sp=(k == 7): e.matmul(
                          pb[_bi][:, 0:512], _l, _r, start=_st, stop=_sp),
                      incs=([(ten, 1)] if k == 7 else []), chain=False)
                bank.close(bi)
                if which in (0, 1):
                    dst = q_tm if which == 0 else k_tm
                    ap_b = brep[:, oh * 512:(oh + 1) * 512]
                    E("vector", lambda e, _bi=bi, _b=ap_b: e.tensor_tensor(
                        dscr[:, 0:512], pb[_bi][:, 0:512], _b, A.add),
                      waits=bank.rw(bi) + [(gps, g_b)])
                    bank.reader(bi, vec)
                    ap_d = dst[:, j, oh * 512:(oh + 1) * 512]
                    E("vector", lambda e, _d=ap_d: e.tensor_scalar_max(
                        _d, dscr[:, 0:512], 0.0))
                else:
                    ap_d = v_tm[:, j, oh * 512:(oh + 1) * 512]
                    ap_b = brep[:, oh * 512:(oh + 1) * 512]
                    E("vector", lambda e, _bi=bi, _d=ap_d, _b=ap_b:
                      e.tensor_tensor(_d, pb[_bi][:, 0:512], _b, A.add),
                      waits=bank.rw(bi) + [(gps, g_b)])
                    bank.reader(bi, vec)
                prod[j] = vec.n
            wdone(wslot)

        if sc == 0:
            dump(2, q_tm)
            dump(3, k_tm)
            dump(4, v_tm)
        _ph(4)
        ao_v = [0] * NT

        def attn_stage(j, h):
            par = h % 2
            hsl = slice(h * DH, (h + 1) * DH)
            ap_q = q_tm[:, j, hsl]
            ap_k = k_tm[:, j, hsl]
            ap_cc = cos_col[:, gt + j:gt + j + 1]
            ap_sc = sin_col[:, gt + j:gt + j + 1]
            for si_, (ap_in, ap_col) in enumerate(
                    ((ap_q, ap_cc), (ap_q, ap_sc), (ap_k, ap_cc), (ap_k, ap_sc))):
                E("vector", lambda e, _i=ap_in, _c=ap_col, _p=par, _s=si_:
                  e.tensor_scalar(kscr[:, _p, _s, :], _i, _c, None, A.mult),
                  waits=([(ten, kscr_free[par])] if si_ == 0 else ()))
            v_scale = vec.n
            bank.open(5, extra_waits=[(vec, v_scale)])
            for si_ in range(4):
                E("tensor", lambda e, _p=par, _s=si_: e.matmul(
                    pb[5][0:64, _s * 128:(_s + 1) * 128], kscr[:, _p, _s, :],
                    ident[:], start=(_s == 0), stop=(_s == 3),
                    is_transpose=True),
                  incs=([(ten, 1)] if si_ == 3 else []), chain=False)
            bank.close(5)
            E("vector", lambda e, _p=par: e.tensor_copy(
                bass.AP(fmh, (_p * 4) * 128, [[2 * 4 * 128, 64], [1, 512]]),
                pb[5][0:64, 0:512]),
              waits=bank.rw(5) + [(ten, fmh_free[par])])
            bank.reader(5, vec)
            v_fm[par] = vec.n

        for j in range(NT if do_attn else 0):
            attn_stage(j, 0)
            for h in range(H):
                par = h % 2
                hp = h // 2
                hsl = slice(h * DH, (h + 1) * DH)
                ab = 1 + par
                bank.open(ab, extra_waits=[(vec, v_fm[par])])
                E("tensor", lambda e, _ab=ab, _p=par: e.matmul(
                    pb[_ab][:, 0:128], fmh[:, _p, 2, :], fmh[:, _p, 0, :],
                    start=True, stop=False), chain=False)
                E("tensor", lambda e, _ab=ab, _p=par: e.matmul(
                    pb[_ab][:, 0:128], fmh[:, _p, 3, :], fmh[:, _p, 1, :],
                    start=False, stop=True), incs=[(ten, 1)], chain=False)
                bank.close(ab)
                E("vector", lambda e, _ab=ab, _p=par: e.tensor_tensor(
                    amt[:, _p, :], pb[_ab][:, 0:128], mask[:], A.mult),
                  waits=bank.rw(ab))
                bank.reader(ab, vec)
                v_amt = vec.n
                if h + 1 < H:
                    attn_stage(j, h + 1)
                ob = 3
                if par == 0:
                    bank.open(ob, extra_waits=[(vec, v_amt), (vec, state_v[h])])
                else:
                    W("tensor", [(vec, v_amt), (vec, state_v[h])])
                tp = (0, 64 * par)
                po = slice(64 * par, 64 * par + 64)
                ap_v = v_tm[:, j, hsl]
                mm6 = [
                    (pb[ob][po, 0:128], ap_v, amt[:, par, :], True, False),
                    (pb[ob][po, 0:128], s_cos[:, h, :], fmh[:, par, 0, :], False, False),
                    (pb[ob][po, 0:128], s_sin[:, h, :], fmh[:, par, 1, :], False, True),
                    (pb[ob][po, 128:256], ones[:, 0:64], amt[:, par, :], True, False),
                    (pb[ob][po, 128:256], z_cos[:, h, :], fmh[:, par, 0, :], False, False),
                    (pb[ob][po, 128:256], z_sin[:, h, :], fmh[:, par, 1, :], False, True),
                ]
                for mi, (o_, l_, r_, st_, sp_) in enumerate(mm6):
                    E("tensor", lambda e, _o=o_, _l=l_, _r=r_, _st=st_, _sp=sp_,
                      _tp=tp: e.matmul(_o, _l, _r, start=_st, stop=_sp,
                                       tile_position=_tp),
                      incs=([(ten, 1)] if mi == 5 else []), chain=False)
                fmh_free[par] = ten.n
                if par == 1:
                    bank.close(ob)
                stb = 4
                if par == 0:
                    bank.open(stb)
                co = 256 * par
                mm4 = [
                    (pb[stb][0:64, co:co + 64], kscr[:, par, 2, :], ap_v,
                     par == 0, False),
                    (pb[stb][0:64, co + 64:co + 128], kscr[:, par, 3, :], ap_v,
                     False, False),
                    (pb[stb][0:64, co + 128:co + 192], kscr[:, par, 2, :],
                     ones[:, 0:64], False, False),
                    (pb[stb][0:64, co + 192:co + 256], kscr[:, par, 3, :],
                     ones[:, 0:64], False, par == 1),
                ]
                for mi, (o_, l_, r_, st_, sp_) in enumerate(mm4):
                    E("tensor", lambda e, _o=o_, _l=l_, _r=r_, _st=st_, _sp=sp_:
                      e.matmul(_o, _l, _r, start=_st, stop=_sp),
                      incs=([(ten, 1)] if mi == 3 else []),
                      chain=False)
                kscr_free[par] = ten.n
                if par == 1:
                    bank.close(stb)

                if par == 1:
                    odw = bank.rw(ob)
                    E("vector", lambda e: e.tensor_scalar_max(
                        dscr[:, 0:128], pb[3][:, 128:256], 1e-6), waits=odw)
                    bank.reader(ob, vec)
                    E("vector", lambda e: e.reciprocal(dscr[:, 128:256],
                                                       dscr[:, 0:128]))
                    ap_ao = xln[:, j, hp * 128:(hp + 1) * 128]
                    E("vector", lambda e, _d=ap_ao: e.tensor_tensor(
                        _d, pb[3][:, 0:128], dscr[:, 128:256], A.mult))
                    bank.reader(ob, vec)
                    stw = bank.rw(stb)
                    for hh, cc in ((h - 1, 0), (h, 256)):
                        for ti, tt in enumerate((s_cos, s_sin, z_cos, z_sin)):
                            ap_t = tt[0:64, hh, :]
                            ap_p = pb[stb][0:64, cc + ti * 64:cc + ti * 64 + 64]
                            E("vector", lambda e, _t=ap_t, _p=ap_p:
                              e.tensor_tensor(_t, _t, _p, A.add),
                              waits=(stw if (ti == 0 and cc == 0) else ()))
                        state_v[hh] = vec.n
                    bank.reader(stb, vec)
            ao_v[j] = vec.n

        if sc == 0:
            dump(5, xln)
        _ph(5)
        oh_ids = [ring_queue(wo_d, 0), ring_queue(wo_d, 1)] if do_wo else []
        if do_wo:
            g_b = set_bias(bod)
        for ohx in range(2 if do_wo else 0):
            wslot, wwait = ring_get(oh_ids[ohx])
            for j in range(NT):
                bi = j % 2
                bank.open(bi, extra_waits=[wwait, (vec, ao_v[j])])
                for k in range(8):
                    ap_l = xln[:, j, k * 128:(k + 1) * 128]
                    ap_r = wbuf[:, wslot, k, 0:512]
                    E("tensor", lambda e, _bi=bi, _l=ap_l, _r=ap_r,
                      _st=(k == 0), _sp=(k == 7): e.matmul(
                          pb[_bi][:, 0:512], _l, _r, start=_st, stop=_sp),
                      incs=([(ten, 1)] if k == 7 else []), chain=False)
                bank.close(bi)
                ap_d = v_tm[:, j, ohx * 512:(ohx + 1) * 512]
                ap_b = brep[:, ohx * 512:(ohx + 1) * 512]
                E("vector", lambda e, _bi=bi, _d=ap_d, _b=ap_b: e.tensor_tensor(
                    _d, pb[_bi][:, 0:512], _b, A.add),
                  waits=bank.rw(bi) + [(gps, g_b)])
                bank.reader(bi, vec)
            wdone(wslot)
        for j in range(NT):
            layernorm(lambda lo, hi, _j=j: v_tm[:, _j, lo:hi],
                      lambda lo, hi: dscr[:, lo:hi], k_tm[:, 0, :])
            ap_x = x_sc[:, j, :]
            E("vector", lambda e, _x=ap_x: e.tensor_tensor(_x, _x, dscr[:, :],
                                                           A.add))

        if sc == 0:
            dump(6, v_tm)
            dump(7, x_sc)
        _ph(6)
        W("vector", [(ten, ten.n)])
        E("vector", lambda e: e.memset(mlpacc[:], 0.0))
        for j in range(NT):
            lv = layernorm(lambda lo, hi, _j=j: x_sc[:, _j, lo:hi],
                           lambda lo, hi, _j=j: xln[:, _j, lo:hi],
                           v_tm[:, 0, :])
            xT_v[j] = transpose8(xln, j, xlnT, [(vec, lv)])

        _ph(7)
        qf_id = ring_queue(wqf_d, 0)
        qslot, qwait = ring_get(qf_id)
        sc_gbase = len(gspecs)

        for j in range(NT if do_expert else 0):
            bank.open(0, extra_waits=[qwait, (vec, xT_v[j])])
            for oc in range(4):
                for k in range(8):
                    ap_l = wbuf[:, qslot, k, oc * 128:(oc + 1) * 128]
                    ap_r = xlnT[:, j, k * 128:(k + 1) * 128]
                    ap_o = pb[0][:, oc * 128:(oc + 1) * 128]
                    E("tensor", lambda e, _o=ap_o, _l=ap_l, _r=ap_r,
                      _st=(k == 0), _sp=(k == 7): e.matmul(_o, _l, _r,
                                                           start=_st, stop=_sp),
                      incs=([(ten, 1)] if (k == 7 and oc == 3) else []),
                      chain=False)
            bank.close(0)
            E("scalar", lambda e: e.copy(qf_all, pb[0][:, 0:512]),
              waits=bank.rw(0))
            bank.reader(0, sca)
            s_qf = sca.n
            for half, bi in ((0, 2), (1, 6)):
                bank.open(bi, extra_waits=[(sca, s_qf), (C["dk"], dk_done)])
                for q2 in range(4):
                    p8 = half * 4 + q2
                    ap_l = qf_chunk(p8 // 2)
                    ap_r = keys_sb[:, p8, :]
                    ap_o = pb[bi][:, q2 * 128:(q2 + 1) * 128]
                    E("tensor", lambda e, _o=ap_o, _l=ap_l, _r=ap_r: e.matmul(
                        _o, _l, _r, start=True, stop=True),
                      incs=([(ten, 1)] if q2 == 3 else []), chain=False)
                bank.close(bi)
                ap_d = sim_half(half)
                E("scalar", lambda e, _d=ap_d, _bi=bi: e.copy(_d, pb[_bi][:, 0:512]),
                  waits=bank.rw(bi))
                bank.reader(bi, sca)
            s_sim = sca.n
            W("vector", [(sca, s_sim)])
            for p8 in range(8):
                E("vector", lambda e, _p=p8: e.max(sxy[:, _p, :], sim_slot(_p)))
                E("vector", lambda e, _p=p8: e.max_index(idxu[:, :], sxy[:, _p, :],
                                                         sim_slot(_p)))
                E("vector", lambda e, _p=p8: e.tensor_copy(ixy[:, _p, :],
                                                           idxu[:, :]))
            E("vector", lambda e: e.tensor_tensor(
                bass.AP(ascore, 0, [[256, 128], [64, 4], [8, 8], [1, 8]]),
                bass.AP(sxy, 0, [[64, 128], [8, 4], [1, 8], [0, 8]]),
                bass.AP(sxy, 32, [[64, 128], [8, 4], [0, 8], [1, 8]]),
                A.add))
            for hh in range(4):
                E("vector", lambda e, _h=hh: e.max(score8[:, _h, :],
                                                   ascore[:, _h, :]))
                E("vector", lambda e, _h=hh: e.max_index(idxu[:, :],
                                                         score8[:, _h, :],
                                                         ascore[:, _h, :]))
                E("vector", lambda e, _h=hh: e.tensor_copy(pk_f[:, _h, :],
                                                           idxu[:, :]))
            E("vector", lambda e: e.tensor_tensor(
                bass.AP(ascore, 0, [[256, 128], [56, 4], [7, 8], [1, 7]]),
                bass.AP(pk_f, 0, [[32, 128], [8, 4], [1, 8], [0, 7]]),
                bass.AP(c8, 1, [[8, 128], [0, 4], [0, 8], [1, 7]]),
                A.is_ge))
            E("vector", lambda e: e.tensor_reduce(
                isel[:, :, :],
                bass.AP(ascore, 0, [[256, 128], [56, 4], [7, 8], [1, 7]]),
                AXX, A.add))
            E("vector", lambda e: e.scalar_tensor_tensor(
                jsel[:], isel[:], -8.0, pk_f[:], A.mult, A.add))
            for selt, off in ((isel, 0), (jsel, 32)):
                E("vector", lambda e, _s=selt: e.tensor_tensor(
                    bass.AP(ascore, 0, [[256, 128], [64, 4], [8, 8], [1, 8]]),
                    bass.AP(_s, 0, [[32, 128], [8, 4], [1, 8], [0, 8]]),
                    bass.AP(iota8, 0, [[8, 128], [0, 4], [0, 8], [1, 8]]),
                    A.is_equal))
                E("vector", lambda e, _o=off: e.tensor_tensor(
                    bass.AP(ascore, 0, [[256, 128], [64, 4], [8, 8], [1, 8]]),
                    bass.AP(ascore, 0, [[256, 128], [64, 4], [8, 8], [1, 8]]),
                    bass.AP(ixy, _o, [[64, 128], [8, 4], [0, 8], [1, 8]]),
                    A.mult))
                E("vector", lambda e, _d=selt: e.tensor_reduce(
                    _d[:, :, :],
                    bass.AP(ascore, 0, [[256, 128], [64, 4], [8, 8], [1, 8]]),
                    AXX, A.add))
            E("vector", lambda e: e.scalar_tensor_tensor(
                idx_f[:], bass.AP(isel, 0, [[32, 128], [1, 32]]), 128.0,
                bass.AP(jsel, 0, [[32, 128], [1, 32]]), A.mult, A.add))
            ap_smw3 = bass.AP(smw, j * 32, [[NT * 32, 128], [8, 4], [1, 8]])
            E("vector", lambda e, _d=ap_smw3: e.tensor_tensor(
                _d, bass.AP(score8, 0, [[32, 128], [8, 4], [1, 8]]),
                bass.AP(score8, 0, [[32, 128], [8, 4], [0, 8]]), A.subtract))
            vv1 = vec.n
            E("scalar", lambda e, _d=ap_smw3: e.activation(_d, _d, AF.Exp),
              waits=[(vec, vv1)])
            ss1 = sca.n
            E("vector", lambda e, _s=ap_smw3: e.tensor_reduce(
                sm_s[:, 0:4], _s, AXX, A.add), waits=[(sca, ss1)])
            E("vector", lambda e: e.reciprocal(sm_s[:, 4:8], sm_s[:, 0:4]))
            E("vector", lambda e, _d=ap_smw3: e.tensor_tensor(
                _d, _d, bass.AP(sm_s, 4, [[8, 128], [1, 4], [0, 8]]), A.mult))
            ap_ii = idx_i16[:, j, :]
            E("vector", lambda e, _d=ap_ii: e.tensor_copy(_d, idx_f[:]))
            v_i16 = vec.n
            W("sync", [(vec, v_i16)])
            for s8 in range(8):
                for dup in range(2):
                    ap_d = bass.AP(idx_tmp,
                                   (dup * 16) * (NT * 256) + j * 256 + s8 * 32,
                                   [[NT * 256, 16], [1, 32]])
                    ap_s = bass.AP(idx_i16, (16 * s8) * (NT * 32) + j * 32,
                                   [[NT * 32, 16], [1, 32]])
                    E("sync", lambda e, _d=ap_d, _s=ap_s: e.dma_start(
                        out=_d, in_=_s), incs=[(C["di"], 16)], chain=False)
            ap_gd = bass.AP(idx_g, j * 256, [[NT * 256, 32], [8, 32], [1, 8]])
            ap_gs = bass.AP(idx_tmp, j * 256, [[NT * 256, 32], [1, 32], [32, 8]])
            E("vector", lambda e, _d=ap_gd, _s=ap_gs: e.tensor_copy(_d, _s),
              waits=[(C["di"], C["di"].n)])
            v_shuf = vec.n
            for g in range(NGRP):
                gspecs.append((j, g, v_shuf))
                gtails.append(None)
            if j == 0 and sc == 0:
                issue_gathers(GB)
        wdone(qslot)

        def consume_unit(u):
            tj = u // 16
            u2 = u % 16
            base = sc_gbase + tj * NGRP
            for gg in (2 * u2, 2 * u2 + 1):
                gi = base + gg
                slot = gi % GB
                dgw = (dg[slot], 16 * (gi // GB + 1))
                ap_g = bass.AP(gbuf, slot * 2 * D, [[GB * 2 * D, 128], [1, D]])
                ap_x = xln[:, tj, :]
                E("vector", lambda e, _g=ap_g, _x=ap_x: e.tensor_tensor(
                    dscr[:, :], _g, _x, A.mult), waits=[dgw])
                ap_w = wdot[:, tj, gg:gg + 1]
                E("vector", lambda e, _w=ap_w: e.tensor_reduce(
                    _w, dscr[:, :], AXX, A.add))
            vv2 = vec.n
            ap_wd = wdot[:, tj, 2 * u2:2 * u2 + 2]
            ap_wf = wfin[:, tj, 2 * u2:2 * u2 + 2]
            ap_sm = smw[:, tj, 2 * u2:2 * u2 + 2]
            E("scalar", lambda e, _i=ap_wd, _o=ap_wf: e.activation(
                _o, _i, AF.Silu), waits=[(vec, vv2)])
            ss2 = sca.n
            E("vector", lambda e, _f=ap_wf, _s=ap_sm: e.tensor_tensor(
                _f, _f, _s, A.mult), waits=[(sca, ss2)])
            for gg in (2 * u2, 2 * u2 + 1):
                gi = base + gg
                slot = gi % GB
                ap_u = bass.AP(gbuf, slot * 2 * D + D, [[GB * 2 * D, 128], [1, D]])
                ap_m = mlpacc[:, tj, :]
                ap_w1 = wfin[:, tj, gg:gg + 1]
                E("vector", lambda e, _u=ap_u, _m=ap_m, _w=ap_w1:
                  e.scalar_tensor_tensor(_m, _u, _w, _m, A.mult, A.add))
                gtails[gi] = (vec, vec.n)

        cu = {"n": 0}

        def drain(kunits):
            if not do_expert:
                return
            for _ in range(kunits):
                u = cu["n"]
                if u >= NT * 16:
                    return
                cu["n"] += 1
                gu_base = sc_gbase + 2 * u
                issue_gathers(gu_base + 2)
                consume_unit(u)
                issue_gathers(gu_base + 4)

        if sc == 0:
            dump(8, xln)
        _ph(8)
        mlp_ids = []
        for ic in range(8 if do_mlp else 0):
            mlp_ids.append(ring_queue(wg_d, ic))
            mlp_ids.append(ring_queue(wu_d, ic))
            mlp_ids.append(ring_queue(wd_d, ic))
        for ic in range(8 if do_mlp else 0):
            gslot, gwait = ring_get(mlp_ids[3 * ic])
            sgl = [0] * NT
            for j in range(NT):
                bank.open(4, extra_waits=[gwait, (vec, xT_v[j])])
                for k in range(8):
                    ap_l = xlnT[:, j, k * 128:(k + 1) * 128]
                    ap_r = wbuf[:, gslot, k, 0:512]
                    E("tensor", lambda e, _l=ap_l, _r=ap_r, _st=(k == 0),
                      _sp=(k == 7): e.matmul(pb[4][:, 0:512], _l, _r,
                                             start=_st, stop=_sp),
                      incs=([(ten, 1)] if k == 7 else []), chain=False)
                bank.close(4)
                ap_gs2 = g_sc[:, j, :]
                E("scalar", lambda e, _d=ap_gs2: e.activation(
                    _d, pb[4][:, 0:512], AF.Silu), waits=bank.rw(4))
                bank.reader(4, sca)
                sgl[j] = sca.n
                drain(1)
            wdone(gslot)
            uslot, uwait = ring_get(mlp_ids[3 * ic + 1])
            gTl = [0] * NT
            for j in range(NT):
                bank.open(5, extra_waits=[uwait, (vec, xT_v[j])])
                for k in range(8):
                    ap_l = xlnT[:, j, k * 128:(k + 1) * 128]
                    ap_r = wbuf[:, uslot, k, 0:512]
                    E("tensor", lambda e, _l=ap_l, _r=ap_r, _st=(k == 0),
                      _sp=(k == 7): e.matmul(pb[5][:, 0:512], _l, _r,
                                             start=_st, stop=_sp),
                      incs=([(ten, 1)] if k == 7 else []), chain=False)
                bank.close(5)
                ap_gs2 = g_sc[:, j, :]
                E("vector", lambda e, _g=ap_gs2: e.tensor_tensor(
                    gu_sb[:, :], _g, pb[5][:, 0:512], A.mult),
                  waits=bank.rw(5) + [(sca, sgl[j])])
                bank.reader(5, vec)
                v_gu = vec.n
                bank.open(6, extra_waits=[(vec, v_gu)])
                for c in range(4):
                    ap_i = gu_sb[:, c * 128:(c + 1) * 128]
                    ap_o = pb[6][:, c * 128:(c + 1) * 128]
                    E("tensor", lambda e, _o=ap_o, _i=ap_i, _st=(c == 0),
                      _sp=(c == 3): e.matmul(_o, _i, ident[:], start=_st,
                                             stop=_sp, is_transpose=True),
                      incs=([(ten, 1)] if c == 3 else []), chain=False)
                bank.close(6)
                ap_gT = bass.AP(guT_sc, j * 512, [[NT * 512, 128], [1, 512]])
                E("scalar", lambda e, _d=ap_gT: e.activation(
                    _d, pb[6][:, 0:512], AF.Copy), waits=bank.rw(6))
                bank.reader(6, sca)
                gTl[j] = sca.n
            wdone(uslot)
            dslot, dwait = ring_get(mlp_ids[3 * ic + 2])
            for j in range(NT):
                for oc2 in range(2):
                    bi = (j % 2) * 2 + oc2
                    bank.open(bi, extra_waits=[dwait, (sca, gTl[j])])
                    for k in range(4):
                        ap_l = bass.AP(guT_sc, j * 512 + k * 128,
                                       [[NT * 512, 128], [1, 128]])
                        ap_r = wbuf[:, dslot, k * 2 + oc2, 0:512]
                        E("tensor", lambda e, _bi=bi, _l=ap_l, _r=ap_r,
                          _st=(k == 0), _sp=(k == 3): e.matmul(
                              pb[_bi][:, 0:512], _l, _r, start=_st, stop=_sp),
                          incs=([(ten, 1)] if k == 3 else []), chain=False)
                    bank.close(bi)
                    ap_m = mlpacc[:, j, oc2 * 512:(oc2 + 1) * 512]
                    E("vector", lambda e, _bi=bi, _m=ap_m: e.scalar_tensor_tensor(
                        _m, pb[_bi][:, 0:512], 1.0, _m, A.mult, A.add),
                      waits=bank.rw(bi))
                    bank.reader(bi, vec)
                drain(1)
            wdone(dslot)
        drain(NT * 16)

        if sc == 0:
            dump(9, mlpacc)
        _ph(9)
        for j in range(NT):
            layernorm(lambda lo, hi, _j=j: mlpacc[:, _j, lo:hi],
                      lambda lo, hi: dscr[:, lo:hi], k_tm[:, 0, :])
            ap_x = x_sc[:, j, :]
            E("vector", lambda e, _x=ap_x: e.tensor_tensor(_x, _x, dscr[:, :],
                                                           A.add))
            v_res = vec.n
            ap_y = y_out[t0 + j * 128:t0 + (j + 1) * 128, :]
            W("sync", [(vec, v_res)])
            E("sync", lambda e, _y=ap_y, _x=ap_x: e.dma_start(out=_y, in_=_x),
              incs=[(C["dy"], 16)], chain=False)
            ny += 1
        x_gate = (C["dy"], 16 * ny)

    W("sync", [(C["dy"], 16 * ny)])

    with nc.Block() as block:
        @block.tensor
        def _(e):
            for fn in P.q["tensor"]:
                fn(e)

        @block.vector
        def _(e):
            for fn in P.q["vector"]:
                fn(e)

        @block.scalar
        def _(e):
            for fn in P.q["scalar"]:
                fn(e)

        @block.gpsimd
        def _(e):
            for fn in P.q["gpsimd"]:
                fn(e)

        @block.sync
        def _(e):
            for fn in P.q["sync"]:
                fn(e)

    for g in reversed(ctx):
        g.__exit__(None, None, None)
    return nc


def _pack_cols(M):
    R, C = M.shape
    kp = R // 128
    ns = C // 512
    return np.ascontiguousarray(
        M.reshape(kp, 128, ns, 512).transpose(2, 1, 0, 3).reshape(ns, 128, kp * 512))


def _pack_rows(M):
    R, C = M.shape
    ns = R // 512
    return np.ascontiguousarray(
        M.reshape(ns, 4, 128, C).transpose(0, 2, 1, 3).reshape(ns, 128, 4 * C))


def host_pack(x, Wq, bq, Wk, bk, Wv, bv, Wo, bo, fWq, keys, down, up, Wg, Wu, Wd):
    f32 = np.float32

    def c_(a):
        return np.ascontiguousarray(np.asarray(a), dtype=f32)

    keys_np = np.asarray(keys)
    keys_h = np.empty((8, 64, 128), f32)
    for p in range(2):
        for h in range(4):
            keys_h[p * 4 + h] = np.transpose(keys_np[h, :, p, :])
    return dict(
        x=c_(x),
        wq=_pack_cols(c_(Wq)), wk=_pack_cols(c_(Wk)),
        wv=_pack_cols(c_(Wv)), wo=_pack_cols(c_(Wo)),
        bq=c_(bq), bk=c_(bk), bv=c_(bv), bo=c_(bo),
        wqf=_pack_cols(c_(np.transpose(np.asarray(fWq)))),
        keysT=c_(keys_h),
        wg=_pack_cols(c_(np.transpose(np.asarray(Wg)))),
        wu=_pack_cols(c_(np.transpose(np.asarray(Wu)))),
        wd=_pack_rows(c_(np.transpose(np.asarray(Wd)))),
        tab=c_(np.concatenate([np.asarray(down), np.asarray(up)], axis=1)))


def kernel(L0_attn_x, L0_attn_Wq, L0_attn_bq, L0_attn_Wk, L0_attn_bk,
           L0_attn_Wv, L0_attn_bv, L0_attn_Wo, L0_attn_bo, L0_ffn_Wq,
           L0_ffn_keys, L0_ffn_down_embed, L0_ffn_up_embed, L0_ffn_Wg,
           L0_ffn_Wu, L0_ffn_Wd):
    B = int(np.asarray(L0_attn_x).shape[0])
    xs = np.asarray(L0_attn_x)
    base = host_pack(xs[0], L0_attn_Wq, L0_attn_bq, L0_attn_Wk, L0_attn_bk,
                     L0_attn_Wv, L0_attn_bv, L0_attn_Wo, L0_attn_bo, L0_ffn_Wq,
                     L0_ffn_keys, L0_ffn_down_embed, L0_ffn_up_embed, L0_ffn_Wg,
                     L0_ffn_Wu, L0_ffn_Wd)
    nc = build_core_program()
    lower_extended_insts(nc)
    in_maps = []
    for c in range(B):
        m = dict(base)
        m["x"] = np.ascontiguousarray(xs[c], dtype=np.float32)
        in_maps.append(m)
    res = run_bass_kernel_spmd(nc, in_maps, core_ids=list(range(B)))
    out = np.stack([np.asarray(res.results[c]["y"]) for c in range(B)], axis=0)
    return out.astype(np.float32)


# version 07
# speedup vs baseline: 1.1477x; 1.0525x over previous
import numpy as np
import concourse.bass as bass
import concourse.mybir as mybir
import concourse.library_config as _lc
from concourse.library_overlay import lower_extended_insts
from concourse.bass_utils import run_bass_kernel_spmd

F32 = mybir.dt.float32
I16 = mybir.dt.int16
U32 = mybir.dt.uint32
AF = mybir.ActivationFunctionType
A = mybir.AluOpType
AXX = mybir.AxisListType.X

D = 1024
H = 16
DH = 64
INTER = 4096
NEXP = 16384
PI = float(np.pi)


class Ctr:
    def __init__(self, h):
        self.h = h
        self.n = 0


class Prog:
    ENGS = ("tensor", "vector", "scalar", "gpsimd", "sync")

    def __init__(self, nc):
        self.nc = nc
        self.q = {e: [] for e in self.ENGS}
        self.chain = {}
        self.disabled = False

    def init_sems(self, sems):
        self.c = {k: Ctr(v) for k, v in sems.items()}
        self.chain = {"vector": self.c["vec"], "scalar": self.c["sca"],
                      "gpsimd": self.c["gps"]}

    def emit(self, eng, fn, waits=(), incs=(), chain=True):
        if self.disabled:
            return None
        chain_ctr = self.chain.get(eng) if chain else None
        w = [(c.h, v) for (c, v) in waits if v > 0]
        amts = []
        if chain_ctr is not None:
            w.append((chain_ctr.h, chain_ctr.n))
            chain_ctr.n += 1
            amts.append((chain_ctr.h, 1))
        for (c, amt) in incs:
            c.n += amt
            amts.append((c.h, amt))

        def run(e, _fn=fn, _w=w, _amts=amts):
            for (h, v) in _w[1:]:
                e.wait_ge(h, v)
            inst = _fn(e)
            if _w:
                inst._wait_ge(_w[0][0], _w[0][1])
            for (h, amt) in _amts:
                inst.then_inc(h, amt)
            return inst

        self.q[eng].append(run)
        return chain_ctr.n if chain_ctr is not None else None

    def wait(self, eng, waits):
        if self.disabled:
            return
        w = [(c.h, v) for (c, v) in waits if v > 0]
        if not w:
            return

        def run(e, _w=w):
            for (h, v) in _w:
                e.wait_ge(h, v)

        self.q[eng].append(run)


class Banks:
    def __init__(self, prog, psum_tensors, ten_ctr):
        self.p = prog
        self.b = psum_tensors
        self.ten = ten_ctr
        self.writer = [0] * len(self.b)
        self.readers = [[] for _ in self.b]

    def open(self, i, extra_waits=()):
        ws = list(self.readers[i]) + list(extra_waits)
        self.p.wait("tensor", ws)
        self.readers[i] = []
        return self.b[i]

    def close(self, i):
        self.writer[i] = self.ten.n

    def rw(self, i):
        return [(self.ten, self.writer[i])]

    def reader(self, i, ctr):
        self.readers[i].append((ctr, ctr.n))


def build_core_program(Tl=4096, SC=512, race=False, do_attn=True, do_expert=True, do_mlp=True, do_proj=True, do_wo=True, phase_limit=99, dbg=False):
    NT = SC // 128
    n_sc = Tl // SC
    GB = 2
    NGRP = 32

    nc = bass.Bass("TRN2", target_bir_lowering=False, detect_race_conditions=race,
                   dynamic_dma_scratch_size=16384)

    x_in = nc.dram_tensor("x", [Tl, D], F32, kind="ExternalInput")
    wq_d = nc.dram_tensor("wq", [2, 128, 4096], F32, kind="ExternalInput")
    wk_d = nc.dram_tensor("wk", [2, 128, 4096], F32, kind="ExternalInput")
    wv_d = nc.dram_tensor("wv", [2, 128, 4096], F32, kind="ExternalInput")
    wo_d = nc.dram_tensor("wo", [2, 128, 4096], F32, kind="ExternalInput")
    bqd = nc.dram_tensor("bq", [D], F32, kind="ExternalInput")
    bkd = nc.dram_tensor("bk", [D], F32, kind="ExternalInput")
    bvd = nc.dram_tensor("bv", [D], F32, kind="ExternalInput")
    bod = nc.dram_tensor("bo", [D], F32, kind="ExternalInput")
    wqf_d = nc.dram_tensor("wqf", [1, 128, 4096], F32, kind="ExternalInput")
    keysd = nc.dram_tensor("keysT", [8, 64, 128], F32, kind="ExternalInput")
    wg_d = nc.dram_tensor("wg", [8, 128, 4096], F32, kind="ExternalInput")
    wu_d = nc.dram_tensor("wu", [8, 128, 4096], F32, kind="ExternalInput")
    wd_d = nc.dram_tensor("wd", [8, 128, 4096], F32, kind="ExternalInput")
    tab = nc.dram_tensor("tab", [NEXP, 2 * D], F32, kind="ExternalInput")
    y_out = nc.dram_tensor("y", [Tl, D], F32, kind="ExternalOutput")
    ydbg = (nc.dram_tensor("ydbg", [10, 128, NT * D], F32, kind="ExternalOutput")
            if dbg else None)

    ctx = []

    def sb(name, shape, dtype=F32):
        g = nc.sbuf_tensor(name, shape, dtype)
        t = g.__enter__()
        ctx.append(g)
        return t

    def psum(name):
        g = nc.psum_tensor(name, [128, 512], F32)
        t = g.__enter__()
        ctx.append(g)
        return t

    def sem(name):
        g = nc.semaphore(name)
        h = g.__enter__()
        ctx.append(g)
        return h

    x_sc = sb("x_sc", [128, NT, D])
    xln = sb("xln", [128, NT, D])
    xlnT = sb("xlnT", [128, NT, D])
    q_tm = sb("q_tm", [128, NT, D])
    k_tm = sb("k_tm", [128, NT, D])
    v65 = sb("v65", [128, NT, H * 65])
    mlpacc = q_tm

    fmh = sb("fmh", [128, 2, 2, 128])
    amt = sb("amt", [128, 2, 128])
    kscr = sb("kscr", [128, 2, 4, 64])
    s2 = sb("s2", [128, H, 65])

    sxy = sb("sxy", [128, 8, 8])
    ixy = sb("ixy", [128, 8, 8])
    idxu = sb("idxu", [128, 8], U32)
    ascore = sb("ascore", [128, 4, 64])
    score8 = sb("score8", [128, 4, 8])
    pk_f = sb("pk_f", [128, 4, 8])
    isel = sb("isel", [128, 4, 8])
    jsel = sb("jsel", [128, 4, 8])
    idx_f = sb("idx_f", [128, 32])
    smw = sb("smw", [128, NT, 32])
    sm_s = sb("sm_s", [128, 8])
    wdot = sb("wdot", [128, NT, 32])
    wfin = sb("wfin", [128, NT, 32])
    idx_i16 = sb("idx_i16", [128, NT, 32], I16)
    idx_tmp = sb("idx_tmp", [128, NT, 256], I16)
    idx_g = sb("idx_g", [128, NT, 256], I16)
    gbuf = sb("gbuf", [128, GB, 2 * D])

    g_sc = sb("g_sc", [128, NT, 512])
    gu_sb = sb("gu_sb", [128, 512])
    guT_sc = sb("guT_sc", [128, NT, 4, 128])
    qf_all = bass.AP(fmh, 0, [[512, 128], [1, 512]])

    def qf_chunk(c):
        return bass.AP(fmh, c * 128, [[512, 128], [1, 128]])

    def sim_slot(p8):
        return bass.AP(guT_sc, p8 * 128, [[NT * 512, 128], [1, 128]])

    def sim_half(half):
        return bass.AP(guT_sc, half * 512, [[NT * 512, 128], [1, 512]])
    wbuf = sb("wbuf", [128, 2, 8, 512])

    ones = sb("ones", [128, 128])
    mask = sb("mask", [128, 128])
    ident = sb("ident", [128, 128])
    iota8 = sb("iota8", [128, 8])
    c8 = sb("c8", [128, 8])
    cos_col = sb("cos_col", [128, Tl // 128])
    sin_col = sb("sin_col", [128, Tl // 128])
    keys_sb = sb("keys_sb", [128, 8, 128])
    brep = sb("brep", [128, D])
    mv = sb("mv", [128, 2])
    rs_t = sb("rs_t", [128, 2])
    dscr = sb("dscr", [128, D])
    ang = sb("ang", [128, 2 * (Tl // 128)])

    pb = [psum(f"pb{i}") for i in range(8)]

    sems = dict(
        vec=sem("vec"), sca=sem("sca"), gps=sem("gps"), ten=sem("ten"),
        dw0=sem("dw0"), dw1=sem("dw1"), dx=sem("dx"), dy=sem("dy"),
        di=sem("di"), dk=sem("dk"), dg0=sem("dg0"), dg1=sem("dg1"),
    )
    P = Prog(nc)
    P.init_sems(sems)
    C = P.c
    ten, vec, sca, gps = C["ten"], C["vec"], C["sca"], C["gps"]
    bank = Banks(P, pb, ten)
    dg = [C["dg0"], C["dg1"]]
    E, W = P.emit, P.wait

    def dump(slot, tens):
        if not dbg:
            return
        W("sync", [(vec, vec.n), (ten, ten.n), (sca, sca.n), (gps, gps.n)])
        ap_s = bass.AP(tens, 0, [[NT * D, 128], [1, NT * D]])
        E("sync", lambda e, _s=slot, _a=ap_s: e.dma_start(
            out=ydbg[_s, :, :], in_=_a), incs=[(C["di"], 16)], chain=False)

    def _ph0(p):
        if phase_limit < p:
            P.disabled = True

    E("vector", lambda e: e.memset(ones[:], 1.0))
    v_ones = vec.n
    E("gpsimd", lambda e: e.affine_select(mask[:], ones[:], [[1, 128]], A.is_ge,
                                          0.0, base=0, channel_multiplier=-1),
      waits=[(vec, v_ones)])
    E("gpsimd", lambda e: e.affine_select(ident[:], ones[:], [[1, 128]],
                                          A.is_equal, 0.0, base=0,
                                          channel_multiplier=-1))
    E("gpsimd", lambda e: e.iota(iota8[:], [[1, 8]], base=0, channel_multiplier=0, allow_small_or_imprecise_dtypes=True))
    g_i8 = gps.n
    E("vector", lambda e: e.tensor_scalar_mul(c8[:], iota8[:], 8.0),
      waits=[(gps, g_i8)])
    ncc = Tl // 128
    E("gpsimd", lambda e: e.iota(ang[:, 0:ncc], [[128, ncc]], base=0,
                                 channel_multiplier=1, allow_small_or_imprecise_dtypes=True))
    E("gpsimd", lambda e: e.iota(ang[:, ncc:2 * ncc], [[-128, ncc]], base=Tl,
                                 channel_multiplier=-1, allow_small_or_imprecise_dtypes=True))
    g_ang = gps.n
    E("gpsimd", lambda e: e.load_library(_lc.mlp), chain=False)
    _ph0(0.2)
    E("scalar", lambda e: e.activation(sin_col[:], ang[:, 0:ncc], AF.Sin,
                                       bias=0.0, scale=PI / (2 * Tl)),
      waits=[(gps, g_ang)])
    E("scalar", lambda e: e.activation(cos_col[:], ang[:, ncc:2 * ncc], AF.Sin,
                                       bias=0.0, scale=PI / (2 * Tl)))
    s_masters = sca.n
    _ph0(0.3)
    for t_ in (fmh, s2, keys_sb):
        E("vector", lambda e, _t=t_: e.memset(_t[:], 0.0))
    E("vector", lambda e: e.memset(idx_tmp[:], 0))
    E("vector", lambda e: e.memset(idx_g[:], 0))
    E("vector", lambda e: e.memset(
        bass.AP(v65, 64, [[NT * H * 65, 128], [65, NT * H]]), 1.0))
    v_zero = vec.n
    _ph0(0.4)
    W("sync", [(vec, v_zero)])
    for j in range(8):
        E("sync", lambda e, _j=j: e.dma_start(
            out=keys_sb[(_j % 2) * 64:(_j % 2) * 64 + 64, _j, :],
            in_=keysd[_j, :, :]),
          incs=[(C["dk"], 16)], chain=False)
    dk_done = C["dk"].n
    _ph0(0.5)

    def set_bias(dram_b):
        W("sync", [(vec, vec.n)])
        E("sync", lambda e, _b=dram_b: e.dma_start(
            out=brep[0:1, :], in_=bass.AP(_b, 0, [[0, 1], [1, D]])),
          incs=[(C["dk"], 16)], chain=False)
        W("gpsimd", [(C["dk"], C["dk"].n)])
        E("gpsimd", lambda e: e.partition_broadcast(
            brep[:, :], brep[0:1, :], channels=128))
        return gps.n

    wsems = [C["dw0"], C["dw1"]]
    wring = {"pending": [], "handles": [], "nq": 0, "consumed": [(vec, 0), (vec, 0)],
             "cnt": [0, 0], "owed": 2}

    def _ring_emit_one():
        if not wring["pending"]:
            wring["owed"] += 1
            return
        (dram, si) = wring["pending"].pop(0)
        i = len(wring["handles"])
        s = i % 2
        war = wring["consumed"][s]
        srcap = bass.AP(dram, si * 128 * 4096, [[4096, 128], [1, 4096]])
        dstap = bass.AP(wbuf, s * 4096, [[2 * 4096, 128], [1, 4096]])
        W("sync", [war])
        E("sync", lambda e, _d=dstap, _s=srcap: e.dma_start(out=_d, in_=_s),
          incs=[(wsems[s], 16)], chain=False)
        wring["cnt"][s] += 1
        wring["handles"].append((s, (wsems[s], 16 * wring["cnt"][s])))

    def ring_queue(dram, si=0, ow=512, kparts=8):
        i = wring["nq"]
        wring["nq"] += 1
        wring["pending"].append((dram, si))
        while wring["owed"] > 0 and wring["pending"]:
            wring["owed"] -= 1
            _ring_emit_one()
        return i

    def ring_get(i):
        assert i < len(wring["handles"]), "slice not yet loaded (ring order bug)"
        return wring["handles"][i]

    def wdone(s):
        wring["consumed"][s] = (ten, ten.n)
        _ring_emit_one()

    def layernorm(src, dst, sqb, extra=()):
        vv0 = vec.n
        E("scalar", lambda e, _s=src(0, 1024), _q=sqb: e.activation(
            _q, _s, AF.Square), waits=[(vec, vv0)] + list(extra))
        ssq = sca.n
        E("vector", lambda e, _s=src(0, 1024): e.tensor_reduce(
            mv[:, 0:1], _s, AXX, A.add))
        E("vector", lambda e, _q=sqb: e.tensor_reduce(mv[:, 1:2], _q, AXX,
                                                      A.add), waits=[(sca, ssq)])
        E("vector", lambda e: e.tensor_scalar_mul(mv[:, 0:1], mv[:, 0:1],
                                                  1.0 / D))
        E("vector", lambda e: e.tensor_scalar_mul(mv[:, 1:2], mv[:, 1:2],
                                                  1.0 / D))
        E("vector", lambda e: e.tensor_tensor(rs_t[:, 0:1], mv[:, 0:1],
                                              mv[:, 0:1], A.mult))
        E("vector", lambda e: e.tensor_tensor(rs_t[:, 0:1], mv[:, 1:2],
                                              rs_t[:, 0:1], A.subtract))
        E("vector", lambda e: e.tensor_scalar(rs_t[:, 0:1], rs_t[:, 0:1], 1e-5,
                                              None, A.add))
        vv = vec.n
        E("scalar", lambda e: e.activation(rs_t[:, 1:2], rs_t[:, 0:1], AF.Sqrt),
          waits=[(vec, vv)])
        ssn = sca.n
        E("vector", lambda e: e.reciprocal(rs_t[:, 0:1], rs_t[:, 1:2]),
          waits=[(sca, ssn)])
        E("vector", lambda e, _s=src(0, 1024), _d=dst(0, 1024): e.tensor_scalar(
            _d, _s, mv[:, 0:1], rs_t[:, 0:1], A.subtract, A.mult))
        return vec.n

    def transpose8(src3, jj, dst3, extra_waits, b0=6, b1=7):
        for half, bi in ((0, b0), (1, b1)):
            bank.open(bi, extra_waits=extra_waits)
            for c in range(4):
                cc = half * 4 + c
                ap_o = pb[bi][:, c * 128:(c + 1) * 128]
                ap_i = src3[:, jj, cc * 128:(cc + 1) * 128]
                E("tensor", lambda e, _o=ap_o, _i=ap_i, _st=(c == 0), _sp=(c == 3):
                  e.matmul(_o, _i, ident[:], start=_st, stop=_sp,
                           is_transpose=True),
                  incs=([(ten, 1)] if c == 3 else []), chain=False)
            bank.close(bi)
            ap_d = dst3[:, jj, half * 512:(half + 1) * 512]
            E("vector", lambda e, _d=ap_d, _bi=bi: e.tensor_copy(
                _d, pb[_bi][:, 0:512]), waits=bank.rw(bi))
            bank.reader(bi, vec)
        return vec.n

    nx = 0
    ny = 0
    x_gate = (vec, 0)
    state_v = {h: 0 for h in range(H)}
    kscr_free = [0, 0]
    fmh_free = [0, 0]
    v_fm = [0, 0]

    gspecs = []
    gtails = []
    gstate = {"issued": 0}
    gtmpl = {"raw": None}

    def gather_emit(e, ap_o, ap_i):
        if gtmpl["raw"] is None:
            inst = e.dma_gather(out_ap=ap_o, in_ap=tab[:, :], idxs_ap=ap_i,
                                num_idxs=128, num_idxs_reg=128, elem_size=2 * D)
            gtmpl["raw"] = inst.ins
            return inst
        t = gtmpl["raw"]
        raw = mybir.InstDMAGatherAnt(
            name=nc.get_next_instruction_name(), engine=mybir.EngineType.Pool,
            ins=[t.ins[0], e.lower_ap(ap_i), t.ins[2]],
            outs=[e.lower_ap(ap_o)],
            transpose=False, num_idxs=128, elem_size=2 * D,
            stride_bytes_256=t.stride_bytes_256, gen_mode=0,
            single_packet=True, queue_num=0)
        return e.add_instruction(raw)

    def issue_gathers(upto):
        while gstate["issued"] < min(upto, len(gspecs)):
            gi = gstate["issued"]
            gstate["issued"] += 1
            (tj, gg, vsh) = gspecs[gi]
            slot = gi % GB
            ws = [(vec, vsh)]
            if gi - GB >= 0:
                ws.append(gtails[gi - GB])
            ap_o = bass.AP(gbuf, slot * 2 * D,
                           [[GB * 2 * D, 128], [2 * D, 1], [1, 2 * D]])
            ap_i = bass.AP(idx_g, tj * 256 + gg * 8, [[NT * 256, 128], [1, 8]])
            W("gpsimd", ws)
            E("gpsimd", lambda e, _o=ap_o, _i=ap_i: gather_emit(e, _o, _i),
              incs=[(dg[slot], 16)], chain=False)

    _ph = _ph0

    for sc in range(n_sc):
        t0 = sc * SC
        gt = t0 // 128
        _ph(1)
        W("sync", [x_gate])
        for j in range(NT):
            ap_d = x_sc[:, j, :]
            ap_s = x_in[t0 + j * 128:t0 + (j + 1) * 128, :]
            E("sync", lambda e, _d=ap_d, _s=ap_s: e.dma_start(out=_d, in_=_s),
              incs=[(C["dx"], 16)], chain=False)
            nx += 1
        dx_need = 16 * nx


        _ph(2)
        xT_v = [0] * NT
        for j in range(NT):
            W("vector", [(C["dx"], dx_need), (C["dk"], dk_done)])
            lv = layernorm(lambda lo, hi, _j=j: x_sc[:, _j, lo:hi],
                           lambda lo, hi, _j=j: xln[:, _j, lo:hi],
                           k_tm[:, 0, :], extra=[(C["dx"], dx_need)])
            xT_v[j] = transpose8(xln, j, xlnT, [(vec, lv)])

        if sc == 0:
            dump(0, xln)
            dump(1, xlnT)
        _ph(3)
        qkv = [(wq_d, 0, 0), (wq_d, 512, 0), (wk_d, 0, 1), (wk_d, 512, 1),
               (wv_d, 0, 2), (wv_d, 512, 2)]
        qkv_ids = [ring_queue(d, o // 512) for (d, o, _) in qkv] if do_proj else []
        prod = [0] * NT
        g_b = 0
        for pi_, (dram, of0, which) in enumerate(qkv if do_proj else []):
            if pi_ % 2 == 0:
                g_b = set_bias([bqd, bkd, bvd][which])
            wslot, wwait = ring_get(qkv_ids[pi_])
            oh = of0 // 512
            for j in range(NT):
                bi = j % 2
                bank.open(bi, extra_waits=[wwait, (vec, xT_v[j])])
                for k in range(8):
                    ap_l = xlnT[:, j, k * 128:(k + 1) * 128]
                    ap_r = wbuf[:, wslot, k, 0:512]
                    E("tensor", lambda e, _bi=bi, _l=ap_l, _r=ap_r,
                      _st=(k == 0), _sp=(k == 7): e.matmul(
                          pb[_bi][:, 0:512], _l, _r, start=_st, stop=_sp),
                      incs=([(ten, 1)] if k == 7 else []), chain=False)
                bank.close(bi)
                if which in (0, 1):
                    dst = q_tm if which == 0 else k_tm
                    ap_b = brep[:, oh * 512:(oh + 1) * 512]
                    E("vector", lambda e, _bi=bi, _b=ap_b: e.tensor_tensor(
                        dscr[:, 0:512], pb[_bi][:, 0:512], _b, A.add),
                      waits=bank.rw(bi) + [(gps, g_b)])
                    bank.reader(bi, vec)
                    ap_d = dst[:, j, oh * 512:(oh + 1) * 512]
                    E("vector", lambda e, _d=ap_d: e.tensor_scalar_max(
                        _d, dscr[:, 0:512], 0.0))
                else:
                    ap_d = bass.AP(v65, j * H * 65 + oh * 8 * 65,
                                   [[NT * H * 65, 128], [65, 8], [1, 64]])
                    ap_b = brep[:, oh * 512:(oh + 1) * 512]
                    E("vector", lambda e, _bi=bi, _d=ap_d, _b=ap_b:
                      e.tensor_tensor(_d, pb[_bi][:, 0:512], _b, A.add),
                      waits=bank.rw(bi) + [(gps, g_b)])
                    bank.reader(bi, vec)
                prod[j] = vec.n
            wdone(wslot)

        if sc == 0:
            dump(2, q_tm)
            dump(3, k_tm)
        _ph(4)
        ao_v = [0] * NT

        def attn_stage(j, h):
            par = h % 2
            hsl = slice(h * DH, (h + 1) * DH)
            ap_q = q_tm[:, j, hsl]
            ap_k = k_tm[:, j, hsl]
            ap_cc = cos_col[:, gt + j:gt + j + 1]
            ap_sc = sin_col[:, gt + j:gt + j + 1]
            for si_, (ap_in, ap_col) in enumerate(
                    ((ap_q, ap_cc), (ap_q, ap_sc), (ap_k, ap_cc), (ap_k, ap_sc))):
                E("vector", lambda e, _i=ap_in, _c=ap_col, _p=par, _s=si_:
                  e.tensor_scalar(kscr[:, _p, _s, :], _i, _c, None, A.mult))
            v_scale = vec.n
            bank.open(5, extra_waits=[(vec, v_scale)])
            for si_ in range(2):
                ap_in = bass.AP(kscr, par * 256 + si_ * 128,
                                [[512, 128], [1, 128]])
                ap_o = pb[5][:, si_ * 128:(si_ + 1) * 128]
                E("tensor", lambda e, _o=ap_o, _i=ap_in, _st=(si_ == 0),
                  _sp=(si_ == 1): e.matmul(_o, _i, ident[:], start=_st,
                                           stop=_sp, is_transpose=True),
                  incs=([(ten, 1)] if si_ == 1 else []), chain=False)
            bank.close(5)
            E("vector", lambda e, _p=par: e.tensor_copy(
                bass.AP(fmh, _p * 256, [[512, 128], [1, 256]]),
                pb[5][:, 0:256]), waits=bank.rw(5))
            bank.reader(5, vec)
            v_fm[par] = vec.n

        def fm_q(par):
            return bass.AP(fmh, par * 256, [[512, 128], [1, 128]])

        def fm_k(par):
            return bass.AP(fmh, par * 256 + 128, [[512, 128], [1, 128]])

        for j in range(NT if do_attn else 0):
            attn_stage(j, 0)
            for h in range(H):
                par = h % 2
                hp = h // 2
                ab = 1 + par
                bank.open(ab, extra_waits=[(vec, v_fm[par])])
                E("tensor", lambda e, _ab=ab, _p=par: e.matmul(
                    pb[_ab][:, 0:128], fm_k(_p), fm_q(_p),
                    start=True, stop=True), incs=[(ten, 1)], chain=False)
                bank.close(ab)
                E("vector", lambda e, _ab=ab, _p=par: e.tensor_tensor(
                    amt[:, _p, :], pb[_ab][:, 0:128], mask[:], A.mult),
                  waits=bank.rw(ab))
                bank.reader(ab, vec)
                v_amt = vec.n
                if h + 1 < H:
                    attn_stage(j, h + 1)
                ob = 3
                if par == 0:
                    bank.open(ob, extra_waits=[(vec, v_amt), (vec, state_v[h])])
                else:
                    W("tensor", [(vec, v_amt), (vec, state_v[h])])
                tp = (0, 64 * par)
                po = slice(64 * par, 64 * par + 64)
                ap_v = bass.AP(v65, j * H * 65 + h * 65,
                               [[NT * H * 65, 128], [1, 64]])
                ap_s = bass.AP(s2, h * 65, [[H * 65, 128], [1, 64]])
                ap_z = bass.AP(s2, h * 65 + 64, [[H * 65, 128], [0, 64]])
                mm4o = [
                    (pb[ob][po, 0:128], ap_v, amt[:, par, :], True, False),
                    (pb[ob][po, 0:128], ap_s, fm_q(par), False, True),
                    (pb[ob][po, 128:256], ones[:, 0:64], amt[:, par, :], True, False),
                    (pb[ob][po, 128:256], ap_z, fm_q(par), False, True),
                ]
                for mi, (o_, l_, r_, st_, sp_) in enumerate(mm4o):
                    E("tensor", lambda e, _o=o_, _l=l_, _r=r_, _st=st_, _sp=sp_,
                      _tp=tp: e.matmul(_o, _l, _r, start=_st, stop=_sp,
                                       tile_position=_tp),
                      incs=([(ten, 1)] if mi == 3 else []), chain=False)
                if par == 1:
                    bank.close(ob)
                stb = 4
                if par == 0:
                    bank.open(stb)
                co = 256 * par
                ap_v65 = bass.AP(v65, j * H * 65 + h * 65,
                                 [[NT * H * 65, 128], [1, 65]])
                E("tensor", lambda e, _p=par, _v=ap_v65, _co=co: e.matmul(
                    pb[stb][:, _co:_co + 65],
                    bass.AP(kscr, _p * 256 + 128, [[512, 128], [1, 128]]),
                    _v, start=True, stop=True),
                  incs=[(ten, 1)], chain=False)
                if par == 1:
                    bank.close(stb)

                if par == 1:
                    odw = bank.rw(ob)
                    E("vector", lambda e: e.tensor_scalar_max(
                        dscr[:, 0:128], pb[3][:, 128:256], 1e-6), waits=odw)
                    bank.reader(ob, vec)
                    E("vector", lambda e: e.reciprocal(dscr[:, 128:256],
                                                       dscr[:, 0:128]))
                    ap_ao = xln[:, j, hp * 128:(hp + 1) * 128]
                    E("vector", lambda e, _d=ap_ao: e.tensor_tensor(
                        _d, pb[3][:, 0:128], dscr[:, 128:256], A.mult))
                    bank.reader(ob, vec)
                    stw = bank.rw(stb)
                    for hh, cc in ((h - 1, 0), (h, 256)):
                        ap_t = s2[:, hh, :]
                        ap_p = pb[stb][:, cc:cc + 65]
                        E("vector", lambda e, _t=ap_t, _p=ap_p:
                          e.tensor_tensor(_t, _t, _p, A.add),
                          waits=(stw if cc == 0 else ()))
                        state_v[hh] = vec.n
                    bank.reader(stb, vec)
            ao_v[j] = vec.n

        if sc == 0:
            dump(5, xln)
        _ph(5)
        oh_ids = [ring_queue(wo_d, 0), ring_queue(wo_d, 1)] if do_wo else []
        if do_wo:
            g_b = set_bias(bod)
        for ohx in range(2 if do_wo else 0):
            wslot, wwait = ring_get(oh_ids[ohx])
            for j in range(NT):
                bi = j % 2
                bank.open(bi, extra_waits=[wwait, (vec, ao_v[j])])
                for k in range(8):
                    ap_l = xln[:, j, k * 128:(k + 1) * 128]
                    ap_r = wbuf[:, wslot, k, 0:512]
                    E("tensor", lambda e, _bi=bi, _l=ap_l, _r=ap_r,
                      _st=(k == 0), _sp=(k == 7): e.matmul(
                          pb[_bi][:, 0:512], _l, _r, start=_st, stop=_sp),
                      incs=([(ten, 1)] if k == 7 else []), chain=False)
                bank.close(bi)
                ap_d = xlnT[:, j, ohx * 512:(ohx + 1) * 512]
                ap_b = brep[:, ohx * 512:(ohx + 1) * 512]
                E("vector", lambda e, _bi=bi, _d=ap_d, _b=ap_b: e.tensor_tensor(
                    _d, pb[_bi][:, 0:512], _b, A.add),
                  waits=bank.rw(bi) + [(gps, g_b)])
                bank.reader(bi, vec)
            wdone(wslot)
        sq_wo = [0] * NT
        for j in range(NT):
            layernorm(lambda lo, hi, _j=j: xlnT[:, _j, lo:hi],
                      lambda lo, hi: dscr[:, lo:hi], k_tm[:, 0, :])
            sq_wo[j] = sca.n
            ap_x = x_sc[:, j, :]
            E("vector", lambda e, _x=ap_x: e.tensor_tensor(_x, _x, dscr[:, :],
                                                           A.add))

        if sc == 0:
            dump(6, xlnT)
            dump(7, x_sc)
        _ph(6)
        W("vector", [(ten, ten.n)])
        E("vector", lambda e: e.memset(mlpacc[:], 0.0))
        for j in range(NT):
            lv = layernorm(lambda lo, hi, _j=j: x_sc[:, _j, lo:hi],
                           lambda lo, hi, _j=j: xln[:, _j, lo:hi],
                           k_tm[:, 0, :])
            xT_v[j] = transpose8(xln, j, xlnT, [(vec, lv), (sca, sq_wo[j])])

        _ph(7)
        qf_id = ring_queue(wqf_d, 0)
        qslot, qwait = ring_get(qf_id)
        sc_gbase = len(gspecs)

        for j in range(NT if do_expert else 0):
            bank.open(0, extra_waits=[qwait, (vec, xT_v[j])])
            for oc in range(4):
                for k in range(8):
                    ap_l = wbuf[:, qslot, k, oc * 128:(oc + 1) * 128]
                    ap_r = xlnT[:, j, k * 128:(k + 1) * 128]
                    ap_o = pb[0][:, oc * 128:(oc + 1) * 128]
                    E("tensor", lambda e, _o=ap_o, _l=ap_l, _r=ap_r,
                      _st=(k == 0), _sp=(k == 7): e.matmul(_o, _l, _r,
                                                           start=_st, stop=_sp),
                      incs=([(ten, 1)] if (k == 7 and oc == 3) else []),
                      chain=False)
            bank.close(0)
            E("scalar", lambda e: e.copy(qf_all, pb[0][:, 0:512]),
              waits=bank.rw(0))
            bank.reader(0, sca)
            s_qf = sca.n
            for half, bi in ((0, 2), (1, 6)):
                bank.open(bi, extra_waits=[(sca, s_qf), (C["dk"], dk_done)])
                for q2 in range(4):
                    p8 = half * 4 + q2
                    ap_l = qf_chunk(p8 // 2)
                    ap_r = keys_sb[:, p8, :]
                    ap_o = pb[bi][:, q2 * 128:(q2 + 1) * 128]
                    E("tensor", lambda e, _o=ap_o, _l=ap_l, _r=ap_r: e.matmul(
                        _o, _l, _r, start=True, stop=True),
                      incs=([(ten, 1)] if q2 == 3 else []), chain=False)
                bank.close(bi)
                ap_d = sim_half(half)
                E("scalar", lambda e, _d=ap_d, _bi=bi: e.copy(_d, pb[_bi][:, 0:512]),
                  waits=bank.rw(bi))
                bank.reader(bi, sca)
            s_sim = sca.n
            W("vector", [(sca, s_sim)])
            for p8 in range(8):
                E("vector", lambda e, _p=p8: e.max(sxy[:, _p, :], sim_slot(_p)))
                E("vector", lambda e, _p=p8: e.max_index(idxu[:, :], sxy[:, _p, :],
                                                         sim_slot(_p)))
                E("vector", lambda e, _p=p8: e.tensor_copy(ixy[:, _p, :],
                                                           idxu[:, :]))
            E("vector", lambda e: e.tensor_tensor(
                bass.AP(ascore, 0, [[256, 128], [64, 4], [8, 8], [1, 8]]),
                bass.AP(sxy, 0, [[64, 128], [8, 4], [1, 8], [0, 8]]),
                bass.AP(sxy, 32, [[64, 128], [8, 4], [0, 8], [1, 8]]),
                A.add))
            for hh in range(4):
                E("vector", lambda e, _h=hh: e.max(score8[:, _h, :],
                                                   ascore[:, _h, :]))
                E("vector", lambda e, _h=hh: e.max_index(idxu[:, :],
                                                         score8[:, _h, :],
                                                         ascore[:, _h, :]))
                E("vector", lambda e, _h=hh: e.tensor_copy(pk_f[:, _h, :],
                                                           idxu[:, :]))
            E("vector", lambda e: e.tensor_tensor(
                bass.AP(ascore, 0, [[256, 128], [56, 4], [7, 8], [1, 7]]),
                bass.AP(pk_f, 0, [[32, 128], [8, 4], [1, 8], [0, 7]]),
                bass.AP(c8, 1, [[8, 128], [0, 4], [0, 8], [1, 7]]),
                A.is_ge))
            E("vector", lambda e: e.tensor_reduce(
                isel[:, :, :],
                bass.AP(ascore, 0, [[256, 128], [56, 4], [7, 8], [1, 7]]),
                AXX, A.add))
            E("vector", lambda e: e.scalar_tensor_tensor(
                jsel[:], isel[:], -8.0, pk_f[:], A.mult, A.add))
            for selt, off in ((isel, 0), (jsel, 32)):
                E("vector", lambda e, _s=selt: e.tensor_tensor(
                    bass.AP(ascore, 0, [[256, 128], [64, 4], [8, 8], [1, 8]]),
                    bass.AP(_s, 0, [[32, 128], [8, 4], [1, 8], [0, 8]]),
                    bass.AP(iota8, 0, [[8, 128], [0, 4], [0, 8], [1, 8]]),
                    A.is_equal))
                E("vector", lambda e, _o=off: e.tensor_tensor(
                    bass.AP(ascore, 0, [[256, 128], [64, 4], [8, 8], [1, 8]]),
                    bass.AP(ascore, 0, [[256, 128], [64, 4], [8, 8], [1, 8]]),
                    bass.AP(ixy, _o, [[64, 128], [8, 4], [0, 8], [1, 8]]),
                    A.mult))
                E("vector", lambda e, _d=selt: e.tensor_reduce(
                    _d[:, :, :],
                    bass.AP(ascore, 0, [[256, 128], [64, 4], [8, 8], [1, 8]]),
                    AXX, A.add))
            E("vector", lambda e: e.scalar_tensor_tensor(
                idx_f[:], bass.AP(isel, 0, [[32, 128], [1, 32]]), 128.0,
                bass.AP(jsel, 0, [[32, 128], [1, 32]]), A.mult, A.add))
            ap_smw3 = bass.AP(smw, j * 32, [[NT * 32, 128], [8, 4], [1, 8]])
            E("vector", lambda e, _d=ap_smw3: e.tensor_tensor(
                _d, bass.AP(score8, 0, [[32, 128], [8, 4], [1, 8]]),
                bass.AP(score8, 0, [[32, 128], [8, 4], [0, 8]]), A.subtract))
            vv1 = vec.n
            E("scalar", lambda e, _d=ap_smw3: e.activation(_d, _d, AF.Exp),
              waits=[(vec, vv1)])
            ss1 = sca.n
            E("vector", lambda e, _s=ap_smw3: e.tensor_reduce(
                sm_s[:, 0:4], _s, AXX, A.add), waits=[(sca, ss1)])
            E("vector", lambda e: e.reciprocal(sm_s[:, 4:8], sm_s[:, 0:4]))
            E("vector", lambda e, _d=ap_smw3: e.tensor_tensor(
                _d, _d, bass.AP(sm_s, 4, [[8, 128], [1, 4], [0, 8]]), A.mult))
            ap_ii = idx_i16[:, j, :]
            E("vector", lambda e, _d=ap_ii: e.tensor_copy(_d, idx_f[:]))
            v_i16 = vec.n
            W("sync", [(vec, v_i16)])
            for s8 in range(8):
                for dup in range(2):
                    ap_d = bass.AP(idx_tmp,
                                   (dup * 16) * (NT * 256) + j * 256 + s8 * 32,
                                   [[NT * 256, 16], [1, 32]])
                    ap_s = bass.AP(idx_i16, (16 * s8) * (NT * 32) + j * 32,
                                   [[NT * 32, 16], [1, 32]])
                    E("sync", lambda e, _d=ap_d, _s=ap_s: e.dma_start(
                        out=_d, in_=_s), incs=[(C["di"], 16)], chain=False)
            ap_gd = bass.AP(idx_g, j * 256, [[NT * 256, 32], [8, 32], [1, 8]])
            ap_gs = bass.AP(idx_tmp, j * 256, [[NT * 256, 32], [1, 32], [32, 8]])
            E("vector", lambda e, _d=ap_gd, _s=ap_gs: e.tensor_copy(_d, _s),
              waits=[(C["di"], C["di"].n)])
            v_shuf = vec.n
            for g in range(NGRP):
                gspecs.append((j, g, v_shuf))
                gtails.append(None)
            if j == 0 and sc == 0:
                issue_gathers(GB)
        wdone(qslot)

        def consume_unit(u):
            tj = u // 16
            u2 = u % 16
            base = sc_gbase + tj * NGRP
            for gg in (2 * u2, 2 * u2 + 1):
                gi = base + gg
                slot = gi % GB
                dgw = (dg[slot], 16 * (gi // GB + 1))
                ap_g = bass.AP(gbuf, slot * 2 * D, [[GB * 2 * D, 128], [1, D]])
                ap_x = xln[:, tj, :]
                E("vector", lambda e, _g=ap_g, _x=ap_x: e.tensor_tensor(
                    dscr[:, :], _g, _x, A.mult), waits=[dgw])
                ap_w = wdot[:, tj, gg:gg + 1]
                E("vector", lambda e, _w=ap_w: e.tensor_reduce(
                    _w, dscr[:, :], AXX, A.add))
            vv2 = vec.n
            ap_wd = wdot[:, tj, 2 * u2:2 * u2 + 2]
            ap_wf = wfin[:, tj, 2 * u2:2 * u2 + 2]
            ap_sm = smw[:, tj, 2 * u2:2 * u2 + 2]
            E("scalar", lambda e, _i=ap_wd, _o=ap_wf: e.activation(
                _o, _i, AF.Silu), waits=[(vec, vv2)])
            ss2 = sca.n
            E("vector", lambda e, _f=ap_wf, _s=ap_sm: e.tensor_tensor(
                _f, _f, _s, A.mult), waits=[(sca, ss2)])
            for gg in (2 * u2, 2 * u2 + 1):
                gi = base + gg
                slot = gi % GB
                ap_u = bass.AP(gbuf, slot * 2 * D + D, [[GB * 2 * D, 128], [1, D]])
                ap_m = mlpacc[:, tj, :]
                ap_w1 = wfin[:, tj, gg:gg + 1]
                E("vector", lambda e, _u=ap_u, _m=ap_m, _w=ap_w1:
                  e.scalar_tensor_tensor(_m, _u, _w, _m, A.mult, A.add))
                gtails[gi] = (vec, vec.n)

        cu = {"n": 0}

        def drain(kunits):
            if not do_expert:
                return
            for _ in range(kunits):
                u = cu["n"]
                if u >= NT * 16:
                    return
                cu["n"] += 1
                gu_base = sc_gbase + 2 * u
                issue_gathers(gu_base + 2)
                consume_unit(u)
                issue_gathers(gu_base + 4)

        if sc == 0:
            dump(8, xln)
        _ph(8)
        mlp_ids = []
        for ic in range(8 if do_mlp else 0):
            mlp_ids.append(ring_queue(wg_d, ic))
            mlp_ids.append(ring_queue(wu_d, ic))
            mlp_ids.append(ring_queue(wd_d, ic))
        for ic in range(8 if do_mlp else 0):
            gslot, gwait = ring_get(mlp_ids[3 * ic])
            sgl = [0] * NT
            for j in range(NT):
                bank.open(4, extra_waits=[gwait, (vec, xT_v[j])])
                for k in range(8):
                    ap_l = xlnT[:, j, k * 128:(k + 1) * 128]
                    ap_r = wbuf[:, gslot, k, 0:512]
                    E("tensor", lambda e, _l=ap_l, _r=ap_r, _st=(k == 0),
                      _sp=(k == 7): e.matmul(pb[4][:, 0:512], _l, _r,
                                             start=_st, stop=_sp),
                      incs=([(ten, 1)] if k == 7 else []), chain=False)
                bank.close(4)
                ap_gs2 = g_sc[:, j, :]
                E("scalar", lambda e, _d=ap_gs2: e.activation(
                    _d, pb[4][:, 0:512], AF.Silu), waits=bank.rw(4))
                bank.reader(4, sca)
                sgl[j] = sca.n
                drain(1)
            wdone(gslot)
            uslot, uwait = ring_get(mlp_ids[3 * ic + 1])
            gTl = [0] * NT
            for j in range(NT):
                bank.open(5, extra_waits=[uwait, (vec, xT_v[j])])
                for k in range(8):
                    ap_l = xlnT[:, j, k * 128:(k + 1) * 128]
                    ap_r = wbuf[:, uslot, k, 0:512]
                    E("tensor", lambda e, _l=ap_l, _r=ap_r, _st=(k == 0),
                      _sp=(k == 7): e.matmul(pb[5][:, 0:512], _l, _r,
                                             start=_st, stop=_sp),
                      incs=([(ten, 1)] if k == 7 else []), chain=False)
                bank.close(5)
                ap_gs2 = g_sc[:, j, :]
                E("vector", lambda e, _g=ap_gs2: e.tensor_tensor(
                    gu_sb[:, :], _g, pb[5][:, 0:512], A.mult),
                  waits=bank.rw(5) + [(sca, sgl[j])])
                bank.reader(5, vec)
                v_gu = vec.n
                bank.open(6, extra_waits=[(vec, v_gu)])
                for c in range(4):
                    ap_i = gu_sb[:, c * 128:(c + 1) * 128]
                    ap_o = pb[6][:, c * 128:(c + 1) * 128]
                    E("tensor", lambda e, _o=ap_o, _i=ap_i, _st=(c == 0),
                      _sp=(c == 3): e.matmul(_o, _i, ident[:], start=_st,
                                             stop=_sp, is_transpose=True),
                      incs=([(ten, 1)] if c == 3 else []), chain=False)
                bank.close(6)
                ap_gT = bass.AP(guT_sc, j * 512, [[NT * 512, 128], [1, 512]])
                E("scalar", lambda e, _d=ap_gT: e.activation(
                    _d, pb[6][:, 0:512], AF.Copy), waits=bank.rw(6))
                bank.reader(6, sca)
                gTl[j] = sca.n
            wdone(uslot)
            dslot, dwait = ring_get(mlp_ids[3 * ic + 2])
            for j in range(NT):
                for oc2 in range(2):
                    bi = (j % 2) * 2 + oc2
                    bank.open(bi, extra_waits=[dwait, (sca, gTl[j])])
                    for k in range(4):
                        ap_l = bass.AP(guT_sc, j * 512 + k * 128,
                                       [[NT * 512, 128], [1, 128]])
                        ap_r = wbuf[:, dslot, k * 2 + oc2, 0:512]
                        E("tensor", lambda e, _bi=bi, _l=ap_l, _r=ap_r,
                          _st=(k == 0), _sp=(k == 3): e.matmul(
                              pb[_bi][:, 0:512], _l, _r, start=_st, stop=_sp),
                          incs=([(ten, 1)] if k == 3 else []), chain=False)
                    bank.close(bi)
                    ap_m = mlpacc[:, j, oc2 * 512:(oc2 + 1) * 512]
                    E("vector", lambda e, _bi=bi, _m=ap_m: e.scalar_tensor_tensor(
                        _m, pb[_bi][:, 0:512], 1.0, _m, A.mult, A.add),
                      waits=bank.rw(bi))
                    bank.reader(bi, vec)
                drain(1)
            wdone(dslot)
        drain(NT * 16)

        if sc == 0:
            dump(9, mlpacc)
        _ph(9)
        for j in range(NT):
            layernorm(lambda lo, hi, _j=j: mlpacc[:, _j, lo:hi],
                      lambda lo, hi: dscr[:, lo:hi], k_tm[:, 0, :])
            ap_x = x_sc[:, j, :]
            E("vector", lambda e, _x=ap_x: e.tensor_tensor(_x, _x, dscr[:, :],
                                                           A.add))
            v_res = vec.n
            ap_y = y_out[t0 + j * 128:t0 + (j + 1) * 128, :]
            W("sync", [(vec, v_res)])
            E("sync", lambda e, _y=ap_y, _x=ap_x: e.dma_start(out=_y, in_=_x),
              incs=[(C["dy"], 16)], chain=False)
            ny += 1
        x_gate = (C["dy"], 16 * ny)

    W("sync", [(C["dy"], 16 * ny)])

    with nc.Block() as block:
        @block.tensor
        def _(e):
            for fn in P.q["tensor"]:
                fn(e)

        @block.vector
        def _(e):
            for fn in P.q["vector"]:
                fn(e)

        @block.scalar
        def _(e):
            for fn in P.q["scalar"]:
                fn(e)

        @block.gpsimd
        def _(e):
            for fn in P.q["gpsimd"]:
                fn(e)

        @block.sync
        def _(e):
            for fn in P.q["sync"]:
                fn(e)

    for g in reversed(ctx):
        g.__exit__(None, None, None)
    return nc


def _pack_cols(M):
    R, C = M.shape
    kp = R // 128
    ns = C // 512
    return np.ascontiguousarray(
        M.reshape(kp, 128, ns, 512).transpose(2, 1, 0, 3).reshape(ns, 128, kp * 512))


def _pack_rows(M):
    R, C = M.shape
    ns = R // 512
    return np.ascontiguousarray(
        M.reshape(ns, 4, 128, C).transpose(0, 2, 1, 3).reshape(ns, 128, 4 * C))


def host_pack(x, Wq, bq, Wk, bk, Wv, bv, Wo, bo, fWq, keys, down, up, Wg, Wu, Wd):
    f32 = np.float32

    def c_(a):
        return np.ascontiguousarray(np.asarray(a), dtype=f32)

    keys_np = np.asarray(keys)
    keys_h = np.empty((8, 64, 128), f32)
    for p in range(2):
        for h in range(4):
            keys_h[p * 4 + h] = np.transpose(keys_np[h, :, p, :])
    return dict(
        x=c_(x),
        wq=_pack_cols(c_(Wq)), wk=_pack_cols(c_(Wk)),
        wv=_pack_cols(c_(Wv)), wo=_pack_cols(c_(Wo)),
        bq=c_(bq), bk=c_(bk), bv=c_(bv), bo=c_(bo),
        wqf=_pack_cols(c_(np.transpose(np.asarray(fWq)))),
        keysT=c_(keys_h),
        wg=_pack_cols(c_(np.transpose(np.asarray(Wg)))),
        wu=_pack_cols(c_(np.transpose(np.asarray(Wu)))),
        wd=_pack_rows(c_(np.transpose(np.asarray(Wd)))),
        tab=c_(np.concatenate([np.asarray(down), np.asarray(up)], axis=1)))


def kernel(L0_attn_x, L0_attn_Wq, L0_attn_bq, L0_attn_Wk, L0_attn_bk,
           L0_attn_Wv, L0_attn_bv, L0_attn_Wo, L0_attn_bo, L0_ffn_Wq,
           L0_ffn_keys, L0_ffn_down_embed, L0_ffn_up_embed, L0_ffn_Wg,
           L0_ffn_Wu, L0_ffn_Wd):
    B = int(np.asarray(L0_attn_x).shape[0])
    xs = np.asarray(L0_attn_x)
    base = host_pack(xs[0], L0_attn_Wq, L0_attn_bq, L0_attn_Wk, L0_attn_bk,
                     L0_attn_Wv, L0_attn_bv, L0_attn_Wo, L0_attn_bo, L0_ffn_Wq,
                     L0_ffn_keys, L0_ffn_down_embed, L0_ffn_up_embed, L0_ffn_Wg,
                     L0_ffn_Wu, L0_ffn_Wd)
    nc = build_core_program()
    lower_extended_insts(nc)
    in_maps = []
    for c in range(B):
        m = dict(base)
        m["x"] = np.ascontiguousarray(xs[c], dtype=np.float32)
        in_maps.append(m)
    res = run_bass_kernel_spmd(nc, in_maps, core_ids=list(range(B)))
    out = np.stack([np.asarray(res.results[c]["y"]) for c in range(B)], axis=0)
    return out.astype(np.float32)


# version 08
# speedup vs baseline: 1.2323x; 1.0737x over previous
import numpy as np
import concourse.bass as bass
import concourse.mybir as mybir
import concourse.library_config as _lc
from concourse.library_overlay import lower_extended_insts
from concourse.bass_utils import run_bass_kernel_spmd

F32 = mybir.dt.float32
I16 = mybir.dt.int16
U32 = mybir.dt.uint32
AF = mybir.ActivationFunctionType
A = mybir.AluOpType
AXX = mybir.AxisListType.X

D = 1024
H = 16
DH = 64
INTER = 4096
NEXP = 16384
PI = float(np.pi)


class Ctr:
    def __init__(self, h):
        self.h = h
        self.n = 0


class Prog:
    ENGS = ("tensor", "vector", "scalar", "gpsimd", "sync")

    def __init__(self, nc):
        self.nc = nc
        self.q = {e: [] for e in self.ENGS}
        self.chain = {}
        self.disabled = False

    def init_sems(self, sems):
        self.c = {k: Ctr(v) for k, v in sems.items()}
        self.chain = {"vector": self.c["vec"], "scalar": self.c["sca"],
                      "gpsimd": self.c["gps"]}

    @staticmethod
    def _coalesce(w):
        m = {}
        out = []
        for (h, v) in w:
            k = id(h)
            if k in m:
                i = m[k]
                if v > out[i][1]:
                    out[i] = (h, v)
            else:
                m[k] = len(out)
                out.append((h, v))
        return out

    def emit(self, eng, fn, waits=(), incs=(), chain=True):
        if self.disabled:
            return None
        chain_ctr = self.chain.get(eng) if chain else None
        w = self._coalesce([(c.h, v) for (c, v) in waits if v > 0])
        amts = []
        if chain_ctr is not None:
            w.append((chain_ctr.h, chain_ctr.n))
            chain_ctr.n += 1
            amts.append((chain_ctr.h, 1))
        for (c, amt) in incs:
            c.n += amt
            amts.append((c.h, amt))

        def run(e, _fn=fn, _w=w, _amts=amts):
            for (h, v) in _w[1:]:
                e.wait_ge(h, v)
            inst = _fn(e)
            if _w:
                inst._wait_ge(_w[0][0], _w[0][1])
            for (h, amt) in _amts:
                inst.then_inc(h, amt)
            return inst

        self.q[eng].append(run)
        return chain_ctr.n if chain_ctr is not None else None

    def wait(self, eng, waits):
        if self.disabled:
            return
        w = self._coalesce([(c.h, v) for (c, v) in waits if v > 0])
        if not w:
            return

        def run(e, _w=w):
            for (h, v) in _w:
                e.wait_ge(h, v)

        self.q[eng].append(run)


class Banks:
    def __init__(self, prog, psum_tensors, ten_ctr):
        self.p = prog
        self.b = psum_tensors
        self.ten = ten_ctr
        self.writer = [0] * len(self.b)
        self.readers = [[] for _ in self.b]

    def open(self, i, extra_waits=()):
        ws = list(self.readers[i]) + list(extra_waits)
        self.p.wait("tensor", ws)
        self.readers[i] = []
        return self.b[i]

    def close(self, i):
        self.writer[i] = self.ten.n

    def rw(self, i):
        return [(self.ten, self.writer[i])]

    def reader(self, i, ctr):
        self.readers[i].append((ctr, ctr.n))


def build_core_program(Tl=4096, SC=512, race=False, do_attn=True, do_expert=True, do_mlp=True, do_proj=True, do_wo=True, phase_limit=99, dbg=False):
    NT = SC // 128
    n_sc = Tl // SC
    GB = 2
    NGRP = 32

    nc = bass.Bass("TRN2", target_bir_lowering=False, detect_race_conditions=race,
                   dynamic_dma_scratch_size=16384)

    x_in = nc.dram_tensor("x", [Tl, D], F32, kind="ExternalInput")
    wq_d = nc.dram_tensor("wq", [2, 128, 4096], F32, kind="ExternalInput")
    wk_d = nc.dram_tensor("wk", [2, 128, 4096], F32, kind="ExternalInput")
    wv_d = nc.dram_tensor("wv", [2, 128, 4096], F32, kind="ExternalInput")
    wo_d = nc.dram_tensor("wo", [2, 128, 4096], F32, kind="ExternalInput")
    bqd = nc.dram_tensor("bq", [D], F32, kind="ExternalInput")
    bkd = nc.dram_tensor("bk", [D], F32, kind="ExternalInput")
    bvd = nc.dram_tensor("bv", [D], F32, kind="ExternalInput")
    bod = nc.dram_tensor("bo", [D], F32, kind="ExternalInput")
    wqf_d = nc.dram_tensor("wqf", [1, 128, 4096], F32, kind="ExternalInput")
    keysd = nc.dram_tensor("keysT", [8, 64, 128], F32, kind="ExternalInput")
    wg_d = nc.dram_tensor("wg", [8, 128, 4096], F32, kind="ExternalInput")
    wu_d = nc.dram_tensor("wu", [8, 128, 4096], F32, kind="ExternalInput")
    wd_d = nc.dram_tensor("wd", [8, 128, 4096], F32, kind="ExternalInput")
    tab = nc.dram_tensor("tab", [NEXP, 2 * D], F32, kind="ExternalInput")
    y_out = nc.dram_tensor("y", [Tl, D], F32, kind="ExternalOutput")
    ydbg = (nc.dram_tensor("ydbg", [10, 128, NT * D], F32, kind="ExternalOutput")
            if dbg else None)

    ctx = []

    def sb(name, shape, dtype=F32):
        g = nc.sbuf_tensor(name, shape, dtype)
        t = g.__enter__()
        ctx.append(g)
        return t

    def psum(name):
        g = nc.psum_tensor(name, [128, 512], F32)
        t = g.__enter__()
        ctx.append(g)
        return t

    def sem(name):
        g = nc.semaphore(name)
        h = g.__enter__()
        ctx.append(g)
        return h

    x_sc = sb("x_sc", [128, NT, D])
    xln = sb("xln", [128, NT, D])
    xlnT = sb("xlnT", [128, NT, D])
    q_tm = sb("q_tm", [128, NT, D])
    k_tm = sb("k_tm", [128, NT, D])
    v65 = sb("v65", [128, NT, H * 65])
    mlpacc = q_tm

    fmh = sb("fmh", [128, 2, 2, 128])
    amt = sb("amt", [128, 2, 128])
    s2 = sb("s2", [128, H, 65])

    sxy = sb("sxy", [128, 8, 8])
    ixy = sb("ixy", [128, 8, 8])
    idxu = sb("idxu", [128, 8], U32)
    ascore = sb("ascore", [128, 4, 64])
    score8 = sb("score8", [128, 4, 8])
    pk_f = sb("pk_f", [128, 4, 8])
    isel = sb("isel", [128, 4, 8])
    jsel = sb("jsel", [128, 4, 8])
    idx_f = sb("idx_f", [128, 32])
    smw = sb("smw", [128, NT, 32])
    sm_s = sb("sm_s", [128, 8])
    wdot = sb("wdot", [128, NT, 32])
    wfin = sb("wfin", [128, NT, 32])
    idx_i16 = sb("idx_i16", [128, NT, 32], I16)
    idx_tmp = sb("idx_tmp", [128, NT, 256], I16)
    idx_g = sb("idx_g", [128, NT, 256], I16)
    gbuf = sb("gbuf", [128, GB, 2 * D])

    g_sc = sb("g_sc", [128, NT, 512])
    gu_sb = sb("gu_sb", [128, 512])
    guT_sc = sb("guT_sc", [128, NT, 4, 128])
    qf_all = bass.AP(fmh, 0, [[512, 128], [1, 512]])

    def qf_chunk(c):
        return bass.AP(fmh, c * 128, [[512, 128], [1, 128]])

    def sim_slot(p8):
        return bass.AP(guT_sc, p8 * 128, [[NT * 512, 128], [1, 128]])

    def sim_half(half):
        return bass.AP(guT_sc, half * 512, [[NT * 512, 128], [1, 512]])

    if NT == 4:
        qsc_t, ksc_t = g_sc, guT_sc
    else:
        qsc_t = sb("qsc", [128, H, 128])
        ksc_t = sb("ksc", [128, H, 128])
    QROW = (NT * 512) if NT == 4 else (H * 128)

    def qsc_h(h):
        return bass.AP(qsc_t, h * 128, [[QROW, 128], [1, 128]])

    def ksc_h(h):
        return bass.AP(ksc_t, h * 128, [[QROW, 128], [1, 128]])
    wbuf = sb("wbuf", [128, 2, 8, 512])

    ones = sb("ones", [128, 128])
    mask = sb("mask", [128, 128])
    ident = sb("ident", [128, 128])
    iota8 = sb("iota8", [128, 8])
    c8 = sb("c8", [128, 8])
    cos_col = sb("cos_col", [128, Tl // 128])
    sin_col = sb("sin_col", [128, Tl // 128])
    keys_sb = sb("keys_sb", [128, 8, 128])
    brep = sb("brep", [128, D])
    mv = sb("mv", [128, 2])
    rs_t = sb("rs_t", [128, 2])
    dscr = sb("dscr", [128, D])
    ang = sb("ang", [128, 2 * (Tl // 128)])

    pb = [psum(f"pb{i}") for i in range(8)]

    sems = dict(
        vec=sem("vec"), sca=sem("sca"), gps=sem("gps"), ten=sem("ten"),
        dw0=sem("dw0"), dw1=sem("dw1"), dx=sem("dx"), dy=sem("dy"),
        di=sem("di"), dk=sem("dk"), dg0=sem("dg0"), dg1=sem("dg1"),
    )
    P = Prog(nc)
    P.init_sems(sems)
    C = P.c
    ten, vec, sca, gps = C["ten"], C["vec"], C["sca"], C["gps"]
    bank = Banks(P, pb, ten)
    dg = [C["dg0"], C["dg1"]]
    E, W = P.emit, P.wait

    def dump(slot, tens):
        if not dbg:
            return
        W("sync", [(vec, vec.n), (ten, ten.n), (sca, sca.n), (gps, gps.n)])
        ap_s = bass.AP(tens, 0, [[NT * D, 128], [1, NT * D]])
        E("sync", lambda e, _s=slot, _a=ap_s: e.dma_start(
            out=ydbg[_s, :, :], in_=_a), incs=[(C["di"], 16)], chain=False)

    def _ph0(p):
        if phase_limit < p:
            P.disabled = True

    E("vector", lambda e: e.memset(ones[:], 1.0))
    v_ones = vec.n
    E("gpsimd", lambda e: e.affine_select(mask[:], ones[:], [[1, 128]], A.is_ge,
                                          0.0, base=0, channel_multiplier=-1),
      waits=[(vec, v_ones)])
    E("gpsimd", lambda e: e.affine_select(ident[:], ones[:], [[1, 128]],
                                          A.is_equal, 0.0, base=0,
                                          channel_multiplier=-1))
    E("gpsimd", lambda e: e.iota(iota8[:], [[1, 8]], base=0, channel_multiplier=0, allow_small_or_imprecise_dtypes=True))
    g_i8 = gps.n
    E("vector", lambda e: e.tensor_scalar_mul(c8[:], iota8[:], 8.0),
      waits=[(gps, g_i8)])
    ncc = Tl // 128
    E("gpsimd", lambda e: e.iota(ang[:, 0:ncc], [[128, ncc]], base=0,
                                 channel_multiplier=1, allow_small_or_imprecise_dtypes=True))
    E("gpsimd", lambda e: e.iota(ang[:, ncc:2 * ncc], [[-128, ncc]], base=Tl,
                                 channel_multiplier=-1, allow_small_or_imprecise_dtypes=True))
    g_ang = gps.n
    E("gpsimd", lambda e: e.load_library(_lc.mlp), chain=False)
    _ph0(0.2)
    E("scalar", lambda e: e.activation(sin_col[:], ang[:, 0:ncc], AF.Sin,
                                       bias=0.0, scale=PI / (2 * Tl)),
      waits=[(gps, g_ang)])
    E("scalar", lambda e: e.activation(cos_col[:], ang[:, ncc:2 * ncc], AF.Sin,
                                       bias=0.0, scale=PI / (2 * Tl)))
    s_masters = sca.n
    _ph0(0.3)
    for t_ in (fmh, s2, keys_sb):
        E("vector", lambda e, _t=t_: e.memset(_t[:], 0.0))
    E("vector", lambda e: e.memset(idx_tmp[:], 0))
    E("vector", lambda e: e.memset(idx_g[:], 0))
    E("vector", lambda e: e.memset(
        bass.AP(v65, 64, [[NT * H * 65, 128], [65, NT * H]]), 1.0))
    v_zero = vec.n
    _ph0(0.4)
    W("sync", [(vec, v_zero)])
    for j in range(8):
        E("sync", lambda e, _j=j: e.dma_start(
            out=keys_sb[(_j % 2) * 64:(_j % 2) * 64 + 64, _j, :],
            in_=keysd[_j, :, :]),
          incs=[(C["dk"], 16)], chain=False)
    dk_done = C["dk"].n
    _ph0(0.5)

    def set_bias(dram_b):
        W("sync", [(vec, vec.n)])
        E("sync", lambda e, _b=dram_b: e.dma_start(
            out=brep[0:1, :], in_=bass.AP(_b, 0, [[0, 1], [1, D]])),
          incs=[(C["dk"], 16)], chain=False)
        W("gpsimd", [(C["dk"], C["dk"].n)])
        E("gpsimd", lambda e: e.partition_broadcast(
            brep[:, :], brep[0:1, :], channels=128))
        return gps.n

    wsems = [C["dw0"], C["dw1"]]
    wring = {"pending": [], "handles": [], "nq": 0, "consumed": [(vec, 0), (vec, 0)],
             "cnt": [0, 0], "owed": 2}

    def _ring_emit_one():
        if not wring["pending"]:
            wring["owed"] += 1
            return
        (dram, si) = wring["pending"].pop(0)
        i = len(wring["handles"])
        s = i % 2
        war = wring["consumed"][s]
        srcap = bass.AP(dram, si * 128 * 4096, [[4096, 128], [1, 4096]])
        dstap = bass.AP(wbuf, s * 4096, [[2 * 4096, 128], [1, 4096]])
        W("sync", [war])
        E("sync", lambda e, _d=dstap, _s=srcap: e.dma_start(out=_d, in_=_s),
          incs=[(wsems[s], 16)], chain=False)
        wring["cnt"][s] += 1
        wring["handles"].append((s, (wsems[s], 16 * wring["cnt"][s])))

    def ring_queue(dram, si=0, ow=512, kparts=8):
        i = wring["nq"]
        wring["nq"] += 1
        wring["pending"].append((dram, si))
        while wring["owed"] > 0 and wring["pending"]:
            wring["owed"] -= 1
            _ring_emit_one()
        return i

    def ring_get(i):
        assert i < len(wring["handles"]), "slice not yet loaded (ring order bug)"
        return wring["handles"][i]

    def wdone(s):
        wring["consumed"][s] = (ten, ten.n)
        _ring_emit_one()

    def layernorm(src, dst, sqb, extra=()):
        vv0 = vec.n
        E("scalar", lambda e, _s=src(0, 1024), _q=sqb: e.activation(
            _q, _s, AF.Square), waits=[(vec, vv0)] + list(extra))
        ssq = sca.n
        E("vector", lambda e, _s=src(0, 1024): e.tensor_reduce(
            mv[:, 0:1], _s, AXX, A.add))
        E("vector", lambda e, _q=sqb: e.tensor_reduce(mv[:, 1:2], _q, AXX,
                                                      A.add), waits=[(sca, ssq)])
        E("vector", lambda e: e.tensor_scalar_mul(mv[:, 0:1], mv[:, 0:1],
                                                  1.0 / D))
        E("vector", lambda e: e.tensor_scalar_mul(mv[:, 1:2], mv[:, 1:2],
                                                  1.0 / D))
        E("vector", lambda e: e.tensor_tensor(rs_t[:, 0:1], mv[:, 0:1],
                                              mv[:, 0:1], A.mult))
        E("vector", lambda e: e.tensor_tensor(rs_t[:, 0:1], mv[:, 1:2],
                                              rs_t[:, 0:1], A.subtract))
        E("vector", lambda e: e.tensor_scalar(rs_t[:, 0:1], rs_t[:, 0:1], 1e-5,
                                              None, A.add))
        vv = vec.n
        E("scalar", lambda e: e.activation(rs_t[:, 1:2], rs_t[:, 0:1], AF.Sqrt),
          waits=[(vec, vv)])
        ssn = sca.n
        E("vector", lambda e: e.reciprocal(rs_t[:, 0:1], rs_t[:, 1:2]),
          waits=[(sca, ssn)])
        E("vector", lambda e, _s=src(0, 1024), _d=dst(0, 1024): e.tensor_scalar(
            _d, _s, mv[:, 0:1], rs_t[:, 0:1], A.subtract, A.mult))
        return vec.n

    def transpose8(src3, jj, dst3, extra_waits, b0=6, b1=7):
        for half, bi in ((0, b0), (1, b1)):
            bank.open(bi, extra_waits=extra_waits)
            for c in range(4):
                cc = half * 4 + c
                ap_o = pb[bi][:, c * 128:(c + 1) * 128]
                ap_i = src3[:, jj, cc * 128:(cc + 1) * 128]
                E("tensor", lambda e, _o=ap_o, _i=ap_i, _st=(c == 0), _sp=(c == 3):
                  e.matmul(_o, _i, ident[:], start=_st, stop=_sp,
                           is_transpose=True),
                  incs=([(ten, 1)] if c == 3 else []), chain=False)
            bank.close(bi)
            ap_d = dst3[:, jj, half * 512:(half + 1) * 512]
            E("vector", lambda e, _d=ap_d, _bi=bi: e.tensor_copy(
                _d, pb[_bi][:, 0:512]), waits=bank.rw(bi))
            bank.reader(bi, vec)
        return vec.n

    nx = 0
    ny = 0
    x_gate = (vec, 0)
    state_v = {h: 0 for h in range(H)}
    alias_free = [0]
    v_fm = [0, 0]

    gspecs = []
    gtails = []
    gstate = {"issued": 0}
    gtmpl = {"raw": None}

    def gather_emit(e, ap_o, ap_i):
        if gtmpl["raw"] is None:
            inst = e.dma_gather(out_ap=ap_o, in_ap=tab[:, :], idxs_ap=ap_i,
                                num_idxs=128, num_idxs_reg=128, elem_size=2 * D)
            gtmpl["raw"] = inst.ins
            return inst
        t = gtmpl["raw"]
        raw = mybir.InstDMAGatherAnt(
            name=nc.get_next_instruction_name(), engine=mybir.EngineType.Pool,
            ins=[t.ins[0], e.lower_ap(ap_i), t.ins[2]],
            outs=[e.lower_ap(ap_o)],
            transpose=False, num_idxs=128, elem_size=2 * D,
            stride_bytes_256=t.stride_bytes_256, gen_mode=0,
            single_packet=True, queue_num=0)
        return e.add_instruction(raw)

    def issue_gathers(upto):
        while gstate["issued"] < min(upto, len(gspecs)):
            gi = gstate["issued"]
            gstate["issued"] += 1
            (tj, gg, vsh) = gspecs[gi]
            slot = gi % GB
            ws = [(vec, vsh)]
            if gi - GB >= 0:
                ws.append(gtails[gi - GB])
            ap_o = bass.AP(gbuf, slot * 2 * D,
                           [[GB * 2 * D, 128], [2 * D, 1], [1, 2 * D]])
            ap_i = bass.AP(idx_g, tj * 256 + gg * 8, [[NT * 256, 128], [1, 8]])
            W("gpsimd", ws)
            E("gpsimd", lambda e, _o=ap_o, _i=ap_i: gather_emit(e, _o, _i),
              incs=[(dg[slot], 16)], chain=False)

    _ph = _ph0

    for sc in range(n_sc):
        t0 = sc * SC
        gt = t0 // 128
        _ph(1)
        W("sync", [x_gate])
        for j in range(NT):
            ap_d = x_sc[:, j, :]
            ap_s = x_in[t0 + j * 128:t0 + (j + 1) * 128, :]
            E("sync", lambda e, _d=ap_d, _s=ap_s: e.dma_start(out=_d, in_=_s),
              incs=[(C["dx"], 16)], chain=False)
            nx += 1
        dx_need = 16 * nx


        _ph(2)
        xT_v = [0] * NT
        for j in range(NT):
            W("vector", [(C["dx"], dx_need), (C["dk"], dk_done)])
            lv = layernorm(lambda lo, hi, _j=j: x_sc[:, _j, lo:hi],
                           lambda lo, hi, _j=j: xln[:, _j, lo:hi],
                           k_tm[:, 0, :], extra=[(C["dx"], dx_need)])
            xT_v[j] = transpose8(xln, j, xlnT, [(vec, lv)])

        if sc == 0:
            dump(0, xln)
            dump(1, xlnT)
        _ph(3)
        qkv = [(wq_d, 0, 0), (wq_d, 512, 0), (wk_d, 0, 1), (wk_d, 512, 1),
               (wv_d, 0, 2), (wv_d, 512, 2)]
        qkv_ids = [ring_queue(d, o // 512) for (d, o, _) in qkv] if do_proj else []
        prod = [0] * NT
        g_b = 0
        for pi_, (dram, of0, which) in enumerate(qkv if do_proj else []):
            if pi_ % 2 == 0:
                g_b = set_bias([bqd, bkd, bvd][which])
            wslot, wwait = ring_get(qkv_ids[pi_])
            oh = of0 // 512
            for j in range(NT):
                bi = j % 2
                bank.open(bi, extra_waits=[wwait, (vec, xT_v[j])])
                for k in range(8):
                    ap_l = xlnT[:, j, k * 128:(k + 1) * 128]
                    ap_r = wbuf[:, wslot, k, 0:512]
                    E("tensor", lambda e, _bi=bi, _l=ap_l, _r=ap_r,
                      _st=(k == 0), _sp=(k == 7): e.matmul(
                          pb[_bi][:, 0:512], _l, _r, start=_st, stop=_sp),
                      incs=([(ten, 1)] if k == 7 else []), chain=False)
                bank.close(bi)
                if which in (0, 1):
                    dst = q_tm if which == 0 else k_tm
                    ap_b = brep[:, oh * 512:(oh + 1) * 512]
                    E("vector", lambda e, _bi=bi, _b=ap_b: e.tensor_tensor(
                        dscr[:, 0:512], pb[_bi][:, 0:512], _b, A.add),
                      waits=bank.rw(bi) + [(gps, g_b)])
                    bank.reader(bi, vec)
                    ap_d = dst[:, j, oh * 512:(oh + 1) * 512]
                    E("vector", lambda e, _d=ap_d: e.tensor_scalar_max(
                        _d, dscr[:, 0:512], 0.0))
                else:
                    ap_d = bass.AP(v65, j * H * 65 + oh * 8 * 65,
                                   [[NT * H * 65, 128], [65, 8], [1, 64]])
                    ap_b = brep[:, oh * 512:(oh + 1) * 512]
                    E("vector", lambda e, _bi=bi, _d=ap_d, _b=ap_b:
                      e.tensor_tensor(_d, pb[_bi][:, 0:512], _b, A.add),
                      waits=bank.rw(bi) + [(gps, g_b)])
                    bank.reader(bi, vec)
                prod[j] = vec.n
            wdone(wslot)

        if sc == 0:
            dump(2, q_tm)
            dump(3, k_tm)
        _ph(4)
        ao_v = [0] * NT

        def attn_stage(j, h, scv):
            par = h % 2
            bank.open(5, extra_waits=[(vec, scv)])
            for si_, src in enumerate((qsc_h(h), ksc_h(h))):
                ap_o = pb[5][:, si_ * 128:(si_ + 1) * 128]
                E("tensor", lambda e, _o=ap_o, _i=src, _st=(si_ == 0),
                  _sp=(si_ == 1): e.matmul(_o, _i, ident[:], start=_st,
                                           stop=_sp, is_transpose=True),
                  incs=([(ten, 1)] if si_ == 1 else []), chain=False)
            bank.close(5)
            E("scalar", lambda e, _p=par: e.copy(
                bass.AP(fmh, _p * 256, [[512, 128], [1, 256]]),
                pb[5][:, 0:256]), waits=bank.rw(5))
            bank.reader(5, sca)
            v_fm[par] = sca.n

        def fm_q(par):
            return bass.AP(fmh, par * 256, [[512, 128], [1, 128]])

        def fm_k(par):
            return bass.AP(fmh, par * 256 + 128, [[512, 128], [1, 128]])

        for j in range(NT if do_attn else 0):
            ap_cc = cos_col[:, gt + j:gt + j + 1]
            ap_sc = sin_col[:, gt + j:gt + j + 1]
            for si_, (tin, tdst, col) in enumerate((
                    (q_tm, qsc_t, ap_cc), (q_tm, qsc_t, ap_sc),
                    (k_tm, ksc_t, ap_cc), (k_tm, ksc_t, ap_sc))):
                ap_i = bass.AP(tin, j * D, [[NT * D, 128], [64, H], [1, 64]])
                ap_d = bass.AP(tdst, (si_ % 2) * 64,
                               [[QROW, 128], [128, H], [1, 64]])
                E("vector", lambda e, _d=ap_d, _i=ap_i, _c=col:
                  e.tensor_scalar(_d, _i, _c, None, A.mult),
                  waits=([(ten, alias_free[0])] if si_ == 0 else ()))
            scv = vec.n
            attn_stage(j, 0, scv)
            for h in range(H):
                par = h % 2
                hp = h // 2
                ab = 1 + par
                bank.open(ab, extra_waits=[(sca, v_fm[par])])
                E("tensor", lambda e, _ab=ab, _p=par: e.matmul(
                    pb[_ab][:, 0:128], fm_k(_p), fm_q(_p),
                    start=True, stop=True), incs=[(ten, 1)], chain=False)
                bank.close(ab)
                E("vector", lambda e, _ab=ab, _p=par: e.tensor_tensor(
                    amt[:, _p, :], pb[_ab][:, 0:128], mask[:], A.mult),
                  waits=bank.rw(ab))
                bank.reader(ab, vec)
                v_amt = vec.n
                if h + 1 < H:
                    attn_stage(j, h + 1, scv)
                ob = 3
                if par == 0:
                    bank.open(ob, extra_waits=[(vec, v_amt), (vec, state_v[h])])
                else:
                    W("tensor", [(vec, v_amt), (vec, state_v[h])])
                tp = (0, 64 * par)
                po = slice(64 * par, 64 * par + 64)
                ap_v = bass.AP(v65, j * H * 65 + h * 65,
                               [[NT * H * 65, 128], [1, 64]])
                ap_s = bass.AP(s2, h * 65, [[H * 65, 128], [1, 64]])
                ap_z = bass.AP(s2, h * 65 + 64, [[H * 65, 128], [0, 64]])
                mm4o = [
                    (pb[ob][po, 0:128], ap_v, amt[:, par, :], True, False),
                    (pb[ob][po, 0:128], ap_s, fm_q(par), False, True),
                    (pb[ob][po, 128:256], ones[:, 0:64], amt[:, par, :], True, False),
                    (pb[ob][po, 128:256], ap_z, fm_q(par), False, True),
                ]
                for mi, (o_, l_, r_, st_, sp_) in enumerate(mm4o):
                    E("tensor", lambda e, _o=o_, _l=l_, _r=r_, _st=st_, _sp=sp_,
                      _tp=tp: e.matmul(_o, _l, _r, start=_st, stop=_sp,
                                       tile_position=_tp),
                      incs=([(ten, 1)] if mi == 3 else []), chain=False)
                if par == 1:
                    bank.close(ob)
                stb = 4
                if par == 0:
                    bank.open(stb)
                co = 256 * par
                ap_v65 = bass.AP(v65, j * H * 65 + h * 65,
                                 [[NT * H * 65, 128], [1, 65]])
                E("tensor", lambda e, _k=ksc_h(h), _v=ap_v65, _co=co: e.matmul(
                    pb[stb][:, _co:_co + 65], _k,
                    _v, start=True, stop=True),
                  incs=[(ten, 1)], chain=False)
                if par == 1:
                    bank.close(stb)

                if par == 1:
                    odw = bank.rw(ob)
                    E("vector", lambda e: e.tensor_scalar_max(
                        dscr[:, 0:128], pb[3][:, 128:256], 1e-6), waits=odw)
                    bank.reader(ob, vec)
                    E("vector", lambda e: e.reciprocal(dscr[:, 128:256],
                                                       dscr[:, 0:128]))
                    ap_ao = xln[:, j, hp * 128:(hp + 1) * 128]
                    E("vector", lambda e, _d=ap_ao: e.tensor_tensor(
                        _d, pb[3][:, 0:128], dscr[:, 128:256], A.mult))
                    bank.reader(ob, vec)
                    stw = bank.rw(stb)
                    for hh, cc in ((h - 1, 0), (h, 256)):
                        ap_t = s2[:, hh, :]
                        ap_p = pb[stb][:, cc:cc + 65]
                        E("vector", lambda e, _t=ap_t, _p=ap_p:
                          e.tensor_tensor(_t, _t, _p, A.add),
                          waits=(stw if cc == 0 else ()))
                        state_v[hh] = vec.n
                    bank.reader(stb, vec)
            ao_v[j] = vec.n
            alias_free[0] = ten.n

        if sc == 0:
            dump(5, xln)
        _ph(5)
        oh_ids = [ring_queue(wo_d, 0), ring_queue(wo_d, 1)] if do_wo else []
        if do_wo:
            g_b = set_bias(bod)
        for ohx in range(2 if do_wo else 0):
            wslot, wwait = ring_get(oh_ids[ohx])
            for j in range(NT):
                bi = j % 2
                bank.open(bi, extra_waits=[wwait, (vec, ao_v[j])])
                for k in range(8):
                    ap_l = xln[:, j, k * 128:(k + 1) * 128]
                    ap_r = wbuf[:, wslot, k, 0:512]
                    E("tensor", lambda e, _bi=bi, _l=ap_l, _r=ap_r,
                      _st=(k == 0), _sp=(k == 7): e.matmul(
                          pb[_bi][:, 0:512], _l, _r, start=_st, stop=_sp),
                      incs=([(ten, 1)] if k == 7 else []), chain=False)
                bank.close(bi)
                ap_d = xlnT[:, j, ohx * 512:(ohx + 1) * 512]
                ap_b = brep[:, ohx * 512:(ohx + 1) * 512]
                E("vector", lambda e, _bi=bi, _d=ap_d, _b=ap_b: e.tensor_tensor(
                    _d, pb[_bi][:, 0:512], _b, A.add),
                  waits=bank.rw(bi) + [(gps, g_b)])
                bank.reader(bi, vec)
            wdone(wslot)
        sq_wo = [0] * NT
        for j in range(NT):
            layernorm(lambda lo, hi, _j=j: xlnT[:, _j, lo:hi],
                      lambda lo, hi: dscr[:, lo:hi], k_tm[:, 0, :])
            sq_wo[j] = sca.n
            ap_x = x_sc[:, j, :]
            E("vector", lambda e, _x=ap_x: e.tensor_tensor(_x, _x, dscr[:, :],
                                                           A.add))

        if sc == 0:
            dump(6, xlnT)
            dump(7, x_sc)
        _ph(6)
        W("vector", [(ten, ten.n)])
        E("vector", lambda e: e.memset(mlpacc[:], 0.0))
        for j in range(NT):
            lv = layernorm(lambda lo, hi, _j=j: x_sc[:, _j, lo:hi],
                           lambda lo, hi, _j=j: xln[:, _j, lo:hi],
                           k_tm[:, 0, :])
            xT_v[j] = transpose8(xln, j, xlnT, [(vec, lv), (sca, sq_wo[j])])

        _ph(7)
        qf_id = ring_queue(wqf_d, 0)
        qslot, qwait = ring_get(qf_id)
        sc_gbase = len(gspecs)

        for j in range(NT if do_expert else 0):
            bank.open(0, extra_waits=[qwait, (vec, xT_v[j])])
            for oc in range(4):
                for k in range(8):
                    ap_l = wbuf[:, qslot, k, oc * 128:(oc + 1) * 128]
                    ap_r = xlnT[:, j, k * 128:(k + 1) * 128]
                    ap_o = pb[0][:, oc * 128:(oc + 1) * 128]
                    E("tensor", lambda e, _o=ap_o, _l=ap_l, _r=ap_r,
                      _st=(k == 0), _sp=(k == 7): e.matmul(_o, _l, _r,
                                                           start=_st, stop=_sp),
                      incs=([(ten, 1)] if (k == 7 and oc == 3) else []),
                      chain=False)
            bank.close(0)
            E("scalar", lambda e: e.copy(qf_all, pb[0][:, 0:512]),
              waits=bank.rw(0))
            bank.reader(0, sca)
            s_qf = sca.n
            for half, bi in ((0, 2), (1, 6)):
                bank.open(bi, extra_waits=[(sca, s_qf), (C["dk"], dk_done)])
                for q2 in range(4):
                    p8 = half * 4 + q2
                    ap_l = qf_chunk(p8 // 2)
                    ap_r = keys_sb[:, p8, :]
                    ap_o = pb[bi][:, q2 * 128:(q2 + 1) * 128]
                    E("tensor", lambda e, _o=ap_o, _l=ap_l, _r=ap_r: e.matmul(
                        _o, _l, _r, start=True, stop=True),
                      incs=([(ten, 1)] if q2 == 3 else []), chain=False)
                bank.close(bi)
                ap_d = sim_half(half)
                E("scalar", lambda e, _d=ap_d, _bi=bi: e.copy(_d, pb[_bi][:, 0:512]),
                  waits=bank.rw(bi))
                bank.reader(bi, sca)
            s_sim = sca.n
            W("vector", [(sca, s_sim)])
            for p8 in range(8):
                E("vector", lambda e, _p=p8: e.max(sxy[:, _p, :], sim_slot(_p)))
                E("vector", lambda e, _p=p8: e.max_index(idxu[:, :], sxy[:, _p, :],
                                                         sim_slot(_p)))
                E("vector", lambda e, _p=p8: e.tensor_copy(ixy[:, _p, :],
                                                           idxu[:, :]))
            E("vector", lambda e: e.tensor_tensor(
                bass.AP(ascore, 0, [[256, 128], [64, 4], [8, 8], [1, 8]]),
                bass.AP(sxy, 0, [[64, 128], [8, 4], [1, 8], [0, 8]]),
                bass.AP(sxy, 32, [[64, 128], [8, 4], [0, 8], [1, 8]]),
                A.add))
            for hh in range(4):
                E("vector", lambda e, _h=hh: e.max(score8[:, _h, :],
                                                   ascore[:, _h, :]))
                E("vector", lambda e, _h=hh: e.max_index(idxu[:, :],
                                                         score8[:, _h, :],
                                                         ascore[:, _h, :]))
                E("vector", lambda e, _h=hh: e.tensor_copy(pk_f[:, _h, :],
                                                           idxu[:, :]))
            E("vector", lambda e: e.tensor_tensor(
                bass.AP(ascore, 0, [[256, 128], [56, 4], [7, 8], [1, 7]]),
                bass.AP(pk_f, 0, [[32, 128], [8, 4], [1, 8], [0, 7]]),
                bass.AP(c8, 1, [[8, 128], [0, 4], [0, 8], [1, 7]]),
                A.is_ge))
            E("vector", lambda e: e.tensor_reduce(
                isel[:, :, :],
                bass.AP(ascore, 0, [[256, 128], [56, 4], [7, 8], [1, 7]]),
                AXX, A.add))
            E("vector", lambda e: e.scalar_tensor_tensor(
                jsel[:], isel[:], -8.0, pk_f[:], A.mult, A.add))
            for selt, off in ((isel, 0), (jsel, 32)):
                E("vector", lambda e, _s=selt: e.tensor_tensor(
                    bass.AP(ascore, 0, [[256, 128], [64, 4], [8, 8], [1, 8]]),
                    bass.AP(_s, 0, [[32, 128], [8, 4], [1, 8], [0, 8]]),
                    bass.AP(iota8, 0, [[8, 128], [0, 4], [0, 8], [1, 8]]),
                    A.is_equal))
                E("vector", lambda e, _o=off: e.tensor_tensor(
                    bass.AP(ascore, 0, [[256, 128], [64, 4], [8, 8], [1, 8]]),
                    bass.AP(ascore, 0, [[256, 128], [64, 4], [8, 8], [1, 8]]),
                    bass.AP(ixy, _o, [[64, 128], [8, 4], [0, 8], [1, 8]]),
                    A.mult))
                E("vector", lambda e, _d=selt: e.tensor_reduce(
                    _d[:, :, :],
                    bass.AP(ascore, 0, [[256, 128], [64, 4], [8, 8], [1, 8]]),
                    AXX, A.add))
            E("vector", lambda e: e.scalar_tensor_tensor(
                idx_f[:], bass.AP(isel, 0, [[32, 128], [1, 32]]), 128.0,
                bass.AP(jsel, 0, [[32, 128], [1, 32]]), A.mult, A.add))
            ap_smw3 = bass.AP(smw, j * 32, [[NT * 32, 128], [8, 4], [1, 8]])
            E("vector", lambda e, _d=ap_smw3: e.tensor_tensor(
                _d, bass.AP(score8, 0, [[32, 128], [8, 4], [1, 8]]),
                bass.AP(score8, 0, [[32, 128], [8, 4], [0, 8]]), A.subtract))
            vv1 = vec.n
            E("scalar", lambda e, _d=ap_smw3: e.activation(_d, _d, AF.Exp),
              waits=[(vec, vv1)])
            ss1 = sca.n
            E("vector", lambda e, _s=ap_smw3: e.tensor_reduce(
                sm_s[:, 0:4], _s, AXX, A.add), waits=[(sca, ss1)])
            E("vector", lambda e: e.reciprocal(sm_s[:, 4:8], sm_s[:, 0:4]))
            E("vector", lambda e, _d=ap_smw3: e.tensor_tensor(
                _d, _d, bass.AP(sm_s, 4, [[8, 128], [1, 4], [0, 8]]), A.mult))
            ap_ii = idx_i16[:, j, :]
            E("vector", lambda e, _d=ap_ii: e.tensor_copy(_d, idx_f[:]))
            v_i16 = vec.n
            W("sync", [(vec, v_i16)])
            for s8 in range(8):
                for dup in range(2):
                    ap_d = bass.AP(idx_tmp,
                                   (dup * 16) * (NT * 256) + j * 256 + s8 * 32,
                                   [[NT * 256, 16], [1, 32]])
                    ap_s = bass.AP(idx_i16, (16 * s8) * (NT * 32) + j * 32,
                                   [[NT * 32, 16], [1, 32]])
                    E("sync", lambda e, _d=ap_d, _s=ap_s: e.dma_start(
                        out=_d, in_=_s), incs=[(C["di"], 16)], chain=False)
            ap_gd = bass.AP(idx_g, j * 256, [[NT * 256, 32], [8, 32], [1, 8]])
            ap_gs = bass.AP(idx_tmp, j * 256, [[NT * 256, 32], [1, 32], [32, 8]])
            E("vector", lambda e, _d=ap_gd, _s=ap_gs: e.tensor_copy(_d, _s),
              waits=[(C["di"], C["di"].n)])
            v_shuf = vec.n
            for g in range(NGRP):
                gspecs.append((j, g, v_shuf))
                gtails.append(None)
            if j == 0 and sc == 0:
                issue_gathers(GB)
        wdone(qslot)

        def consume_unit(u):
            tj = u // 16
            u2 = u % 16
            base = sc_gbase + tj * NGRP
            for gg in (2 * u2, 2 * u2 + 1):
                gi = base + gg
                slot = gi % GB
                dgw = (dg[slot], 16 * (gi // GB + 1))
                ap_g = bass.AP(gbuf, slot * 2 * D, [[GB * 2 * D, 128], [1, D]])
                ap_x = xln[:, tj, :]
                E("vector", lambda e, _g=ap_g, _x=ap_x: e.tensor_tensor(
                    dscr[:, :], _g, _x, A.mult), waits=[dgw])
                ap_w = wdot[:, tj, gg:gg + 1]
                E("vector", lambda e, _w=ap_w: e.tensor_reduce(
                    _w, dscr[:, :], AXX, A.add))
            vv2 = vec.n
            ap_wd = wdot[:, tj, 2 * u2:2 * u2 + 2]
            ap_wf = wfin[:, tj, 2 * u2:2 * u2 + 2]
            ap_sm = smw[:, tj, 2 * u2:2 * u2 + 2]
            E("scalar", lambda e, _i=ap_wd, _o=ap_wf: e.activation(
                _o, _i, AF.Silu), waits=[(vec, vv2)])
            ss2 = sca.n
            E("vector", lambda e, _f=ap_wf, _s=ap_sm: e.tensor_tensor(
                _f, _f, _s, A.mult), waits=[(sca, ss2)])
            for gg in (2 * u2, 2 * u2 + 1):
                gi = base + gg
                slot = gi % GB
                ap_u = bass.AP(gbuf, slot * 2 * D + D, [[GB * 2 * D, 128], [1, D]])
                ap_m = mlpacc[:, tj, :]
                ap_w1 = wfin[:, tj, gg:gg + 1]
                E("vector", lambda e, _u=ap_u, _m=ap_m, _w=ap_w1:
                  e.scalar_tensor_tensor(_m, _u, _w, _m, A.mult, A.add))
                gtails[gi] = (vec, vec.n)

        cu = {"n": 0}

        def drain(kunits):
            if not do_expert:
                return
            for _ in range(kunits):
                u = cu["n"]
                if u >= NT * 16:
                    return
                cu["n"] += 1
                gu_base = sc_gbase + 2 * u
                issue_gathers(gu_base + 2)
                consume_unit(u)
                issue_gathers(gu_base + 4)

        if sc == 0:
            dump(8, xln)
        _ph(8)
        mlp_ids = []
        for ic in range(8 if do_mlp else 0):
            mlp_ids.append(ring_queue(wg_d, ic))
            mlp_ids.append(ring_queue(wu_d, ic))
            mlp_ids.append(ring_queue(wd_d, ic))
        for ic in range(8 if do_mlp else 0):
            gslot, gwait = ring_get(mlp_ids[3 * ic])
            sgl = [0] * NT
            for j in range(NT):
                bank.open(4, extra_waits=[gwait, (vec, xT_v[j])])
                for k in range(8):
                    ap_l = xlnT[:, j, k * 128:(k + 1) * 128]
                    ap_r = wbuf[:, gslot, k, 0:512]
                    E("tensor", lambda e, _l=ap_l, _r=ap_r, _st=(k == 0),
                      _sp=(k == 7): e.matmul(pb[4][:, 0:512], _l, _r,
                                             start=_st, stop=_sp),
                      incs=([(ten, 1)] if k == 7 else []), chain=False)
                bank.close(4)
                ap_gs2 = g_sc[:, j, :]
                E("scalar", lambda e, _d=ap_gs2: e.activation(
                    _d, pb[4][:, 0:512], AF.Silu), waits=bank.rw(4))
                bank.reader(4, sca)
                sgl[j] = sca.n
                drain(1)
            wdone(gslot)
            uslot, uwait = ring_get(mlp_ids[3 * ic + 1])
            gTl = [0] * NT
            for j in range(NT):
                bank.open(5, extra_waits=[uwait, (vec, xT_v[j])])
                for k in range(8):
                    ap_l = xlnT[:, j, k * 128:(k + 1) * 128]
                    ap_r = wbuf[:, uslot, k, 0:512]
                    E("tensor", lambda e, _l=ap_l, _r=ap_r, _st=(k == 0),
                      _sp=(k == 7): e.matmul(pb[5][:, 0:512], _l, _r,
                                             start=_st, stop=_sp),
                      incs=([(ten, 1)] if k == 7 else []), chain=False)
                bank.close(5)
                ap_gs2 = g_sc[:, j, :]
                E("vector", lambda e, _g=ap_gs2: e.tensor_tensor(
                    gu_sb[:, :], _g, pb[5][:, 0:512], A.mult),
                  waits=bank.rw(5) + [(sca, sgl[j])])
                bank.reader(5, vec)
                v_gu = vec.n
                bank.open(6, extra_waits=[(vec, v_gu)])
                for c in range(4):
                    ap_i = gu_sb[:, c * 128:(c + 1) * 128]
                    ap_o = pb[6][:, c * 128:(c + 1) * 128]
                    E("tensor", lambda e, _o=ap_o, _i=ap_i, _st=(c == 0),
                      _sp=(c == 3): e.matmul(_o, _i, ident[:], start=_st,
                                             stop=_sp, is_transpose=True),
                      incs=([(ten, 1)] if c == 3 else []), chain=False)
                bank.close(6)
                ap_gT = bass.AP(guT_sc, j * 512, [[NT * 512, 128], [1, 512]])
                E("scalar", lambda e, _d=ap_gT: e.activation(
                    _d, pb[6][:, 0:512], AF.Copy), waits=bank.rw(6))
                bank.reader(6, sca)
                gTl[j] = sca.n
            wdone(uslot)
            dslot, dwait = ring_get(mlp_ids[3 * ic + 2])
            for j in range(NT):
                for oc2 in range(2):
                    bi = (j % 2) * 2 + oc2
                    bank.open(bi, extra_waits=[dwait, (sca, gTl[j])])
                    for k in range(4):
                        ap_l = bass.AP(guT_sc, j * 512 + k * 128,
                                       [[NT * 512, 128], [1, 128]])
                        ap_r = wbuf[:, dslot, k * 2 + oc2, 0:512]
                        E("tensor", lambda e, _bi=bi, _l=ap_l, _r=ap_r,
                          _st=(k == 0), _sp=(k == 3): e.matmul(
                              pb[_bi][:, 0:512], _l, _r, start=_st, stop=_sp),
                          incs=([(ten, 1)] if k == 3 else []), chain=False)
                    bank.close(bi)
                    ap_m = mlpacc[:, j, oc2 * 512:(oc2 + 1) * 512]
                    E("vector", lambda e, _bi=bi, _m=ap_m: e.scalar_tensor_tensor(
                        _m, pb[_bi][:, 0:512], 1.0, _m, A.mult, A.add),
                      waits=bank.rw(bi))
                    bank.reader(bi, vec)
                drain(1)
            wdone(dslot)
        drain(NT * 16)
        alias_free[0] = ten.n

        if sc == 0:
            dump(9, mlpacc)
        _ph(9)
        for j in range(NT):
            layernorm(lambda lo, hi, _j=j: mlpacc[:, _j, lo:hi],
                      lambda lo, hi: dscr[:, lo:hi], k_tm[:, 0, :])
            ap_x = x_sc[:, j, :]
            E("vector", lambda e, _x=ap_x: e.tensor_tensor(_x, _x, dscr[:, :],
                                                           A.add))
            v_res = vec.n
            ap_y = y_out[t0 + j * 128:t0 + (j + 1) * 128, :]
            W("sync", [(vec, v_res)])
            E("sync", lambda e, _y=ap_y, _x=ap_x: e.dma_start(out=_y, in_=_x),
              incs=[(C["dy"], 16)], chain=False)
            ny += 1
        x_gate = (C["dy"], 16 * ny)

    W("sync", [(C["dy"], 16 * ny)])

    with nc.Block() as block:
        @block.tensor
        def _(e):
            for fn in P.q["tensor"]:
                fn(e)

        @block.vector
        def _(e):
            for fn in P.q["vector"]:
                fn(e)

        @block.scalar
        def _(e):
            for fn in P.q["scalar"]:
                fn(e)

        @block.gpsimd
        def _(e):
            for fn in P.q["gpsimd"]:
                fn(e)

        @block.sync
        def _(e):
            for fn in P.q["sync"]:
                fn(e)

    for g in reversed(ctx):
        g.__exit__(None, None, None)
    return nc


def _pack_cols(M):
    R, C = M.shape
    kp = R // 128
    ns = C // 512
    return np.ascontiguousarray(
        M.reshape(kp, 128, ns, 512).transpose(2, 1, 0, 3).reshape(ns, 128, kp * 512))


def _pack_rows(M):
    R, C = M.shape
    ns = R // 512
    return np.ascontiguousarray(
        M.reshape(ns, 4, 128, C).transpose(0, 2, 1, 3).reshape(ns, 128, 4 * C))


def host_pack(x, Wq, bq, Wk, bk, Wv, bv, Wo, bo, fWq, keys, down, up, Wg, Wu, Wd):
    f32 = np.float32

    def c_(a):
        return np.ascontiguousarray(np.asarray(a), dtype=f32)

    keys_np = np.asarray(keys)
    keys_h = np.empty((8, 64, 128), f32)
    for p in range(2):
        for h in range(4):
            keys_h[p * 4 + h] = np.transpose(keys_np[h, :, p, :])
    return dict(
        x=c_(x),
        wq=_pack_cols(c_(Wq)), wk=_pack_cols(c_(Wk)),
        wv=_pack_cols(c_(Wv)), wo=_pack_cols(c_(Wo)),
        bq=c_(bq), bk=c_(bk), bv=c_(bv), bo=c_(bo),
        wqf=_pack_cols(c_(np.transpose(np.asarray(fWq)))),
        keysT=c_(keys_h),
        wg=_pack_cols(c_(np.transpose(np.asarray(Wg)))),
        wu=_pack_cols(c_(np.transpose(np.asarray(Wu)))),
        wd=_pack_rows(c_(np.transpose(np.asarray(Wd)))),
        tab=c_(np.concatenate([np.asarray(down), np.asarray(up)], axis=1)))


def kernel(L0_attn_x, L0_attn_Wq, L0_attn_bq, L0_attn_Wk, L0_attn_bk,
           L0_attn_Wv, L0_attn_bv, L0_attn_Wo, L0_attn_bo, L0_ffn_Wq,
           L0_ffn_keys, L0_ffn_down_embed, L0_ffn_up_embed, L0_ffn_Wg,
           L0_ffn_Wu, L0_ffn_Wd):
    B = int(np.asarray(L0_attn_x).shape[0])
    xs = np.asarray(L0_attn_x)
    base = host_pack(xs[0], L0_attn_Wq, L0_attn_bq, L0_attn_Wk, L0_attn_bk,
                     L0_attn_Wv, L0_attn_bv, L0_attn_Wo, L0_attn_bo, L0_ffn_Wq,
                     L0_ffn_keys, L0_ffn_down_embed, L0_ffn_up_embed, L0_ffn_Wg,
                     L0_ffn_Wu, L0_ffn_Wd)
    nc = build_core_program()
    lower_extended_insts(nc)
    in_maps = []
    for c in range(B):
        m = dict(base)
        m["x"] = np.ascontiguousarray(xs[c], dtype=np.float32)
        in_maps.append(m)
    res = run_bass_kernel_spmd(nc, in_maps, core_ids=list(range(B)))
    out = np.stack([np.asarray(res.results[c]["y"]) for c in range(B)], axis=0)
    return out.astype(np.float32)


# version 09 (final)
# speedup vs baseline: 1.4181x; 1.1508x over previous
import numpy as np
import concourse.bass as bass
import concourse.mybir as mybir
import concourse.library_config as _lc
from concourse.library_overlay import lower_extended_insts
from concourse.bass_utils import run_bass_kernel_spmd

F32 = mybir.dt.float32
I16 = mybir.dt.int16
U32 = mybir.dt.uint32
AF = mybir.ActivationFunctionType
A = mybir.AluOpType
AXX = mybir.AxisListType.X

D = 1024
H = 16
DH = 64
INTER = 4096
NEXP = 16384
PI = float(np.pi)


class Ctr:
    def __init__(self, h):
        self.h = h
        self.n = 0


class Prog:
    ENGS = ("tensor", "vector", "scalar", "gpsimd", "sync")

    def __init__(self, nc):
        self.nc = nc
        self.q = {e: [] for e in self.ENGS}
        self.chain = {}
        self.disabled = False

    def init_sems(self, sems):
        self.c = {k: Ctr(v) for k, v in sems.items()}
        self.chain = {"vector": self.c["vec"], "scalar": self.c["sca"],
                      "gpsimd": self.c["gps"]}

    @staticmethod
    def _coalesce(w):
        m = {}
        out = []
        for (h, v) in w:
            k = id(h)
            if k in m:
                i = m[k]
                if v > out[i][1]:
                    out[i] = (h, v)
            else:
                m[k] = len(out)
                out.append((h, v))
        return out

    def emit(self, eng, fn, waits=(), incs=(), chain=True):
        if self.disabled:
            return None
        chain_ctr = self.chain.get(eng) if chain else None
        w = self._coalesce([(c.h, v) for (c, v) in waits if v > 0])
        amts = []
        if chain_ctr is not None:
            w.append((chain_ctr.h, chain_ctr.n))
            chain_ctr.n += 1
            amts.append((chain_ctr.h, 1))
        for (c, amt) in incs:
            c.n += amt
            amts.append((c.h, amt))

        def run(e, _fn=fn, _w=w, _amts=amts):
            for (h, v) in _w[1:]:
                e.wait_ge(h, v)
            inst = _fn(e)
            if _w:
                inst._wait_ge(_w[0][0], _w[0][1])
            for (h, amt) in _amts:
                inst.then_inc(h, amt)
            return inst

        self.q[eng].append(run)
        return chain_ctr.n if chain_ctr is not None else None

    def wait(self, eng, waits):
        if self.disabled:
            return
        w = self._coalesce([(c.h, v) for (c, v) in waits if v > 0])
        if not w:
            return

        def run(e, _w=w):
            for (h, v) in _w:
                e.wait_ge(h, v)

        self.q[eng].append(run)


class Banks:
    def __init__(self, prog, psum_tensors, ten_ctr):
        self.p = prog
        self.b = psum_tensors
        self.ten = ten_ctr
        self.writer = [0] * len(self.b)
        self.readers = [[] for _ in self.b]

    def open(self, i, extra_waits=()):
        ws = list(self.readers[i]) + list(extra_waits)
        self.p.wait("tensor", ws)
        self.readers[i] = []
        return self.b[i]

    def close(self, i):
        self.writer[i] = self.ten.n

    def rw(self, i):
        return [(self.ten, self.writer[i])]

    def reader(self, i, ctr):
        self.readers[i].append((ctr, ctr.n))


def build_core_program(Tl=4096, SC=512, race=False, do_attn=True, do_expert=True, do_mlp=True, do_proj=True, do_wo=True, phase_limit=99, dbg=False):
    NT = SC // 128
    n_sc = Tl // SC
    GB = 3
    NGRP = 32

    nc = bass.Bass("TRN2", target_bir_lowering=False, detect_race_conditions=race,
                   dynamic_dma_scratch_size=16384)

    x_in = nc.dram_tensor("x", [Tl, D], F32, kind="ExternalInput")
    wq_d = nc.dram_tensor("wq", [2, 128, 4096], F32, kind="ExternalInput")
    wk_d = nc.dram_tensor("wk", [2, 128, 4096], F32, kind="ExternalInput")
    wv_d = nc.dram_tensor("wv", [2, 128, 4096], F32, kind="ExternalInput")
    wo_d = nc.dram_tensor("wo", [2, 128, 4096], F32, kind="ExternalInput")
    bqd = nc.dram_tensor("bq", [D], F32, kind="ExternalInput")
    bkd = nc.dram_tensor("bk", [D], F32, kind="ExternalInput")
    bvd = nc.dram_tensor("bv", [D], F32, kind="ExternalInput")
    bod = nc.dram_tensor("bo", [D], F32, kind="ExternalInput")
    wqf_d = nc.dram_tensor("wqf", [1, 128, 4096], F32, kind="ExternalInput")
    keysd = nc.dram_tensor("keysT", [8, 64, 128], F32, kind="ExternalInput")
    wg_d = nc.dram_tensor("wg", [8, 128, 4096], F32, kind="ExternalInput")
    wu_d = nc.dram_tensor("wu", [8, 128, 4096], F32, kind="ExternalInput")
    wd_d = nc.dram_tensor("wd", [8, 128, 4096], F32, kind="ExternalInput")
    tab = nc.dram_tensor("tab", [NEXP, 2 * D], F32, kind="ExternalInput")
    y_out = nc.dram_tensor("y", [Tl, D], F32, kind="ExternalOutput")
    ydbg = (nc.dram_tensor("ydbg", [10, 128, NT * D], F32, kind="ExternalOutput")
            if dbg else None)

    ctx = []

    def sb(name, shape, dtype=F32):
        g = nc.sbuf_tensor(name, shape, dtype)
        t = g.__enter__()
        ctx.append(g)
        return t

    def psum(name):
        g = nc.psum_tensor(name, [128, 512], F32)
        t = g.__enter__()
        ctx.append(g)
        return t

    def sem(name):
        g = nc.semaphore(name)
        h = g.__enter__()
        ctx.append(g)
        return h

    x_sc = sb("x_sc", [128, NT, D])
    xln = sb("xln", [128, NT, D])
    xlnT = sb("xlnT", [128, NT, D])
    q_tm = sb("q_tm", [128, NT, D])
    k_tm = sb("k_tm", [128, NT, D])
    v65 = sb("v65", [128, NT, H * 65])
    mlpacc = q_tm

    fmh = sb("fmh", [128, 2, 2, 128])
    amt = sb("amt", [128, 2, 128])
    s2 = sb("s2", [128, H, 65])

    sxy = sb("sxy", [128, 8, 8])
    ixy = sb("ixy", [128, 8, 8])
    idxu = sb("idxu", [128, 8], U32)
    ascore = sb("ascore", [128, 4, 64])
    score8 = sb("score8", [128, 4, 8])
    pk_f = sb("pk_f", [128, 4, 8])
    isel = sb("isel", [128, 4, 8])
    jsel = sb("jsel", [128, 4, 8])
    idx_f = sb("idx_f", [128, 32])
    smw = sb("smw", [128, NT, 32])
    sm_s = sb("sm_s", [128, 8])
    wdot = sb("wdot", [128, NT, 32])
    wfin = sb("wfin", [128, NT, 32])
    idx_i16 = sb("idx_i16", [128, NT, 32], I16)
    idx_tmp = sb("idx_tmp", [128, NT, 256], I16)
    idx_g = sb("idx_g", [128, NT, 256], I16)
    gbuf = sb("gbuf", [128, GB, 2 * D])

    g_sc = sb("g_sc", [128, NT, 512])
    gu_sb = sb("gu_sb", [128, 512])
    guT_sc = sb("guT_sc", [128, NT, 4, 128])
    qf_all = bass.AP(fmh, 0, [[512, 128], [1, 512]])

    def qf_chunk(c):
        return bass.AP(fmh, c * 128, [[512, 128], [1, 128]])

    def sim_slot(p8):
        return bass.AP(guT_sc, p8 * 128, [[NT * 512, 128], [1, 128]])

    def sim_half(half):
        return bass.AP(guT_sc, half * 512, [[NT * 512, 128], [1, 512]])

    if NT == 4:
        qsc_t, ksc_t = g_sc, guT_sc
    else:
        qsc_t = sb("qsc", [128, H, 128])
        ksc_t = sb("ksc", [128, H, 128])
    QROW = (NT * 512) if NT == 4 else (H * 128)

    def qsc_h(h):
        return bass.AP(qsc_t, h * 128, [[QROW, 128], [1, 128]])

    def ksc_h(h):
        return bass.AP(ksc_t, h * 128, [[QROW, 128], [1, 128]])
    wbuf = sb("wbuf", [128, 2, 8, 512])

    ones = sb("ones", [128, 128])
    mask = sb("mask", [128, 128])
    ident = sb("ident", [128, 128])
    iota8 = sb("iota8", [128, 8])
    c8 = sb("c8", [128, 8])
    cos_col = sb("cos_col", [128, Tl // 128])
    sin_col = sb("sin_col", [128, Tl // 128])
    keys_sb = sb("keys_sb", [128, 8, 128])
    brep = sb("brep", [128, D])
    mv = sb("mv", [128, 2])
    rs_t = sb("rs_t", [128, 2])
    dscr = sb("dscr", [128, D])
    ang = sb("ang", [128, 2 * (Tl // 128)])

    pb = [psum(f"pb{i}") for i in range(8)]

    sems = dict(
        vec=sem("vec"), sca=sem("sca"), gps=sem("gps"), ten=sem("ten"),
        dw0=sem("dw0"), dw1=sem("dw1"), dx=sem("dx"), dy=sem("dy"),
        di=sem("di"), dk=sem("dk"), dg0=sem("dg0"), dg1=sem("dg1"),
        dg2=sem("dg2"),
    )
    P = Prog(nc)
    P.init_sems(sems)
    C = P.c
    ten, vec, sca, gps = C["ten"], C["vec"], C["sca"], C["gps"]
    bank = Banks(P, pb, ten)
    dg = [C["dg0"], C["dg1"], C["dg2"]]
    E, W = P.emit, P.wait

    def dump(slot, tens):
        if not dbg:
            return
        W("sync", [(vec, vec.n), (ten, ten.n), (sca, sca.n), (gps, gps.n)])
        ap_s = bass.AP(tens, 0, [[NT * D, 128], [1, NT * D]])
        E("sync", lambda e, _s=slot, _a=ap_s: e.dma_start(
            out=ydbg[_s, :, :], in_=_a), incs=[(C["di"], 16)], chain=False)

    def _ph0(p):
        if phase_limit < p:
            P.disabled = True

    E("vector", lambda e: e.memset(ones[:], 1.0))
    v_ones = vec.n
    E("gpsimd", lambda e: e.affine_select(mask[:], ones[:], [[1, 128]], A.is_ge,
                                          0.0, base=0, channel_multiplier=-1),
      waits=[(vec, v_ones)])
    E("gpsimd", lambda e: e.affine_select(ident[:], ones[:], [[1, 128]],
                                          A.is_equal, 0.0, base=0,
                                          channel_multiplier=-1))
    E("gpsimd", lambda e: e.iota(iota8[:], [[1, 8]], base=0, channel_multiplier=0, allow_small_or_imprecise_dtypes=True))
    g_i8 = gps.n
    E("vector", lambda e: e.tensor_scalar_mul(c8[:], iota8[:], 8.0),
      waits=[(gps, g_i8)])
    ncc = Tl // 128
    E("gpsimd", lambda e: e.iota(ang[:, 0:ncc], [[128, ncc]], base=0,
                                 channel_multiplier=1, allow_small_or_imprecise_dtypes=True))
    E("gpsimd", lambda e: e.iota(ang[:, ncc:2 * ncc], [[-128, ncc]], base=Tl,
                                 channel_multiplier=-1, allow_small_or_imprecise_dtypes=True))
    g_ang = gps.n
    E("gpsimd", lambda e: e.load_library(_lc.mlp), chain=False)
    _ph0(0.2)
    E("scalar", lambda e: e.activation(sin_col[:], ang[:, 0:ncc], AF.Sin,
                                       bias=0.0, scale=PI / (2 * Tl)),
      waits=[(gps, g_ang)])
    E("scalar", lambda e: e.activation(cos_col[:], ang[:, ncc:2 * ncc], AF.Sin,
                                       bias=0.0, scale=PI / (2 * Tl)))
    s_masters = sca.n
    _ph0(0.3)
    for t_ in (fmh, s2, keys_sb):
        E("vector", lambda e, _t=t_: e.memset(_t[:], 0.0))
    E("vector", lambda e: e.memset(idx_tmp[:], 0))
    E("vector", lambda e: e.memset(idx_g[:], 0))
    E("vector", lambda e: e.memset(
        bass.AP(v65, 64, [[NT * H * 65, 128], [65, NT * H]]), 1.0))
    v_zero = vec.n
    _ph0(0.4)
    W("sync", [(vec, v_zero)])
    for j in range(8):
        E("sync", lambda e, _j=j: e.dma_start(
            out=keys_sb[(_j % 2) * 64:(_j % 2) * 64 + 64, _j, :],
            in_=keysd[_j, :, :]),
          incs=[(C["dk"], 16)], chain=False)
    dk_done = C["dk"].n
    _ph0(0.5)

    def set_bias(dram_b):
        W("sync", [(vec, vec.n)])
        E("sync", lambda e, _b=dram_b: e.dma_start(
            out=brep[0:1, :], in_=bass.AP(_b, 0, [[0, 1], [1, D]])),
          incs=[(C["dk"], 16)], chain=False)
        W("gpsimd", [(C["dk"], C["dk"].n)])
        E("gpsimd", lambda e: e.partition_broadcast(
            brep[:, :], brep[0:1, :], channels=128))
        return gps.n

    wsems = [C["dw0"], C["dw1"]]
    wring = {"pending": [], "handles": [], "nq": 0, "consumed": [(vec, 0), (vec, 0)],
             "cnt": [0, 0], "owed": 2}

    def _ring_emit_one():
        if not wring["pending"]:
            wring["owed"] += 1
            return
        (dram, si) = wring["pending"].pop(0)
        i = len(wring["handles"])
        s = i % 2
        war = wring["consumed"][s]
        srcap = bass.AP(dram, si * 128 * 4096, [[4096, 128], [1, 4096]])
        dstap = bass.AP(wbuf, s * 4096, [[2 * 4096, 128], [1, 4096]])
        W("sync", [war])
        E("sync", lambda e, _d=dstap, _s=srcap: e.dma_start(out=_d, in_=_s),
          incs=[(wsems[s], 16)], chain=False)
        wring["cnt"][s] += 1
        wring["handles"].append((s, (wsems[s], 16 * wring["cnt"][s])))

    def ring_queue(dram, si=0, ow=512, kparts=8):
        i = wring["nq"]
        wring["nq"] += 1
        wring["pending"].append((dram, si))
        while wring["owed"] > 0 and wring["pending"]:
            wring["owed"] -= 1
            _ring_emit_one()
        return i

    def ring_get(i):
        assert i < len(wring["handles"]), "slice not yet loaded (ring order bug)"
        return wring["handles"][i]

    def wdone(s):
        wring["consumed"][s] = (ten, ten.n)
        _ring_emit_one()

    def layernorm(src, dst, sqb, extra=()):
        vv0 = vec.n
        E("scalar", lambda e, _s=src(0, 1024), _q=sqb: e.activation(
            _q, _s, AF.Square), waits=[(vec, vv0)] + list(extra))
        ssq = sca.n
        E("vector", lambda e, _s=src(0, 1024): e.tensor_reduce(
            mv[:, 0:1], _s, AXX, A.add))
        E("vector", lambda e, _q=sqb: e.tensor_reduce(mv[:, 1:2], _q, AXX,
                                                      A.add), waits=[(sca, ssq)])
        E("vector", lambda e: e.tensor_scalar_mul(mv[:, 0:1], mv[:, 0:1],
                                                  1.0 / D))
        E("vector", lambda e: e.tensor_scalar_mul(mv[:, 1:2], mv[:, 1:2],
                                                  1.0 / D))
        E("vector", lambda e: e.tensor_tensor(rs_t[:, 0:1], mv[:, 0:1],
                                              mv[:, 0:1], A.mult))
        E("vector", lambda e: e.tensor_tensor(rs_t[:, 0:1], mv[:, 1:2],
                                              rs_t[:, 0:1], A.subtract))
        E("vector", lambda e: e.tensor_scalar(rs_t[:, 0:1], rs_t[:, 0:1], 1e-5,
                                              None, A.add))
        vv = vec.n
        E("scalar", lambda e: e.activation(rs_t[:, 1:2], rs_t[:, 0:1], AF.Sqrt),
          waits=[(vec, vv)])
        ssn = sca.n
        E("vector", lambda e: e.reciprocal(rs_t[:, 0:1], rs_t[:, 1:2]),
          waits=[(sca, ssn)])
        E("vector", lambda e, _s=src(0, 1024), _d=dst(0, 1024): e.tensor_scalar(
            _d, _s, mv[:, 0:1], rs_t[:, 0:1], A.subtract, A.mult))
        return vec.n

    def transpose8(src3, jj, dst3, extra_waits, b0=6, b1=7):
        for half, bi in ((0, b0), (1, b1)):
            bank.open(bi, extra_waits=extra_waits)
            for c in range(4):
                cc = half * 4 + c
                ap_o = pb[bi][:, c * 128:(c + 1) * 128]
                ap_i = src3[:, jj, cc * 128:(cc + 1) * 128]
                E("tensor", lambda e, _o=ap_o, _i=ap_i, _st=(c == 0), _sp=(c == 3):
                  e.matmul(_o, _i, ident[:], start=_st, stop=_sp,
                           is_transpose=True),
                  incs=([(ten, 1)] if c == 3 else []), chain=False)
            bank.close(bi)
            ap_d = dst3[:, jj, half * 512:(half + 1) * 512]
            E("vector", lambda e, _d=ap_d, _bi=bi: e.tensor_copy(
                _d, pb[_bi][:, 0:512]), waits=bank.rw(bi))
            bank.reader(bi, vec)
        return vec.n

    nx = 0
    ny = 0
    x_gate = (vec, 0)
    state_v = {h: 0 for h in range(H)}
    alias_free = [0]
    v_fm = [0, 0]

    gspecs = []
    gtails = []
    gstate = {"issued": 0}
    gtmpl = {"raw": None}

    def gather_emit(e, ap_o, ap_i):
        if gtmpl["raw"] is None:
            inst = e.dma_gather(out_ap=ap_o, in_ap=tab[:, :], idxs_ap=ap_i,
                                num_idxs=128, num_idxs_reg=128, elem_size=2 * D)
            gtmpl["raw"] = inst.ins
            return inst
        t = gtmpl["raw"]
        raw = mybir.InstDMAGatherAnt(
            name=nc.get_next_instruction_name(), engine=mybir.EngineType.Pool,
            ins=[t.ins[0], e.lower_ap(ap_i), t.ins[2]],
            outs=[e.lower_ap(ap_o)],
            transpose=False, num_idxs=128, elem_size=2 * D,
            stride_bytes_256=t.stride_bytes_256, gen_mode=0,
            single_packet=True, queue_num=0)
        return e.add_instruction(raw)

    def issue_gathers(upto):
        while gstate["issued"] < min(upto, len(gspecs)):
            gi = gstate["issued"]
            gstate["issued"] += 1
            (tj, gg, vsh) = gspecs[gi]
            slot = gi % GB
            ws = [(vec, vsh)]
            if gi - GB >= 0:
                ws.append(gtails[gi - GB])
            ap_o = bass.AP(gbuf, slot * 2 * D,
                           [[GB * 2 * D, 128], [2 * D, 1], [1, 2 * D]])
            ap_i = bass.AP(idx_g, tj * 256 + gg * 8, [[NT * 256, 128], [1, 8]])
            W("gpsimd", ws)
            E("gpsimd", lambda e, _o=ap_o, _i=ap_i: gather_emit(e, _o, _i),
              incs=[(dg[slot], 16)], chain=False)

    _ph = _ph0

    for sc in range(n_sc):
        t0 = sc * SC
        gt = t0 // 128
        _ph(1)
        W("sync", [x_gate])
        for j in range(NT):
            ap_d = x_sc[:, j, :]
            ap_s = x_in[t0 + j * 128:t0 + (j + 1) * 128, :]
            E("sync", lambda e, _d=ap_d, _s=ap_s: e.dma_start(out=_d, in_=_s),
              incs=[(C["dx"], 16)], chain=False)
            nx += 1
        dx_need = 16 * nx


        _ph(2)
        xT_v = [0] * NT
        for j in range(NT):
            W("vector", [(C["dx"], dx_need), (C["dk"], dk_done)])
            lv = layernorm(lambda lo, hi, _j=j: x_sc[:, _j, lo:hi],
                           lambda lo, hi, _j=j: xln[:, _j, lo:hi],
                           k_tm[:, 0, :], extra=[(C["dx"], dx_need)])
            xT_v[j] = transpose8(xln, j, xlnT, [(vec, lv)])

        if sc == 0:
            dump(0, xln)
            dump(1, xlnT)
        _ph(3)
        qkv = [(wq_d, 0, 0), (wq_d, 512, 0), (wk_d, 0, 1), (wk_d, 512, 1),
               (wv_d, 0, 2), (wv_d, 512, 2)]
        qkv_ids = [ring_queue(d, o // 512) for (d, o, _) in qkv] if do_proj else []
        prod = [0] * NT
        g_b = 0
        for pi_, (dram, of0, which) in enumerate(qkv if do_proj else []):
            if pi_ % 2 == 0:
                g_b = set_bias([bqd, bkd, bvd][which])
            wslot, wwait = ring_get(qkv_ids[pi_])
            oh = of0 // 512
            for j in range(NT):
                bi = j % 2
                bank.open(bi, extra_waits=[wwait, (vec, xT_v[j])])
                for k in range(8):
                    ap_l = xlnT[:, j, k * 128:(k + 1) * 128]
                    ap_r = wbuf[:, wslot, k, 0:512]
                    E("tensor", lambda e, _bi=bi, _l=ap_l, _r=ap_r,
                      _st=(k == 0), _sp=(k == 7): e.matmul(
                          pb[_bi][:, 0:512], _l, _r, start=_st, stop=_sp),
                      incs=([(ten, 1)] if k == 7 else []), chain=False)
                bank.close(bi)
                if which in (0, 1):
                    dst = q_tm if which == 0 else k_tm
                    ap_b = brep[:, oh * 512:(oh + 1) * 512]
                    E("vector", lambda e, _bi=bi, _b=ap_b: e.tensor_tensor(
                        dscr[:, 0:512], pb[_bi][:, 0:512], _b, A.add),
                      waits=bank.rw(bi) + [(gps, g_b)])
                    bank.reader(bi, vec)
                    ap_d = dst[:, j, oh * 512:(oh + 1) * 512]
                    E("vector", lambda e, _d=ap_d: e.tensor_scalar_max(
                        _d, dscr[:, 0:512], 0.0))
                else:
                    ap_d = bass.AP(v65, j * H * 65 + oh * 8 * 65,
                                   [[NT * H * 65, 128], [65, 8], [1, 64]])
                    ap_b = brep[:, oh * 512:(oh + 1) * 512]
                    E("vector", lambda e, _bi=bi, _d=ap_d, _b=ap_b:
                      e.tensor_tensor(_d, pb[_bi][:, 0:512], _b, A.add),
                      waits=bank.rw(bi) + [(gps, g_b)])
                    bank.reader(bi, vec)
                prod[j] = vec.n
            wdone(wslot)

        if sc == 0:
            dump(2, q_tm)
            dump(3, k_tm)
        _ph(4)
        ao_v = [0] * NT

        def attn_stage(j, h, scv):
            par = h % 2
            bank.open(5, extra_waits=[(vec, scv)])
            for si_, src in enumerate((qsc_h(h), ksc_h(h))):
                ap_o = pb[5][:, si_ * 128:(si_ + 1) * 128]
                E("tensor", lambda e, _o=ap_o, _i=src, _st=(si_ == 0),
                  _sp=(si_ == 1): e.matmul(_o, _i, ident[:], start=_st,
                                           stop=_sp, is_transpose=True),
                  incs=([(ten, 1)] if si_ == 1 else []), chain=False)
            bank.close(5)
            E("scalar", lambda e, _p=par: e.copy(
                bass.AP(fmh, _p * 256, [[512, 128], [1, 256]]),
                pb[5][:, 0:256]), waits=bank.rw(5))
            bank.reader(5, sca)
            v_fm[par] = sca.n

        def fm_q(par):
            return bass.AP(fmh, par * 256, [[512, 128], [1, 128]])

        def fm_k(par):
            return bass.AP(fmh, par * 256 + 128, [[512, 128], [1, 128]])

        for j in range(NT if do_attn else 0):
            ap_cc = cos_col[:, gt + j:gt + j + 1]
            ap_sc = sin_col[:, gt + j:gt + j + 1]
            for si_, (tin, tdst, col) in enumerate((
                    (q_tm, qsc_t, ap_cc), (q_tm, qsc_t, ap_sc),
                    (k_tm, ksc_t, ap_cc), (k_tm, ksc_t, ap_sc))):
                ap_i = bass.AP(tin, j * D, [[NT * D, 128], [64, H], [1, 64]])
                ap_d = bass.AP(tdst, (si_ % 2) * 64,
                               [[QROW, 128], [128, H], [1, 64]])
                E("vector", lambda e, _d=ap_d, _i=ap_i, _c=col:
                  e.tensor_scalar(_d, _i, _c, None, A.mult),
                  waits=([(ten, alias_free[0])] if si_ == 0 else ()))
            scv = vec.n
            attn_stage(j, 0, scv)
            for h in range(H):
                par = h % 2
                hp = h // 2
                ab = 1 + par
                bank.open(ab, extra_waits=[(sca, v_fm[par])])
                E("tensor", lambda e, _ab=ab, _p=par: e.matmul(
                    pb[_ab][:, 0:128], fm_k(_p), fm_q(_p),
                    start=True, stop=True), incs=[(ten, 1)], chain=False)
                bank.close(ab)
                E("vector", lambda e, _ab=ab, _p=par: e.tensor_tensor(
                    amt[:, _p, :], pb[_ab][:, 0:128], mask[:], A.mult),
                  waits=bank.rw(ab))
                bank.reader(ab, vec)
                v_amt = vec.n
                if h + 1 < H:
                    attn_stage(j, h + 1, scv)
                ob = 3
                if par == 0:
                    bank.open(ob, extra_waits=[(vec, v_amt), (vec, state_v[h])])
                else:
                    W("tensor", [(vec, v_amt), (vec, state_v[h])])
                tp = (0, 64 * par)
                po = slice(64 * par, 64 * par + 64)
                ap_v = bass.AP(v65, j * H * 65 + h * 65,
                               [[NT * H * 65, 128], [1, 64]])
                ap_s = bass.AP(s2, h * 65, [[H * 65, 128], [1, 64]])
                ap_z = bass.AP(s2, h * 65 + 64, [[H * 65, 128], [0, 64]])
                mm4o = [
                    (pb[ob][po, 0:128], ap_v, amt[:, par, :], True, False),
                    (pb[ob][po, 0:128], ap_s, fm_q(par), False, True),
                    (pb[ob][po, 128:256], ones[:, 0:64], amt[:, par, :], True, False),
                    (pb[ob][po, 128:256], ap_z, fm_q(par), False, True),
                ]
                for mi, (o_, l_, r_, st_, sp_) in enumerate(mm4o):
                    E("tensor", lambda e, _o=o_, _l=l_, _r=r_, _st=st_, _sp=sp_,
                      _tp=tp: e.matmul(_o, _l, _r, start=_st, stop=_sp,
                                       tile_position=_tp),
                      incs=([(ten, 1)] if mi == 3 else []), chain=False)
                if par == 1:
                    bank.close(ob)
                stb = 4
                if par == 0:
                    bank.open(stb)
                co = 256 * par
                ap_v65 = bass.AP(v65, j * H * 65 + h * 65,
                                 [[NT * H * 65, 128], [1, 65]])
                E("tensor", lambda e, _k=ksc_h(h), _v=ap_v65, _co=co: e.matmul(
                    pb[stb][:, _co:_co + 65], _k,
                    _v, start=True, stop=True),
                  incs=[(ten, 1)], chain=False)
                if par == 1:
                    bank.close(stb)

                if par == 1:
                    odw = bank.rw(ob)
                    E("vector", lambda e: e.tensor_scalar_max(
                        dscr[:, 0:128], pb[3][:, 128:256], 1e-6), waits=odw)
                    bank.reader(ob, vec)
                    E("vector", lambda e: e.reciprocal(dscr[:, 128:256],
                                                       dscr[:, 0:128]))
                    ap_ao = xln[:, j, hp * 128:(hp + 1) * 128]
                    E("vector", lambda e, _d=ap_ao: e.tensor_tensor(
                        _d, pb[3][:, 0:128], dscr[:, 128:256], A.mult))
                    bank.reader(ob, vec)
                    stw = bank.rw(stb)
                    for hh, cc in ((h - 1, 0), (h, 256)):
                        ap_t = s2[:, hh, :]
                        ap_p = pb[stb][:, cc:cc + 65]
                        E("vector", lambda e, _t=ap_t, _p=ap_p:
                          e.tensor_tensor(_t, _t, _p, A.add),
                          waits=(stw if cc == 0 else ()))
                        state_v[hh] = vec.n
                    bank.reader(stb, vec)
            ao_v[j] = vec.n
            alias_free[0] = ten.n

        if sc == 0:
            dump(5, xln)
        _ph(5)
        oh_ids = [ring_queue(wo_d, 0), ring_queue(wo_d, 1)] if do_wo else []
        if do_wo:
            g_b = set_bias(bod)
        for ohx in range(2 if do_wo else 0):
            wslot, wwait = ring_get(oh_ids[ohx])
            for j in range(NT):
                bi = j % 2
                bank.open(bi, extra_waits=[wwait, (vec, ao_v[j])])
                for k in range(8):
                    ap_l = xln[:, j, k * 128:(k + 1) * 128]
                    ap_r = wbuf[:, wslot, k, 0:512]
                    E("tensor", lambda e, _bi=bi, _l=ap_l, _r=ap_r,
                      _st=(k == 0), _sp=(k == 7): e.matmul(
                          pb[_bi][:, 0:512], _l, _r, start=_st, stop=_sp),
                      incs=([(ten, 1)] if k == 7 else []), chain=False)
                bank.close(bi)
                ap_d = xlnT[:, j, ohx * 512:(ohx + 1) * 512]
                ap_b = brep[:, ohx * 512:(ohx + 1) * 512]
                E("vector", lambda e, _bi=bi, _d=ap_d, _b=ap_b: e.tensor_tensor(
                    _d, pb[_bi][:, 0:512], _b, A.add),
                  waits=bank.rw(bi) + [(gps, g_b)])
                bank.reader(bi, vec)
            wdone(wslot)
        sq_wo = [0] * NT
        for j in range(NT):
            layernorm(lambda lo, hi, _j=j: xlnT[:, _j, lo:hi],
                      lambda lo, hi: dscr[:, lo:hi], k_tm[:, 0, :])
            sq_wo[j] = sca.n
            ap_x = x_sc[:, j, :]
            E("vector", lambda e, _x=ap_x: e.tensor_tensor(_x, _x, dscr[:, :],
                                                           A.add))

        if sc == 0:
            dump(6, xlnT)
            dump(7, x_sc)
        _ph(6)
        W("vector", [(ten, ten.n)])
        E("vector", lambda e: e.memset(mlpacc[:], 0.0))
        for j in range(NT):
            lv = layernorm(lambda lo, hi, _j=j: x_sc[:, _j, lo:hi],
                           lambda lo, hi, _j=j: xln[:, _j, lo:hi],
                           k_tm[:, 0, :])
            xT_v[j] = transpose8(xln, j, xlnT, [(vec, lv), (sca, sq_wo[j])])

        _ph(7)
        qf_id = ring_queue(wqf_d, 0)
        qslot, qwait = ring_get(qf_id)
        sc_gbase = len(gspecs)

        for j in range(NT if do_expert else 0):
            bank.open(0, extra_waits=[qwait, (vec, xT_v[j])])
            for oc in range(4):
                for k in range(8):
                    ap_l = wbuf[:, qslot, k, oc * 128:(oc + 1) * 128]
                    ap_r = xlnT[:, j, k * 128:(k + 1) * 128]
                    ap_o = pb[0][:, oc * 128:(oc + 1) * 128]
                    E("tensor", lambda e, _o=ap_o, _l=ap_l, _r=ap_r,
                      _st=(k == 0), _sp=(k == 7): e.matmul(_o, _l, _r,
                                                           start=_st, stop=_sp),
                      incs=([(ten, 1)] if (k == 7 and oc == 3) else []),
                      chain=False)
            bank.close(0)
            E("scalar", lambda e: e.copy(qf_all, pb[0][:, 0:512]),
              waits=bank.rw(0))
            bank.reader(0, sca)
            s_qf = sca.n
            for half, bi in ((0, 2), (1, 6)):
                bank.open(bi, extra_waits=[(sca, s_qf), (C["dk"], dk_done)])
                for q2 in range(4):
                    p8 = half * 4 + q2
                    ap_l = qf_chunk(p8 // 2)
                    ap_r = keys_sb[:, p8, :]
                    ap_o = pb[bi][:, q2 * 128:(q2 + 1) * 128]
                    E("tensor", lambda e, _o=ap_o, _l=ap_l, _r=ap_r: e.matmul(
                        _o, _l, _r, start=True, stop=True),
                      incs=([(ten, 1)] if q2 == 3 else []), chain=False)
                bank.close(bi)
                ap_d = sim_half(half)
                E("scalar", lambda e, _d=ap_d, _bi=bi: e.copy(_d, pb[_bi][:, 0:512]),
                  waits=bank.rw(bi))
                bank.reader(bi, sca)
            s_sim = sca.n
            W("vector", [(sca, s_sim)])
            for p8 in range(8):
                E("vector", lambda e, _p=p8: e.max(sxy[:, _p, :], sim_slot(_p)))
                E("vector", lambda e, _p=p8: e.max_index(idxu[:, :], sxy[:, _p, :],
                                                         sim_slot(_p)))
                E("vector", lambda e, _p=p8: e.tensor_copy(ixy[:, _p, :],
                                                           idxu[:, :]))
            E("vector", lambda e: e.tensor_tensor(
                bass.AP(ascore, 0, [[256, 128], [64, 4], [8, 8], [1, 8]]),
                bass.AP(sxy, 0, [[64, 128], [8, 4], [1, 8], [0, 8]]),
                bass.AP(sxy, 32, [[64, 128], [8, 4], [0, 8], [1, 8]]),
                A.add))
            for hh in range(4):
                E("vector", lambda e, _h=hh: e.max(score8[:, _h, :],
                                                   ascore[:, _h, :]))
                E("vector", lambda e, _h=hh: e.max_index(idxu[:, :],
                                                         score8[:, _h, :],
                                                         ascore[:, _h, :]))
                E("vector", lambda e, _h=hh: e.tensor_copy(pk_f[:, _h, :],
                                                           idxu[:, :]))
            E("vector", lambda e: e.tensor_tensor(
                bass.AP(ascore, 0, [[256, 128], [56, 4], [7, 8], [1, 7]]),
                bass.AP(pk_f, 0, [[32, 128], [8, 4], [1, 8], [0, 7]]),
                bass.AP(c8, 1, [[8, 128], [0, 4], [0, 8], [1, 7]]),
                A.is_ge))
            E("vector", lambda e: e.tensor_reduce(
                isel[:, :, :],
                bass.AP(ascore, 0, [[256, 128], [56, 4], [7, 8], [1, 7]]),
                AXX, A.add))
            E("vector", lambda e: e.scalar_tensor_tensor(
                jsel[:], isel[:], -8.0, pk_f[:], A.mult, A.add))
            for selt, off in ((isel, 0), (jsel, 32)):
                E("vector", lambda e, _s=selt: e.tensor_tensor(
                    bass.AP(ascore, 0, [[256, 128], [64, 4], [8, 8], [1, 8]]),
                    bass.AP(_s, 0, [[32, 128], [8, 4], [1, 8], [0, 8]]),
                    bass.AP(iota8, 0, [[8, 128], [0, 4], [0, 8], [1, 8]]),
                    A.is_equal))
                E("vector", lambda e, _o=off: e.tensor_tensor(
                    bass.AP(ascore, 0, [[256, 128], [64, 4], [8, 8], [1, 8]]),
                    bass.AP(ascore, 0, [[256, 128], [64, 4], [8, 8], [1, 8]]),
                    bass.AP(ixy, _o, [[64, 128], [8, 4], [0, 8], [1, 8]]),
                    A.mult))
                E("vector", lambda e, _d=selt: e.tensor_reduce(
                    _d[:, :, :],
                    bass.AP(ascore, 0, [[256, 128], [64, 4], [8, 8], [1, 8]]),
                    AXX, A.add))
            E("vector", lambda e: e.scalar_tensor_tensor(
                idx_f[:], bass.AP(isel, 0, [[32, 128], [1, 32]]), 128.0,
                bass.AP(jsel, 0, [[32, 128], [1, 32]]), A.mult, A.add))
            ap_smw3 = bass.AP(smw, j * 32, [[NT * 32, 128], [8, 4], [1, 8]])
            E("vector", lambda e, _d=ap_smw3: e.tensor_tensor(
                _d, bass.AP(score8, 0, [[32, 128], [8, 4], [1, 8]]),
                bass.AP(score8, 0, [[32, 128], [8, 4], [0, 8]]), A.subtract))
            vv1 = vec.n
            E("scalar", lambda e, _d=ap_smw3: e.activation(_d, _d, AF.Exp),
              waits=[(vec, vv1)])
            ss1 = sca.n
            E("vector", lambda e, _s=ap_smw3: e.tensor_reduce(
                sm_s[:, 0:4], _s, AXX, A.add), waits=[(sca, ss1)])
            E("vector", lambda e: e.reciprocal(sm_s[:, 4:8], sm_s[:, 0:4]))
            E("vector", lambda e, _d=ap_smw3: e.tensor_tensor(
                _d, _d, bass.AP(sm_s, 4, [[8, 128], [1, 4], [0, 8]]), A.mult))
            ap_ii = idx_i16[:, j, :]
            E("vector", lambda e, _d=ap_ii: e.tensor_copy(_d, idx_f[:]))
            v_i16 = vec.n
            W("sync", [(vec, v_i16)])
            for s8 in range(8):
                for dup in range(2):
                    ap_d = bass.AP(idx_tmp,
                                   (dup * 16) * (NT * 256) + j * 256 + s8 * 32,
                                   [[NT * 256, 16], [1, 32]])
                    ap_s = bass.AP(idx_i16, (16 * s8) * (NT * 32) + j * 32,
                                   [[NT * 32, 16], [1, 32]])
                    E("sync", lambda e, _d=ap_d, _s=ap_s: e.dma_start(
                        out=_d, in_=_s), incs=[(C["di"], 16)], chain=False)
            ap_gd = bass.AP(idx_g, j * 256, [[NT * 256, 32], [8, 32], [1, 8]])
            ap_gs = bass.AP(idx_tmp, j * 256, [[NT * 256, 32], [1, 32], [32, 8]])
            E("vector", lambda e, _d=ap_gd, _s=ap_gs: e.tensor_copy(_d, _s),
              waits=[(C["di"], C["di"].n)])
            v_shuf = vec.n
            for g in range(NGRP):
                gspecs.append((j, g, v_shuf))
                gtails.append(None)
            if j == 0 and sc == 0:
                issue_gathers(GB)
        wdone(qslot)

        def consume_unit(u):
            tj = u // 16
            u2 = u % 16
            base = sc_gbase + tj * NGRP
            for gg in (2 * u2, 2 * u2 + 1):
                gi = base + gg
                slot = gi % GB
                dgw = (dg[slot], 16 * (gi // GB + 1))
                ap_g = bass.AP(gbuf, slot * 2 * D, [[GB * 2 * D, 128], [1, D]])
                ap_x = xln[:, tj, :]
                ap_w = wdot[:, tj, gg:gg + 1]
                E("vector", lambda e, _g=ap_g, _x=ap_x, _w=ap_w:
                  e.tensor_tensor_reduce(dscr[:, :], _g, _x, 1.0, 0.0,
                                         A.mult, A.add, _w), waits=[dgw])
            vv2 = vec.n
            ap_wd = wdot[:, tj, 2 * u2:2 * u2 + 2]
            ap_wf = wfin[:, tj, 2 * u2:2 * u2 + 2]
            ap_sm = smw[:, tj, 2 * u2:2 * u2 + 2]
            E("scalar", lambda e, _i=ap_wd, _o=ap_wf: e.activation(
                _o, _i, AF.Silu), waits=[(vec, vv2)])
            ss2 = sca.n
            E("vector", lambda e, _f=ap_wf, _s=ap_sm: e.tensor_tensor(
                _f, _f, _s, A.mult), waits=[(sca, ss2)])
            for gg in (2 * u2, 2 * u2 + 1):
                gi = base + gg
                slot = gi % GB
                ap_u = bass.AP(gbuf, slot * 2 * D + D, [[GB * 2 * D, 128], [1, D]])
                ap_m = mlpacc[:, tj, :]
                ap_w1 = wfin[:, tj, gg:gg + 1]
                E("vector", lambda e, _u=ap_u, _m=ap_m, _w=ap_w1:
                  e.scalar_tensor_tensor(_m, _u, _w, _m, A.mult, A.add))
                gtails[gi] = (vec, vec.n)

        cu = {"n": 0}

        def drain(kunits):
            if not do_expert:
                return
            for _ in range(kunits):
                u = cu["n"]
                if u >= NT * 16:
                    return
                cu["n"] += 1
                gu_base = sc_gbase + 2 * u
                issue_gathers(gu_base + GB)
                consume_unit(u)
                issue_gathers(gu_base + 2 + GB)

        if sc == 0:
            dump(8, xln)
        _ph(8)
        mlp_ids = []
        for ic in range(8 if do_mlp else 0):
            mlp_ids.append(ring_queue(wg_d, ic))
            mlp_ids.append(ring_queue(wu_d, ic))
            mlp_ids.append(ring_queue(wd_d, ic))
        for ic in range(8 if do_mlp else 0):
            gslot, gwait = ring_get(mlp_ids[3 * ic])
            sgl = [0] * NT
            for j in range(NT):
                bank.open(4, extra_waits=[gwait, (vec, xT_v[j])])
                for k in range(8):
                    ap_l = xlnT[:, j, k * 128:(k + 1) * 128]
                    ap_r = wbuf[:, gslot, k, 0:512]
                    E("tensor", lambda e, _l=ap_l, _r=ap_r, _st=(k == 0),
                      _sp=(k == 7): e.matmul(pb[4][:, 0:512], _l, _r,
                                             start=_st, stop=_sp),
                      incs=([(ten, 1)] if k == 7 else []), chain=False)
                bank.close(4)
                ap_gs2 = g_sc[:, j, :]
                E("scalar", lambda e, _d=ap_gs2: e.activation(
                    _d, pb[4][:, 0:512], AF.Silu), waits=bank.rw(4))
                bank.reader(4, sca)
                sgl[j] = sca.n
                drain(1)
            wdone(gslot)
            uslot, uwait = ring_get(mlp_ids[3 * ic + 1])
            gTl = [0] * NT
            for j in range(NT):
                bank.open(5, extra_waits=[uwait, (vec, xT_v[j])])
                for k in range(8):
                    ap_l = xlnT[:, j, k * 128:(k + 1) * 128]
                    ap_r = wbuf[:, uslot, k, 0:512]
                    E("tensor", lambda e, _l=ap_l, _r=ap_r, _st=(k == 0),
                      _sp=(k == 7): e.matmul(pb[5][:, 0:512], _l, _r,
                                             start=_st, stop=_sp),
                      incs=([(ten, 1)] if k == 7 else []), chain=False)
                bank.close(5)
                ap_gs2 = g_sc[:, j, :]
                E("vector", lambda e, _g=ap_gs2: e.tensor_tensor(
                    gu_sb[:, :], _g, pb[5][:, 0:512], A.mult),
                  waits=bank.rw(5) + [(sca, sgl[j])])
                bank.reader(5, vec)
                v_gu = vec.n
                bank.open(6, extra_waits=[(vec, v_gu)])
                for c in range(4):
                    ap_i = gu_sb[:, c * 128:(c + 1) * 128]
                    ap_o = pb[6][:, c * 128:(c + 1) * 128]
                    E("tensor", lambda e, _o=ap_o, _i=ap_i, _st=(c == 0),
                      _sp=(c == 3): e.matmul(_o, _i, ident[:], start=_st,
                                             stop=_sp, is_transpose=True),
                      incs=([(ten, 1)] if c == 3 else []), chain=False)
                bank.close(6)
                ap_gT = bass.AP(guT_sc, j * 512, [[NT * 512, 128], [1, 512]])
                E("scalar", lambda e, _d=ap_gT: e.activation(
                    _d, pb[6][:, 0:512], AF.Copy), waits=bank.rw(6))
                bank.reader(6, sca)
                gTl[j] = sca.n
            wdone(uslot)
            dslot, dwait = ring_get(mlp_ids[3 * ic + 2])
            for j in range(NT):
                for oc2 in range(2):
                    bi = (j % 2) * 2 + oc2
                    bank.open(bi, extra_waits=[dwait, (sca, gTl[j])])
                    for k in range(4):
                        ap_l = bass.AP(guT_sc, j * 512 + k * 128,
                                       [[NT * 512, 128], [1, 128]])
                        ap_r = wbuf[:, dslot, k * 2 + oc2, 0:512]
                        E("tensor", lambda e, _bi=bi, _l=ap_l, _r=ap_r,
                          _st=(k == 0), _sp=(k == 3): e.matmul(
                              pb[_bi][:, 0:512], _l, _r, start=_st, stop=_sp),
                          incs=([(ten, 1)] if k == 3 else []), chain=False)
                    bank.close(bi)
                    ap_m = mlpacc[:, j, oc2 * 512:(oc2 + 1) * 512]
                    E("vector", lambda e, _bi=bi, _m=ap_m: e.scalar_tensor_tensor(
                        _m, pb[_bi][:, 0:512], 1.0, _m, A.mult, A.add),
                      waits=bank.rw(bi))
                    bank.reader(bi, vec)
                drain(1)
            wdone(dslot)
        drain(NT * 16)
        alias_free[0] = ten.n

        if sc == 0:
            dump(9, mlpacc)
        _ph(9)
        for j in range(NT):
            layernorm(lambda lo, hi, _j=j: mlpacc[:, _j, lo:hi],
                      lambda lo, hi: dscr[:, lo:hi], k_tm[:, 0, :])
            ap_x = x_sc[:, j, :]
            E("vector", lambda e, _x=ap_x: e.tensor_tensor(_x, _x, dscr[:, :],
                                                           A.add))
            v_res = vec.n
            ap_y = y_out[t0 + j * 128:t0 + (j + 1) * 128, :]
            W("sync", [(vec, v_res)])
            E("sync", lambda e, _y=ap_y, _x=ap_x: e.dma_start(out=_y, in_=_x),
              incs=[(C["dy"], 16)], chain=False)
            ny += 1
        x_gate = (C["dy"], 16 * ny)

    W("sync", [(C["dy"], 16 * ny)])

    with nc.Block() as block:
        @block.tensor
        def _(e):
            for fn in P.q["tensor"]:
                fn(e)

        @block.vector
        def _(e):
            for fn in P.q["vector"]:
                fn(e)

        @block.scalar
        def _(e):
            for fn in P.q["scalar"]:
                fn(e)

        @block.gpsimd
        def _(e):
            for fn in P.q["gpsimd"]:
                fn(e)

        @block.sync
        def _(e):
            for fn in P.q["sync"]:
                fn(e)

    for g in reversed(ctx):
        g.__exit__(None, None, None)
    return nc


def _pack_cols(M):
    R, C = M.shape
    kp = R // 128
    ns = C // 512
    return np.ascontiguousarray(
        M.reshape(kp, 128, ns, 512).transpose(2, 1, 0, 3).reshape(ns, 128, kp * 512))


def _pack_rows(M):
    R, C = M.shape
    ns = R // 512
    return np.ascontiguousarray(
        M.reshape(ns, 4, 128, C).transpose(0, 2, 1, 3).reshape(ns, 128, 4 * C))


def host_pack(x, Wq, bq, Wk, bk, Wv, bv, Wo, bo, fWq, keys, down, up, Wg, Wu, Wd):
    f32 = np.float32

    def c_(a):
        return np.ascontiguousarray(np.asarray(a), dtype=f32)

    keys_np = np.asarray(keys)
    keys_h = np.empty((8, 64, 128), f32)
    for p in range(2):
        for h in range(4):
            keys_h[p * 4 + h] = np.transpose(keys_np[h, :, p, :])
    return dict(
        x=c_(x),
        wq=_pack_cols(c_(Wq)), wk=_pack_cols(c_(Wk)),
        wv=_pack_cols(c_(Wv)), wo=_pack_cols(c_(Wo)),
        bq=c_(bq), bk=c_(bk), bv=c_(bv), bo=c_(bo),
        wqf=_pack_cols(c_(np.transpose(np.asarray(fWq)))),
        keysT=c_(keys_h),
        wg=_pack_cols(c_(np.transpose(np.asarray(Wg)))),
        wu=_pack_cols(c_(np.transpose(np.asarray(Wu)))),
        wd=_pack_rows(c_(np.transpose(np.asarray(Wd)))),
        tab=c_(np.concatenate([np.asarray(down), np.asarray(up)], axis=1)))


def kernel(L0_attn_x, L0_attn_Wq, L0_attn_bq, L0_attn_Wk, L0_attn_bk,
           L0_attn_Wv, L0_attn_bv, L0_attn_Wo, L0_attn_bo, L0_ffn_Wq,
           L0_ffn_keys, L0_ffn_down_embed, L0_ffn_up_embed, L0_ffn_Wg,
           L0_ffn_Wu, L0_ffn_Wd):
    B = int(np.asarray(L0_attn_x).shape[0])
    xs = np.asarray(L0_attn_x)
    base = host_pack(xs[0], L0_attn_Wq, L0_attn_bq, L0_attn_Wk, L0_attn_bk,
                     L0_attn_Wv, L0_attn_bv, L0_attn_Wo, L0_attn_bo, L0_ffn_Wq,
                     L0_ffn_keys, L0_ffn_down_embed, L0_ffn_up_embed, L0_ffn_Wg,
                     L0_ffn_Wu, L0_ffn_Wd)
    nc = build_core_program()
    lower_extended_insts(nc)
    in_maps = []
    for c in range(B):
        m = dict(base)
        m["x"] = np.ascontiguousarray(xs[c], dtype=np.float32)
        in_maps.append(m)
    res = run_bass_kernel_spmd(nc, in_maps, core_ids=list(range(B)))
    out = np.stack([np.asarray(res.results[c]["y"]) for c in range(B)], axis=0)
    return out.astype(np.float32)
